# version 01
# baseline (speedup 1.0000x reference)
import contextlib
import numpy as np
import concourse.bass as bass
import concourse.mybir as mybir
from concourse.bass_utils import run_bass_kernel_spmd

F32 = mybir.dt.float32
I32 = mybir.dt.int32
BF16 = mybir.dt.bfloat16
AF = mybir.ActivationFunctionType
ALU = mybir.AluOpType
AX = mybir.AxisListType

B, S, D = 32, 2048, 1024
H, DH = 16, 64
NCORE = 8
SEQ = B // NCORE
NCH = D // 128
NTT = S // 512
NT2 = S // 128

INV_2PI = float(1.0 / (2 * np.pi))
HALF_PI = float(np.pi / 2)
T2PI_HI = float(np.float32(round((2 * np.pi) * 2048.0) / 2048.0))
T2PI_LO = float(np.float32(2 * np.pi - T2PI_HI))
EPS = 1e-5

ENGINES = ("sync", "gpsimd", "tensor", "vector", "scalar")


class Slot:
    __slots__ = ("w", "r")

    def __init__(self):
        self.w = None
        self.r = []


class Sched:
    def __init__(self):
        self.ops = {e: [] for e in ENGINES}
        self.cnt = {}
        self.hiwater = {e: {} for e in ENGINES}

    def op(self, eng, fn, waits=(), inc=None, incn=1):
        hw = self.hiwater[eng]
        pruned = {}
        for t in waits:
            if t is None:
                continue
            s, v = t
            if hw.get(s, -1) >= v or pruned.get(s, -1) >= v:
                continue
            pruned[s] = v
        for s, v in pruned.items():
            hw[s] = v
        tok = None
        if inc is not None:
            self.cnt[inc] = self.cnt.get(inc, 0) + incn
            tok = (inc, self.cnt[inc])
        self.ops[eng].append((fn, tuple(pruned.items()), inc, incn))
        return tok

    def replay(self, eng_name, eng, sems):
        for fn, waits, inc, incn in self.ops[eng_name]:
            for s, v in waits:
                eng.wait_ge(sems[s], v)
            instr = fn(eng)
            if inc is not None:
                if instr is None:
                    eng.sem_inc(sems[inc], incn)
                else:
                    instr.then_inc(sems[inc], incn)


def build_program(phases=("ffn1", "attn", "ffn2"), dbg=False):
    nc = bass.Bass("TRN2", target_bir_lowering=False)

    x_in = nc.dram_tensor("x_in", [SEQ * S, D], F32, kind="ExternalInput").ap()
    wq_t = nc.dram_tensor("wq_t", [D, 3 * D], F32, kind="ExternalInput").ap()
    wo_t = nc.dram_tensor("wo_t", [D, D], F32, kind="ExternalInput").ap()
    wr_t = nc.dram_tensor("wr_t", [D, D], F32, kind="ExternalInput").ap()
    wk_t = nc.dram_tensor("wk_t", [D, D], F32, kind="ExternalInput").ap()
    wv_t = nc.dram_tensor("wv_t", [D, D], F32, kind="ExternalInput").ap()
    mu_r = nc.dram_tensor("mu_r", [D], F32, kind="ExternalInput").ap()
    mu_k = nc.dram_tensor("mu_k", [D], F32, kind="ExternalInput").ap()
    out = nc.dram_tensor("out", [SEQ * S, D], F32, kind="ExternalOutput").ap()

    wsp = {}
    for nm, dout in [("wq", 3 * D), ("wo", D), ("wr", D), ("wk", D), ("wv", D)]:
        for h in ("h", "l"):
            wsp[nm + h] = nc.dram_tensor(f"wsp_{nm}{h}", [D, dout], BF16).ap()
    kw = dict(kind="ExternalOutput") if dbg else {}
    q_sp = {h: nc.dram_tensor(f"q_sp{h}", [D, S], BF16, **kw).ap() for h in "hl"}
    k_sp = {h: nc.dram_tensor(f"k_sp{h}", [D, S], BF16, **kw).ap() for h in "hl"}
    v_sp = {h: nc.dram_tensor(f"v_sp{h}", [S, D], BF16, **kw).ap() for h in "hl"}
    dbg_crep = nc.dram_tensor("dbg_crep", [128, H], F32, kind="ExternalOutput").ap() if dbg else None

    sched = Sched()
    E = sched.op
    stack = contextlib.ExitStack()
    slots = {}

    def sb(name, shape, dtype):
        t = stack.enter_context(nc.sbuf_tensor(name, shape, dtype))
        slots[name] = Slot()
        return t

    ident32 = sb("ident32", [128, 128], F32)
    identb = sb("identb", [128, 128], BF16)
    onesf = sb("onesf", [1, 128], F32)
    rt_b = sb("rt_b", [128, 128], BF16)
    ones_b = sb("ones_b", [128, 128], BF16)
    blk_b = sb("blk_b", [128, 2], BF16)
    cosr = sb("cosr", [128, S], F32)
    sinr = sb("sinr", [128, S], F32)
    mur = sb("mur", [128, NCH], F32)
    muk = sb("muk", [128, NCH], F32)
    cEps = sb("cEps", [128, 1], F32)
    cHpi = sb("cHpi", [128, 1], F32)
    invf_c = sb("invf_c", [128, 1], F32)
    crep = sb("crep", [128, H], F32)
    bnd = sb("bnd", [128, NCH * NTT], F32)
    rowb = sb("rowb", [128, S], F32)
    rrep = sb("rrep", [128, S], F32)
    mrrep = sb("mrrep", [128, S], F32)
    yt = sb("yt", [128, NCH * S], F32)
    sa = sb("sa", [128, 4104], F32)
    sB = sb("sB", [128, 4096], F32)
    g1 = sb("g1", [128, 4096], BF16)
    g2 = sb("g2", [128, 4096], BF16)
    g3 = sb("g3", [128, 4096], BF16)
    g4 = sb("g4", [128, 4096], BF16)
    kbuf = sb("kbuf", [128, 2 * S], BF16)
    vbuf = sb("vbuf", [128, 2 * S], BF16)
    qbuf = sb("qbuf", [128, 1024], BF16)
    wbuf = sb("wbuf", [128, 2048], BF16)
    wbuf2 = sb("wbuf2", [128, 2048], BF16)
    iost = sb("iost", [128, 1024], F32)
    rden = sb("rden", [128, 512], F32)
    nn_i = sb("nn_i", [128, 512], I32)


    pbt = []
    for i in range(7):
        t = stack.enter_context(nc.psum_tensor(f"pb{i}", [128, 512], F32))
        slots[f"pb{i}"] = Slot()
        pbt.append(t)
    pbb = stack.enter_context(nc.psum_tensor("pbb", [128, 1024], BF16))
    slots["pbb"] = Slot()

    def pbank(bk):
        return pbt[int(bk[2])]

    SEMNAMES = ["LD", "LW", "ST", "TE", "VE", "SC", "GP"]

    def SW(name):
        s = slots[name]
        return list(s.r) + ([s.w] if s.w else [])

    def SR(name):
        s = slots[name]
        return [s.w] if s.w else []

    def setw(name, tok):
        s = slots[name]
        s.w = tok
        s.r = []

    def setr(name, tok):
        if tok is not None:
            slots[name].r.append(tok)

    CONST_NAMES = ["ident32", "identb", "onesf", "rt_b", "ones_b",
                   "blk_b", "cEps", "cHpi", "invf_c"]

    def emit_consts():
        g = "gpsimd"
        E(g, lambda e: e.memset(cEps[:], EPS))
        E(g, lambda e: e.memset(cHpi[:], HALF_PI))
        E(g, lambda e: e.memset(onesf[:], 1.0))
        E(g, lambda e: e.memset(ones_b[:], 1.0))
        E(g, lambda e: e.memset(blk_b[:], 0.0))
        E(g, lambda e: e.memset(blk_b[0:64, 0:1], 1.0))
        E(g, lambda e: e.memset(blk_b[64:128, 1:2], 1.0))
        for t, _ in ((ident32, F32), (identb, BF16)):
            E(g, lambda e, t=t: e.memset(t[:], 1.0))
            E(g, lambda e, t=t: e.affine_select(out=t[:], in_=t[:],
                                                pattern=[[1, 128]], compare_op=ALU.is_equal,
                                                fill=0.0, base=0, channel_multiplier=-1))
        E(g, lambda e: e.memset(rt_b[:], -1.0))
        E(g, lambda e: e.affine_select(out=rt_b[:], in_=rt_b[:],
                                       pattern=[[1, 128]], compare_op=ALU.is_equal,
                                       fill=0.0, base=32, channel_multiplier=-1))
        E(g, lambda e: e.affine_select(out=rt_b[:], in_=rt_b[:],
                                       pattern=[[0, 2], [-1, 64]], compare_op=ALU.is_ge,
                                       fill=0.0, base=31, channel_multiplier=0))
        E(g, lambda e: e.memset(sa[:, 0:128], 1.0))
        E(g, lambda e: e.affine_select(out=sa[:, 0:128], in_=sa[:, 0:128],
                                       pattern=[[1, 128]], compare_op=ALU.is_equal,
                                       fill=0.0, base=-32, channel_multiplier=-1))
        tgp = E(g, lambda e: e.affine_select(out=sa[:, 0:128], in_=sa[:, 0:128],
                                             pattern=[[0, 2], [1, 64]], compare_op=ALU.is_ge,
                                             fill=0.0, base=-32, channel_multiplier=0),
                inc="GP")
        tve = E("vector", lambda e: e.tensor_tensor(out=rt_b[:], in0=rt_b[:],
                                                    in1=sa[:, 0:128], op=ALU.add),
                waits=[tgp], inc="VE")
        setw("sa", tve)

        t1 = E("sync", lambda e: e.dma_start(out=mur[:], in_=mu_r.rearrange("(c p) -> p c", p=128),
                                             allow_slow_non_contiguous=True),
               inc="LD", incn=16)
        t2 = E("sync", lambda e: e.dma_start(out=muk[:], in_=mu_k.rearrange("(c p) -> p c", p=128),
                                             allow_slow_non_contiguous=True),
               inc="LD", incn=16)
        setw("mur", t1)
        setw("muk", t2)

        tio = E(g, lambda e: e.iota(out=sinr[:, :], pattern=[[1, S]], base=0,
                                    channel_multiplier=0,
                                    allow_small_or_imprecise_dtypes=True), inc="GP")
        for j in range(128):
            val = float(np.float32(10000.0 ** (-(j % 32) / 32.0)))
            E(g, lambda e, j=j, val=val: e.memset(rowb[0:1, j:j + 1], val))
        tgp2 = E(g, lambda e: e.engine_nop(), inc="GP")
        ttr0 = E("tensor", lambda e: e.transpose(pbt[6][:, 0:1], rowb[0:1, 0:128], onesf[0:1, 0:1]),
                 waits=[tgp2] + SW("pb6"), inc="TE")
        setw("pb6", ttr0)
        tiv = E("vector", lambda e: e.tensor_copy(out=invf_c[:], in_=pbt[6][:, 0:1]),
                waits=SR("pb6"), inc="VE")
        setr("pb6", tiv)
        setw("rowb", tgp2)
        tv1 = E("vector", lambda e: e.tensor_scalar(out=sinr[:], in0=sinr[:],
                                                    scalar1=invf_c[:], scalar2=None,
                                                    op0=ALU.mult),
                waits=[tio, tiv], inc="VE")
        tlast = tv1
        for j in range(S // 512):
            c0 = j * 512
            ta = E("vector", lambda e, c0=c0: e.tensor_scalar(out=nn_i[:], in0=sinr[:, c0:c0 + 512],
                                                              scalar1=INV_2PI, scalar2=None,
                                                              op0=ALU.mult),
                   waits=[tlast], inc="VE")
            tb = E("vector", lambda e, c0=c0: e.tensor_copy(out=sB[:, c0:c0 + 512], in_=nn_i[:]),
                   waits=[ta], inc="VE")
            tlast = tb
        tv4 = E("vector", lambda e: e.scalar_tensor_tensor(out=sinr[:], in0=sB[:, 0:S],
                                                           scalar=-T2PI_HI, in1=sinr[:],
                                                           op0=ALU.mult, op1=ALU.add),
                waits=[tlast], inc="VE")
        tv5 = E("vector", lambda e: e.scalar_tensor_tensor(out=sinr[:], in0=sB[:, 0:S],
                                                           scalar=-T2PI_LO, in1=sinr[:],
                                                           op0=ALU.mult, op1=ALU.add),
                waits=[tv4], inc="VE")
        ts1 = E("scalar", lambda e: e.activation(out=sB[:, 0:S], in_=sinr[:], func=AF.Abs),
                waits=[tv5], inc="SC")
        ts2 = E("scalar", lambda e: e.activation(out=cosr[:], in_=sB[:, 0:S], func=AF.Sin,
                                                 scale=-1.0, bias=cHpi[:]),
                waits=[ts1], inc="SC")
        ts3 = E("scalar", lambda e: e.activation(out=sinr[:], in_=sinr[:], func=AF.Sin),
                waits=[ts2], inc="SC")
        setw("cosr", ts2)
        setw("sinr", ts3)
        setw("sB", ts1)
        tcb = E(g, lambda e: e.engine_nop(), inc="GP")
        for nm in CONST_NAMES:
            slots[nm].w = tcb

    def emit_wsplit():
        srcs = [(wq_t, "wq", 3 * D), (wo_t, "wo", D), (wr_t, "wr", D),
                (wk_t, "wk", D), (wv_t, "wv", D)]
        for src, nm, dout in srcs:
            for kc in range(NCH):
                tld = E("sync", lambda e, src=src, kc=kc, dout=dout:
                        e.dma_start(out=sa[:, 0:dout], in_=src[kc * 128:(kc + 1) * 128, :]),
                        waits=SW("sa"), inc="LD", incn=16)
                setw("sa", tld)
                th = E("vector", lambda e, dout=dout: e.tensor_copy(out=g1[:, 0:dout], in_=sa[:, 0:dout]),
                       waits=SR("sa") + SW("g1"), inc="VE")
                setw("g1", th)
                tl = E("vector", lambda e, dout=dout: e.tensor_tensor(out=g2[:, 0:dout], in0=sa[:, 0:dout],
                                                                      in1=g1[:, 0:dout], op=ALU.subtract),
                       waits=SW("g2"), inc="VE")
                setw("g2", tl)
                setr("sa", tl)
                sh = E("gpsimd", lambda e, nm=nm, kc=kc, dout=dout: e.dma_start(
                    out=wsp[nm + "h"][kc * 128:(kc + 1) * 128, :], in_=g1[:, 0:dout]),
                    waits=SR("g1"), inc="ST", incn=16)
                sl = E("gpsimd", lambda e, nm=nm, kc=kc, dout=dout: e.dma_start(
                    out=wsp[nm + "l"][kc * 128:(kc + 1) * 128, :], in_=g2[:, 0:dout]),
                    waits=SR("g2"), inc="ST", incn=16)
                setr("g1", sh)
                setr("g2", sl)

    def emit_load_x(s):
        ytw = SW("yt")
        for t2 in range(NT2):
            tld = E("sync", lambda e, s=s, t2=t2:
                    e.dma_start(out=iost[:], in_=x_in[s * S + t2 * 128: s * S + (t2 + 1) * 128, :]),
                    waits=SW("iost"), inc="LD", incn=16)
            setw("iost", tld)
            for c in range(NCH):
                bk = f"pb{5 + (c % 2)}"
                tmm = E("tensor", lambda e, c=c, bk=bk:
                        e.transpose(pbank(bk)[:, 0:128], iost[:, c * 128:(c + 1) * 128], ident32[:]),
                        waits=SR("iost") + SW(bk) + SR("ident32"), inc="TE")
                setw(bk, tmm)
                setr("iost", tmm)
                tev = E("scalar", lambda e, c=c, t2=t2, bk=bk:
                        e.copy(out=yt[:, c * S + t2 * 128: c * S + (t2 + 1) * 128], in_=pbank(bk)[:, 0:128]),
                        waits=SR(bk) + (ytw if (t2 == 0 and c == 0) else []), inc="SC")
                setr(bk, tev)
        setw("yt", tev)

    def emit_store(s):
        laststore = None
        for t2 in range(NT2):
            evs = []
            for c in range(NCH):
                bk = f"pb{5 + (c % 2)}"
                tmm = E("tensor", lambda e, c=c, t2=t2, bk=bk:
                        e.transpose(pbank(bk)[:, 0:128],
                                    yt[:, c * S + t2 * 128: c * S + (t2 + 1) * 128], ident32[:]),
                        waits=SR("yt") + SW(bk), inc="TE")
                setw(bk, tmm)
                setr("yt", tmm)
                tev = E("scalar", lambda e, c=c, bk=bk:
                        e.copy(out=iost[:, c * 128:(c + 1) * 128], in_=pbank(bk)[:, 0:128]),
                        waits=SR(bk) + (SW("iost") if c == 0 else []), inc="SC")
                setr(bk, tev)
                evs.append(tev)
            slots["iost"].w = evs[-1]
            slots["iost"].r = []
            laststore = E("gpsimd", lambda e, s=s, t2=t2:
                          e.dma_start(out=out[s * S + t2 * 128: s * S + (t2 + 1) * 128, :], in_=iost[:]),
                          waits=evs, inc="ST", incn=16)
            setr("iost", laststore)
        return laststore

    def emit_ln(s):
        for tt in range(NTT):
            ty = tq2 = None
            for c in range(NCH):
                w0 = c * S + tt * 512
                th = E("vector", lambda e, w0=w0: e.tensor_copy(out=g1[:, 0:512], in_=yt[:, w0:w0 + 512]),
                       waits=SR("yt") + SW("g1"), inc="VE")
                setw("g1", th)
                tl = E("vector", lambda e, w0=w0: e.tensor_tensor(out=g2[:, 0:512], in0=yt[:, w0:w0 + 512],
                                                                  in1=g1[:, 0:512], op=ALU.subtract),
                       waits=SW("g2"), inc="VE")
                setw("g2", tl)
                tqs = E("scalar", lambda e, w0=w0: e.activation(out=sB[:, 0:512], in_=yt[:, w0:w0 + 512],
                                                                func=AF.Square),
                        waits=SR("yt") + SW("sB"), inc="SC")
                setw("sB", tqs)
                th2 = E("vector", lambda e: e.tensor_copy(out=g3[:, 0:512], in_=sB[:, 0:512]),
                        waits=SR("sB") + SW("g3"), inc="VE")
                setw("g3", th2)
                tl2 = E("vector", lambda e: e.tensor_tensor(out=g4[:, 0:512], in0=sB[:, 0:512],
                                                            in1=g3[:, 0:512], op=ALU.subtract),
                        waits=SW("g4"), inc="VE")
                setw("g4", tl2)
                setr("sB", tl2)
                first = (c == 0)
                stop = (c == NCH - 1)
                E("tensor", lambda e, first=first: e.matmul(pbt[6][0:1, :], ones_b[:, 0:1], g1[:, 0:512],
                                                            start=first, stop=False),
                  waits=SR("g1") + SR("g2") + SR("ones_b") + (SW("pb6") if first else []))
                ty = E("tensor", lambda e, stop=stop: e.matmul(pbt[6][0:1, :], ones_b[:, 0:1], g2[:, 0:512],
                                                               start=False, stop=stop), inc="TE")
                setr("g1", ty)
                setr("g2", ty)
                E("tensor", lambda e, first=first: e.matmul(pbt[5][0:1, :], ones_b[:, 0:1], g3[:, 0:512],
                                                            start=first, stop=False),
                  waits=SR("g3") + SR("g4") + (SW("pb5") if first else []))
                tq2 = E("tensor", lambda e, stop=stop: e.matmul(pbt[5][0:1, :], ones_b[:, 0:1], g4[:, 0:512],
                                                                start=False, stop=stop), inc="TE")
                setr("g3", tq2)
                setr("g4", tq2)
            setw("pb6", ty)
            setw("pb5", tq2)
            e1 = E("vector", lambda e, tt=tt: e.tensor_copy(out=rowb[0:1, tt * 512:(tt + 1) * 512],
                                                            in_=pbt[6][0:1, :]),
                   waits=SR("pb6") + (SW("rowb") if tt == 0 else []), inc="VE")
            setr("pb6", e1)
            e2 = E("vector", lambda e, tt=tt: e.tensor_copy(out=sa[0:1, tt * 512:(tt + 1) * 512],
                                                            in_=pbt[5][0:1, :]),
                   waits=SR("pb5") + (SW("sa") if tt == 0 else []), inc="VE")
            setr("pb5", e2)
            if tt == 0:
                slots["rowb"].w = e2
                slots["rowb"].r = []
                slots["sa"].w = e2
                slots["sa"].r = []
        slots["rowb"].w = e2
        slots["sa"].w = e2
        tv = E("vector", lambda e: e.tensor_scalar(out=sa[0:1, 2048:2048 + S], in0=rowb[0:1, 0:S],
                                                   scalar1=1.0 / D, scalar2=None, op0=ALU.mult),
               inc="VE")
        tv2 = E("vector", lambda e: e.tensor_tensor(out=rowb[0:1, 0:S], in0=sa[0:1, 2048:2048 + S],
                                                    in1=sa[0:1, 2048:2048 + S], op=ALU.mult),
                waits=[tv], inc="VE")
        tv3 = E("vector", lambda e: e.tensor_scalar(out=sa[0:1, 0:S], in0=sa[0:1, 0:S],
                                                    scalar1=1.0 / D, scalar2=None, op0=ALU.mult),
                waits=[tv2], inc="VE")
        tv4 = E("vector", lambda e: e.tensor_tensor(out=sa[0:1, 0:S], in0=sa[0:1, 0:S],
                                                    in1=rowb[0:1, 0:S], op=ALU.subtract),
                waits=[tv3], inc="VE")
        ts = E("scalar", lambda e: e.activation(out=sa[0:1, 0:S], in_=sa[0:1, 0:S],
                                                func=AF.Sqrt, bias=cEps[0:1, :]),
               waits=[tv4], inc="SC")
        tv5 = E("vector", lambda e: e.reciprocal(out=rowb[0:1, 0:S], in_=sa[0:1, 0:S]),
                waits=[ts], inc="VE")
        tv6 = E("vector", lambda e: e.tensor_tensor(out=sa[0:1, 2048:2048 + S],
                                                    in0=sa[0:1, 2048:2048 + S],
                                                    in1=rowb[0:1, 0:S], op=ALU.mult),
                waits=[tv5], inc="VE")
        slots["rowb"].w = tv6
        slots["sa"].w = tv6
        te2 = None
        for tt in range(NTT):
            bk = f"pb{5 + (tt % 2)}"
            tm = E("tensor", lambda e, tt=tt, bk=bk:
                   e.matmul(pbank(bk)[:, 0:512], onesf[0:1, :], rowb[0:1, tt * 512:(tt + 1) * 512],
                            start=True, stop=True),
                   waits=SR("rowb") + SW(bk) + SR("onesf"), inc="TE")
            setw(bk, tm)
            setr("rowb", tm)
            te1 = E("scalar", lambda e, tt=tt, bk=bk: e.copy(out=rrep[:, tt * 512:(tt + 1) * 512],
                                                             in_=pbank(bk)[:, 0:512]),
                    waits=SR(bk) + (SW("rrep") if tt == 0 else []), inc="SC")
            setr(bk, te1)
            if tt == 0:
                slots["rrep"].w = te1
                slots["rrep"].r = []
            tm2 = E("tensor", lambda e, tt=tt, bk=bk:
                    e.matmul(pbank(bk)[:, 0:512], onesf[0:1, :],
                             sa[0:1, 2048 + tt * 512:2048 + (tt + 1) * 512], start=True, stop=True),
                    waits=SR("sa") + [te1] + SR("onesf"), inc="TE")
            setw(bk, tm2)
            setr("sa", tm2)
            te2 = E("scalar", lambda e, tt=tt, bk=bk: e.copy(out=mrrep[:, tt * 512:(tt + 1) * 512],
                                                             in_=pbank(bk)[:, 0:512]),
                    waits=SR(bk) + (SW("mrrep") if tt == 0 else []), inc="SC")
            setr(bk, te2)
            if tt == 0:
                slots["mrrep"].w = te2
                slots["mrrep"].r = []
        slots["rrep"].w = te2
        slots["mrrep"].w = te2

    def mm3(bk, wsl_h, wsl_l, xh_t, xl_t, xsl, extra_waits):
        tok = None
        for kc in range(NCH):
            wh_ = wsl_h(kc)
            wl_ = wsl_l(kc)
            xh_ = xsl(xh_t, kc)
            xl_ = xsl(xl_t, kc)
            w = (extra_waits + SW(bk)) if kc == 0 else []
            E("tensor", lambda e, wh_=wh_, xh_=xh_, kc=kc, bk=bk:
              e.matmul(pbank(bk)[:], wh_, xh_, start=(kc == 0), stop=False), waits=w)
            E("tensor", lambda e, wl_=wl_, xh_=xh_, bk=bk:
              e.matmul(pbank(bk)[:], wl_, xh_, start=False, stop=False))
            tok = E("tensor", lambda e, wh_=wh_, xl_=xl_, kc=kc, bk=bk:
                    e.matmul(pbank(bk)[:], wh_, xl_, start=False, stop=(kc == NCH - 1)),
                    inc=("TE" if kc == NCH - 1 else None))
        setw(bk, tok)
        return tok

    def wload(bname, buf, src_h, src_l, oc):
        t1 = E("sync", lambda e, oc=oc, buf=buf, src_h=src_h: e.dma_start(
            out=buf[:, 0:1024].rearrange("p (kc d) -> p kc d", d=128),
            in_=src_h[:, oc * 128:(oc + 1) * 128].rearrange("(kc p) d -> p kc d", p=128)),
            waits=SW(bname), inc="LW", incn=16)
        t2 = E("sync", lambda e, oc=oc, buf=buf, src_l=src_l: e.dma_start(
            out=buf[:, 1024:2048].rearrange("p (kc d) -> p kc d", d=128),
            in_=src_l[:, oc * 128:(oc + 1) * 128].rearrange("(kc p) d -> p kc d", p=128)),
            inc="LW", incn=16)
        setw(bname, t2)
        return [t1, t2]

    def emit_ffn(s):
        t = None
        for c in range(NCH):
            for j in range(1, NTT):
                col = c * S + j * 512 - 1
                t = E("vector", lambda e, c=c, j=j, col=col:
                      e.tensor_copy(out=bnd[:, (c * NTT + j):(c * NTT + j) + 1], in_=yt[:, col:col + 1]),
                      waits=SR("yt") + (SW("bnd") if (c == 0 and j == 1) else []), inc="VE")
                setr("yt", t)
        setw("bnd", t)
        for tt in range(NTT):
            r0 = tt * 512
            bq = None
            for c in range(NCH):
                base = c * 513
                if tt == 0:
                    bq = E("vector", lambda e, base=base: e.memset(sa[:, base:base + 1], 0.0),
                           waits=(SW("sa") if c == 0 else []), inc="VE")
                else:
                    bq = E("vector", lambda e, base=base, c=c, tt=tt, r0=r0:
                           e.tensor_tensor(out=sa[:, base:base + 1],
                                           in0=bnd[:, c * NTT + tt:c * NTT + tt + 1],
                                           in1=rrep[:, r0 - 1:r0], op=ALU.mult),
                           waits=SR("bnd") + SR("rrep") + (SW("sa") if c == 0 else []), inc="VE")
            if tt != 0:
                bq2 = None
                for c in range(NCH):
                    base = c * 513
                    bq2 = E("vector", lambda e, base=base, r0=r0:
                            e.tensor_tensor(out=sa[:, base:base + 1], in0=sa[:, base:base + 1],
                                            in1=mrrep[:, r0 - 1:r0], op=ALU.subtract),
                            waits=SR("mrrep") + ([bq] if c == 0 else []), inc="VE")
                bq = bq2
            mix_last = None
            for c in range(NCH):
                w0 = c * S + tt * 512
                base = c * 513
                t1 = E("vector", lambda e, base=base, w0=w0, r0=r0:
                       e.tensor_tensor(out=sa[:, base + 1: base + 513], in0=yt[:, w0:w0 + 512],
                                       in1=rrep[:, r0:r0 + 512], op=ALU.mult),
                       waits=SR("yt") + SR("rrep"), inc="VE")
                setr("yt", t1)
                E("vector", lambda e, base=base, r0=r0:
                  e.tensor_tensor(out=sa[:, base + 1: base + 513], in0=sa[:, base + 1: base + 513],
                                  in1=mrrep[:, r0:r0 + 512], op=ALU.subtract),
                  waits=SR("mrrep"), inc="VE")
                E("vector", lambda e, base=base, c=c:
                  e.tensor_tensor(out=sB[:, c * 512:(c + 1) * 512], in0=sa[:, base + 1: base + 513],
                                  in1=sa[:, base: base + 512], op=ALU.subtract),
                  waits=([bq] if c == 0 else []) + (SW("sB") if c == 0 else []), inc="VE")
                E("vector", lambda e, c=c:
                  e.scalar_tensor_tensor(out=g3[:, c * 512:(c + 1) * 512],
                                         in0=sB[:, c * 512:(c + 1) * 512], scalar=muk[:, c:c + 1],
                                         in1=sa[:, c * 513: c * 513 + 512],
                                         op0=ALU.mult, op1=ALU.add),
                  waits=SR("muk") + (SW("g3") if c == 0 else []), inc="VE")
                E("vector", lambda e, c=c:
                  e.scalar_tensor_tensor(out=iost[:, 0:512],
                                         in0=sB[:, c * 512:(c + 1) * 512], scalar=muk[:, c:c + 1],
                                         in1=sa[:, c * 513: c * 513 + 512],
                                         op0=ALU.mult, op1=ALU.add),
                  waits=SW("iost"), inc="VE")
                t7 = E("vector", lambda e, c=c:
                       e.tensor_tensor(out=g4[:, c * 512:(c + 1) * 512], in0=iost[:, 0:512],
                                       in1=g3[:, c * 512:(c + 1) * 512], op=ALU.subtract),
                       waits=(SW("g4") if c == 0 else []), inc="VE")
                slots["iost"].w = t7
                slots["iost"].r = []
                E("vector", lambda e, c=c:
                  e.scalar_tensor_tensor(out=g1[:, c * 512:(c + 1) * 512],
                                         in0=sB[:, c * 512:(c + 1) * 512], scalar=mur[:, c:c + 1],
                                         in1=sa[:, c * 513: c * 513 + 512],
                                         op0=ALU.mult, op1=ALU.add),
                  waits=SR("mur") + (SW("g1") if c == 0 else []), inc="VE")
                E("vector", lambda e, c=c:
                  e.scalar_tensor_tensor(out=iost[:, 512:1024],
                                         in0=sB[:, c * 512:(c + 1) * 512], scalar=mur[:, c:c + 1],
                                         in1=sa[:, c * 513: c * 513 + 512],
                                         op0=ALU.mult, op1=ALU.add),
                  inc="VE")
                t9 = E("vector", lambda e, c=c:
                       e.tensor_tensor(out=g2[:, c * 512:(c + 1) * 512], in0=iost[:, 512:1024],
                                       in1=g1[:, c * 512:(c + 1) * 512], op=ALU.subtract),
                       waits=(SW("g2") if c == 0 else []), inc="VE")
                slots["iost"].w = t9
                mix_last = t9
            for nm in ("g1", "g2", "g3", "g4", "sa", "sB"):
                slots[nm].w = mix_last
                slots[nm].r = []
            ter = None
            for oc in range(NCH):
                wts = wload("wbuf", wbuf, wsp["wrh"], wsp["wrl"], oc)
                bk = f"pb{0 + (oc % 2)}"
                tok = mm3(bk,
                          lambda kc: wbuf[:, kc * 128:(kc + 1) * 128],
                          lambda kc: wbuf[:, 1024 + kc * 128:1024 + (kc + 1) * 128],
                          g1, g2, lambda t_, kc: t_[:, kc * 512:(kc + 1) * 512],
                          SR("g1") + SR("g2") + wts)
                setr("g1", tok)
                setr("g2", tok)
                setr("wbuf", tok)
                ter = E("scalar", lambda e, oc=oc, bk=bk:
                        e.activation(out=sa[:, oc * 512:(oc + 1) * 512], in_=pbank(bk)[:],
                                     func=AF.Sigmoid),
                        waits=SR(bk) + (SW("sa") if oc == 0 else []), inc="SC")
                setr(bk, ter)
                if oc == 0:
                    slots["sa"].w = ter
                    slots["sa"].r = []
            slots["sa"].w = ter
            tk4 = None
            for oc in range(NCH):
                wts = wload("wbuf2", wbuf2, wsp["wkh"], wsp["wkl"], oc)
                bk = f"pb{2 + (oc % 2)}"
                tok = mm3(bk,
                          lambda kc: wbuf2[:, kc * 128:(kc + 1) * 128],
                          lambda kc: wbuf2[:, 1024 + kc * 128:1024 + (kc + 1) * 128],
                          g3, g4, lambda t_, kc: t_[:, kc * 512:(kc + 1) * 512],
                          SR("g3") + SR("g4") + wts)
                setr("g3", tok)
                setr("g4", tok)
                setr("wbuf2", tok)
                tk1 = E("scalar", lambda e, oc=oc, bk=bk:
                        e.activation(out=sB[:, oc * 512:(oc + 1) * 512], in_=pbank(bk)[:],
                                     func=AF.Relu),
                        waits=SR(bk) + (SW("sB") if oc == 0 else []), inc="SC")
                setr(bk, tk1)
                if oc == 0:
                    slots["sB"].w = tk1
                    slots["sB"].r = []
                tk2 = E("scalar", lambda e, oc=oc:
                        e.activation(out=sB[:, oc * 512:(oc + 1) * 512],
                                     in_=sB[:, oc * 512:(oc + 1) * 512], func=AF.Square),
                        inc="SC")
                tk3 = E("vector", lambda e, oc=oc: e.tensor_copy(out=g1[:, oc * 512:(oc + 1) * 512],
                                                                 in_=sB[:, oc * 512:(oc + 1) * 512]),
                        waits=[tk2] + (SW("g1") if oc == 0 else []), inc="VE")
                if oc == 0:
                    slots["g1"].w = tk3
                    slots["g1"].r = []
                tk4 = E("vector", lambda e, oc=oc: e.tensor_tensor(out=g2[:, oc * 512:(oc + 1) * 512],
                                                                   in0=sB[:, oc * 512:(oc + 1) * 512],
                                                                   in1=g1[:, oc * 512:(oc + 1) * 512],
                                                                   op=ALU.subtract),
                        waits=(SW("g2") if oc == 0 else []), inc="VE")
                if oc == 0:
                    slots["g2"].w = tk4
                    slots["g2"].r = []
            slots["g1"].w = tk4
            slots["g2"].w = tk4
            for oc in range(NCH):
                wts = wload("wbuf", wbuf, wsp["wvh"], wsp["wvl"], oc)
                bk = f"pb{4 + (oc % 2)}"
                tok = mm3(bk,
                          lambda kc: wbuf[:, kc * 128:(kc + 1) * 128],
                          lambda kc: wbuf[:, 1024 + kc * 128:1024 + (kc + 1) * 128],
                          g1, g2, lambda t_, kc: t_[:, kc * 512:(kc + 1) * 512],
                          SR("g1") + SR("g2") + wts)
                setr("g1", tok)
                setr("g2", tok)
                setr("wbuf", tok)
                w0 = oc * S + tt * 512
                tv1 = E("vector", lambda e, oc=oc, bk=bk:
                        e.tensor_tensor(out=iost[:, 0:512], in0=pbank(bk)[:],
                                        in1=sa[:, oc * 512:(oc + 1) * 512], op=ALU.mult),
                        waits=SR(bk) + SR("sa") + SW("iost"), inc="VE")
                setr(bk, tv1)
                setr("sa", tv1)
                slots["iost"].w = tv1
                slots["iost"].r = []
                tv2 = E("vector", lambda e, w0=w0:
                        e.scalar_tensor_tensor(out=yt[:, w0:w0 + 512], in0=iost[:, 0:512],
                                               scalar=0.5, in1=yt[:, w0:w0 + 512],
                                               op0=ALU.mult, op1=ALU.add),
                        waits=SR("yt"), inc="VE")
                slots["iost"].r.append(tv2)
                slots["yt"].w = tv2
                slots["yt"].r = []

    pf_slots = {off: Slot() for off in (0, 512, 1024, 1536)}
    pp_slots = {off: Slot() for off in (0, 512, 1024, 1536)}

    def emit_attn(s):
        qn_toks = []
        for tt in range(NTT):
            r0 = tt * 512
            for c in range(NCH):
                w0 = c * S + tt * 512
                t1 = E("vector", lambda e, c=c, w0=w0, r0=r0:
                       e.tensor_tensor(out=sa[:, c * 512:(c + 1) * 512], in0=yt[:, w0:w0 + 512],
                                       in1=rrep[:, r0:r0 + 512], op=ALU.mult),
                       waits=SR("yt") + SR("rrep") + (SW("sa") if c == 0 else []), inc="VE")
                setr("yt", t1)
                if c == 0:
                    slots["sa"].w = t1
                    slots["sa"].r = []
                E("vector", lambda e, c=c, r0=r0:
                  e.tensor_tensor(out=sa[:, c * 512:(c + 1) * 512], in0=sa[:, c * 512:(c + 1) * 512],
                                  in1=mrrep[:, r0:r0 + 512], op=ALU.subtract),
                  waits=SR("mrrep"), inc="VE")
                t3 = E("vector", lambda e, c=c: e.tensor_copy(out=g1[:, c * 512:(c + 1) * 512],
                                                              in_=sa[:, c * 512:(c + 1) * 512]),
                       waits=(SW("g1") if c == 0 else []), inc="VE")
                if c == 0:
                    slots["g1"].w = t3
                    slots["g1"].r = []
                t4 = E("vector", lambda e, c=c: e.tensor_tensor(out=g2[:, c * 512:(c + 1) * 512],
                                                                in0=sa[:, c * 512:(c + 1) * 512],
                                                                in1=g1[:, c * 512:(c + 1) * 512],
                                                                op=ALU.subtract),
                       waits=(SW("g2") if c == 0 else []), inc="VE")
                if c == 0:
                    slots["g2"].w = t4
                    slots["g2"].r = []
            slots["g1"].w = t4
            slots["g2"].w = t4
            slots["sa"].w = t4
            for oc in range(3 * NCH):
                wb, wbn = (wbuf, "wbuf") if oc % 2 == 0 else (wbuf2, "wbuf2")
                wts = wload(wbn, wb, wsp["wqh"], wsp["wql"], oc)
                bk = f"pb{0 + (oc % 2)}"
                tok = mm3(bk,
                          lambda kc, wb=wb: wb[:, kc * 128:(kc + 1) * 128],
                          lambda kc, wb=wb: wb[:, 1024 + kc * 128:1024 + (kc + 1) * 128],
                          g1, g2, lambda t_, kc: t_[:, kc * 512:(kc + 1) * 512],
                          SR("g1") + SR("g2") + wts)
                setr("g1", tok)
                setr("g2", tok)
                setr(wbn, tok)
                hp = oc % NCH
                if oc < 2 * NCH:
                    is_q = oc < NCH
                    col = hp * 512
                    e1 = E("scalar", lambda e, col=col, bk=bk:
                           e.copy(out=g3[:, col:col + 512], in_=pbank(bk)[:]),
                           waits=SR(bk) + list(slots["g3"].r) + ([slots["g3"].w] if slots["g3"].w else []),
                           inc="SC")
                    slots["g3"].w = e1
                    slots["g3"].r = []
                    e2 = E("vector", lambda e, col=col, bk=bk:
                           e.tensor_tensor(out=g4[:, col:col + 512], in0=pbank(bk)[:],
                                           in1=g3[:, col:col + 512], op=ALU.subtract),
                           waits=SR(bk) + [e1] + list(slots["g4"].r) + ([slots["g4"].w] if slots["g4"].w else []),
                           inc="VE")
                    slots["g4"].w = e2
                    slots["g4"].r = []
                    E("tensor", lambda e, col=col: e.matmul(pbt[6][:], rt_b[:], g3[:, col:col + 512],
                                                            start=True, stop=False),
                      waits=[e1] + SW("pb6") + SR("rt_b"))
                    tr = E("tensor", lambda e, col=col: e.matmul(pbt[6][:], rt_b[:], g4[:, col:col + 512],
                                                                 start=False, stop=True),
                           waits=[e2], inc="TE")
                    setw("pb6", tr)
                    tv1 = E("vector", lambda e, bk=bk, r0=r0:
                            e.tensor_tensor(out=sB[:, 0:512], in0=pbank(bk)[:],
                                            in1=cosr[:, r0:r0 + 512], op=ALU.mult),
                            waits=SR(bk) + SR("cosr") + SW("sB"), inc="VE")
                    slots["sB"].w = tv1
                    slots["sB"].r = []
                    setr(bk, tv1)
                    tv2 = E("vector", lambda e, r0=r0:
                            e.tensor_tensor(out=sB[:, 512:1024], in0=pbt[6][:],
                                            in1=sinr[:, r0:r0 + 512], op=ALU.mult),
                            waits=SR("pb6") + SR("sinr"), inc="VE")
                    setr("pb6", tv2)
                    E("vector", lambda e: e.tensor_tensor(out=sB[:, 0:512], in0=sB[:, 0:512],
                                                          in1=sB[:, 512:1024], op=ALU.add),
                      inc="VE")
                    tv4 = E("vector", lambda e, col=col: e.tensor_copy(out=g3[:, col:col + 512],
                                                                       in_=sB[:, 0:512]),
                            waits=[tr], inc="VE")
                    slots["g3"].w = tv4
                    tv5 = E("vector", lambda e, col=col: e.tensor_tensor(out=g4[:, col:col + 512],
                                                                         in0=sB[:, 0:512],
                                                                         in1=g3[:, col:col + 512],
                                                                         op=ALU.subtract),
                            inc="VE")
                    slots["g4"].w = tv5
                    slots["sB"].w = tv5
                    dst = q_sp if is_q else k_sp
                    s1 = E("gpsimd", lambda e, dst=dst, hp=hp, tt=tt, col=col:
                           e.dma_start(out=dst["h"][hp * 128:(hp + 1) * 128, tt * 512:(tt + 1) * 512],
                                       in_=g3[:, col:col + 512]),
                           waits=[tv4], inc="ST", incn=16)
                    s2 = E("gpsimd", lambda e, dst=dst, hp=hp, tt=tt, col=col:
                           e.dma_start(out=dst["l"][hp * 128:(hp + 1) * 128, tt * 512:(tt + 1) * 512],
                                       in_=g4[:, col:col + 512]),
                           waits=[tv5], inc="ST", incn=16)
                    setr("g3", s1)
                    setr("g4", s2)
                    tq1 = E("vector", lambda e, col=col:
                            e.tensor_tensor(out=vbuf[:, 0:512], in0=g3[:, col:col + 512],
                                            in1=g3[:, col:col + 512], op=ALU.mult),
                            waits=[tv4] + SW("vbuf"), inc="VE")
                    setw("vbuf", tq1)
                    setr("g3", tq1)
                    for j in range(2):
                        head = 2 * hp + j
                        tqm = E("tensor", lambda e, j=j: e.matmul(pbt[5][0:1, :], blk_b[:, j:j + 1],
                                                                  vbuf[:, 0:512],
                                                                  start=True, stop=True),
                                waits=[tq1] + SW("pb5") + SR("blk_b"), inc="TE")
                        setw("pb5", tqm)
                        setr("vbuf", tqm)
                        coloff = (0 if is_q else 64) + head * 4 + tt
                        trm = E("vector", lambda e, coloff=coloff:
                                e.tensor_reduce(out=rowb[0:1, coloff:coloff + 1],
                                                in_=pbt[5][0:1, :], axis=AX.X, op=ALU.max),
                                waits=SR("pb5") + (qn_toks[-1:] if qn_toks else SW("rowb")),
                                inc="VE")
                        setr("pb5", trm)
                        qn_toks.append(trm)
                else:
                    e1 = E("scalar", lambda e, bk=bk: e.copy(out=g3[:, 0:512], in_=pbank(bk)[:]),
                           waits=SR(bk) + list(slots["g3"].r) + ([slots["g3"].w] if slots["g3"].w else []),
                           inc="SC")
                    slots["g3"].w = e1
                    slots["g3"].r = []
                    e2 = E("vector", lambda e, bk=bk: e.tensor_tensor(out=g4[:, 0:512],
                                                                      in0=pbank(bk)[:],
                                                                      in1=g3[:, 0:512], op=ALU.subtract),
                           waits=SR(bk) + [e1] + list(slots["g4"].r) + ([slots["g4"].w] if slots["g4"].w else []),
                           inc="VE")
                    slots["g4"].w = e2
                    slots["g4"].r = []
                    setr(bk, e2)
                    for half, gt, gn, etok in (("h", g3, "g3", e1), ("l", g4, "g4", e2)):
                        for sub in range(4):
                            tmm = E("tensor", lambda e, gt=gt, sub=sub:
                                    e.transpose(pbb[:, 0:128], gt[:, sub * 128:(sub + 1) * 128], identb[:]),
                                    waits=[etok] + SW("pbb") + SR("identb"), inc="TE")
                            setw("pbb", tmm)
                            setr(gn, tmm)
                            tev = E("vector", lambda e: e.tensor_copy(out=vbuf[:, 512:640], in_=pbb[:, 0:128]),
                                    waits=SR("pbb") + SW("vbuf"), inc="VE")
                            setw("vbuf", tev)
                            setr("pbb", tev)
                            tst = E("gpsimd", lambda e, half=half, hp=hp, tt=tt, sub=sub:
                                    e.dma_start(out=v_sp[half][tt * 512 + sub * 128: tt * 512 + (sub + 1) * 128,
                                                               hp * 128:(hp + 1) * 128],
                                                in_=vbuf[:, 512:640]),
                                    waits=[tev], inc="ST", incn=16)
                            setr("vbuf", tst)
        tprev = qn_toks[-1]
        for h in range(H):
            tprev = E("vector", lambda e, h=h: e.tensor_reduce(out=rowb[0:1, 128 + h:129 + h],
                                                               in_=rowb[0:1, h * 4:h * 4 + 4],
                                                               axis=AX.X, op=ALU.max),
                      waits=[tprev], inc="VE")
            tprev = E("vector", lambda e, h=h: e.tensor_reduce(out=rowb[0:1, 144 + h:145 + h],
                                                               in_=rowb[0:1, 64 + h * 4:64 + h * 4 + 4],
                                                               axis=AX.X, op=ALU.max),
                      waits=[tprev], inc="VE")
        tsq = E("scalar", lambda e: e.activation(out=rowb[0:1, 128:160], in_=rowb[0:1, 128:160],
                                                 func=AF.Sqrt),
                waits=[tprev], inc="SC")
        tvc = E("vector", lambda e: e.tensor_tensor(out=rowb[0:1, 160:176], in0=rowb[0:1, 128:144],
                                                    in1=rowb[0:1, 144:160], op=ALU.mult),
                waits=[tsq], inc="VE")
        tvc2 = E("vector", lambda e: e.tensor_scalar(out=rowb[0:1, 160:176], in0=rowb[0:1, 160:176],
                                                     scalar1=-1.01 * 0.125, scalar2=-0.01,
                                                     op0=ALU.mult, op1=ALU.add),
                 waits=[tvc], inc="VE")
        trep = E("tensor", lambda e: e.matmul(pbt[5][:, 0:16], onesf[0:1, :], rowb[0:1, 160:176],
                                              start=True, stop=True),
                 waits=[tvc2] + SW("pb5") + SR("onesf"), inc="TE")
        setw("pb5", trep)
        setr("rowb", trep)
        slots["rowb"].w = tvc2
        tcrep = E("scalar", lambda e: e.copy(out=crep[:], in_=pbt[5][:, 0:16]),
                  waits=SR("pb5"), inc="SC")
        setr("pb5", tcrep)
        setw("crep", tcrep)
        if dbg:
            E("gpsimd", lambda e: e.dma_start(out=dbg_crep[:, :], in_=crep[:]),
              waits=SR("crep"), inc="ST", incn=16)

        trans = SW("sB") + SW("g3") + SW("g4")
        for off in (0, 512, 1024, 1536):
            pf_slots[off].r = list(trans)
            pf_slots[off].w = None
            pp_slots[off].r = list(trans)
            pp_slots[off].w = None
        pf_ap = sB
        pph_ap = g3

        ta3 = None
        for tt in range(NTT):
            nkt = (tt + 1) * 4
            for hp in range(NCH):
                headA, headB = 2 * hp, 2 * hp + 1
                lk1 = E("sync", lambda e, hp=hp, nkt=nkt:
                        e.dma_start(out=kbuf[:, 0:nkt * 128],
                                    in_=k_sp["h"][hp * 128:(hp + 1) * 128, 0:nkt * 128]),
                        waits=SW("kbuf"), inc="LD", incn=16)
                lk2 = E("sync", lambda e, hp=hp, nkt=nkt:
                        e.dma_start(out=kbuf[:, S:S + nkt * 128],
                                    in_=k_sp["l"][hp * 128:(hp + 1) * 128, 0:nkt * 128]),
                        inc="LD", incn=16)
                setw("kbuf", lk2)
                lv1 = E("sync", lambda e, hp=hp, nkt=nkt:
                        e.dma_start(out=vbuf[:, 0:nkt * 128].rearrange("p (n c) -> p n c", c=128),
                                    in_=v_sp["h"][0:nkt * 128, hp * 128:(hp + 1) * 128]
                                    .rearrange("(n p) c -> p n c", p=128)),
                        waits=SW("vbuf"), inc="LD", incn=16)
                lv2 = E("sync", lambda e, hp=hp, nkt=nkt:
                        e.dma_start(out=vbuf[:, S:S + nkt * 128].rearrange("p (n c) -> p n c", c=128),
                                    in_=v_sp["l"][0:nkt * 128, hp * 128:(hp + 1) * 128]
                                    .rearrange("(n p) c -> p n c", p=128)),
                        inc="LD", incn=16)
                setw("vbuf", lv2)
                lq1 = E("sync", lambda e, hp=hp, tt=tt:
                        e.dma_start(out=qbuf[:, 0:512],
                                    in_=q_sp["h"][hp * 128:(hp + 1) * 128, tt * 512:(tt + 1) * 512]),
                        waits=SW("qbuf"), inc="LD", incn=16)
                lq2 = E("sync", lambda e, hp=hp, tt=tt:
                        e.dma_start(out=qbuf[:, 512:1024],
                                    in_=q_sp["l"][hp * 128:(hp + 1) * 128, tt * 512:(tt + 1) * 512]),
                        inc="LD", incn=16)
                setw("qbuf", lq2)
                last_sc = {}
                pv_tok = den_tok = None
                for c in range(nkt):
                    ib = c % 2
                    bkA = f"pb{0 + ib * 2}"
                    bkB = f"pb{1 + ib * 2}"
                    kc0 = c * 128
                    for (bkX, p0) in ((bkA, 0), (bkB, 64)):
                        w0 = SW(bkX) + [lk1, lk2, lq1, lq2]
                        E("tensor", lambda e, bkX=bkX, p0=p0, kc0=kc0:
                          e.matmul(pbank(bkX)[:], kbuf[p0:p0 + 64, kc0:kc0 + 128],
                                   qbuf[p0:p0 + 64, 0:512], start=True, stop=False,
                                   tile_position=(p0, 0)),
                          waits=w0)
                        E("tensor", lambda e, bkX=bkX, p0=p0, kc0=kc0:
                          e.matmul(pbank(bkX)[:], kbuf[p0:p0 + 64, kc0:kc0 + 128],
                                   qbuf[p0:p0 + 64, 512:1024], start=False, stop=False,
                                   tile_position=(p0, 0)))
                        tsc = E("tensor", lambda e, bkX=bkX, p0=p0, kc0=kc0:
                                e.matmul(pbank(bkX)[:], kbuf[p0:p0 + 64, S + kc0:S + kc0 + 128],
                                         qbuf[p0:p0 + 64, 0:512], start=False, stop=True,
                                         tile_position=(p0, 0)), inc="TE")
                        setw(bkX, tsc)
                        last_sc[bkX] = tsc
                    partial = (c >= tt * 4)
                    for (bkX, head, off0) in ((bkA, headA, 0), (bkB, headB, 1024)):
                        pcol = off0 + ib * 512
                        ps_ = pf_slots[pcol]
                        pp_ = pp_slots[pcol]
                        te = E("scalar", lambda e, bkX=bkX, head=head, pcol=pcol:
                               e.activation(out=pf_ap[:, pcol:pcol + 512], in_=pbank(bkX)[:],
                                            func=AF.Exp, scale=0.125, bias=crep[:, head:head + 1]),
                               waits=SR(bkX) + SR("crep") + list(ps_.r) + ([ps_.w] if ps_.w else []),
                               inc="SC")
                        setr(bkX, te)
                        ps_.w = te
                        ps_.r = []
                        if partial:
                            tm = E("gpsimd", lambda e, pcol=pcol, tt=tt, c=c:
                                   e.affine_select(out=pf_ap[:, pcol:pcol + 512],
                                                   in_=pf_ap[:, pcol:pcol + 512],
                                                   pattern=[[1, 512]], compare_op=ALU.is_ge,
                                                   fill=0.0, base=tt * 512 - c * 128,
                                                   channel_multiplier=-1),
                                   waits=[te], inc="GP")
                            ps_.w = tm
                        th = E("vector", lambda e, pcol=pcol:
                               e.tensor_copy(out=pph_ap[:, pcol:pcol + 512], in_=pf_ap[:, pcol:pcol + 512]),
                               waits=[ps_.w] + list(pp_.r) + ([pp_.w] if pp_.w else []), inc="VE")
                        ps_.r.append(th)
                        tl = E("vector", lambda e, pcol=pcol:
                               e.tensor_tensor(out=pph_ap[:, 2048 + pcol:2048 + pcol + 512],
                                               in0=pf_ap[:, pcol:pcol + 512],
                                               in1=pph_ap[:, pcol:pcol + 512], op=ALU.subtract),
                               inc="VE")
                        ps_.r.append(tl)
                        pp_.w = tl
                        pp_.r = []
                    stop = (c == nkt - 1)
                    first = (c == 0)
                    for (off0, p0) in ((0, 0), (1024, 64)):
                        pcol = off0 + ib * 512
                        pp_ = pp_slots[pcol]
                        colA = kc0 + (0 if p0 == 0 else 64)
                        ww = ([pp_.w] if pp_.w else []) + [lv1, lv2]
                        if first:
                            ww += SW("pb4") + SW("pb5")
                        E("tensor", lambda e, pcol=pcol, p0=p0, colA=colA, first=first:
                          e.matmul(pbt[4][p0:p0 + 64, :], vbuf[:, colA:colA + 64],
                                   pph_ap[:, pcol:pcol + 512], start=first, stop=False,
                                   tile_position=(0, p0)),
                          waits=ww)
                        E("tensor", lambda e, pcol=pcol, p0=p0, colA=colA:
                          e.matmul(pbt[4][p0:p0 + 64, :], vbuf[:, S + colA:S + colA + 64],
                                   pph_ap[:, pcol:pcol + 512], start=False, stop=False,
                                   tile_position=(0, p0)))
                        tpv = E("tensor", lambda e, pcol=pcol, p0=p0, colA=colA, stop=stop:
                                e.matmul(pbt[4][p0:p0 + 64, :], vbuf[:, colA:colA + 64],
                                         pph_ap[:, 2048 + pcol:2048 + pcol + 512], start=False,
                                         stop=stop, tile_position=(0, p0)),
                                inc=("TE" if (stop and p0 == 64) else None))
                        ob0 = 0 if p0 == 0 else 64
                        E("tensor", lambda e, pcol=pcol, p0=p0, ob0=ob0, first=first:
                          e.matmul(pbt[5][p0:p0 + 64, :], ones_b[:, ob0:ob0 + 64],
                                   pph_ap[:, pcol:pcol + 512], start=first, stop=False,
                                   tile_position=(0, p0)),
                          waits=SR("ones_b"))
                        tdn = E("tensor", lambda e, pcol=pcol, p0=p0, ob0=ob0, stop=stop:
                                e.matmul(pbt[5][p0:p0 + 64, :], ones_b[:, ob0:ob0 + 64],
                                         pph_ap[:, 2048 + pcol:2048 + pcol + 512], start=False,
                                         stop=stop, tile_position=(0, p0)),
                                inc=("TE" if (stop and p0 == 64) else None))
                        pp_.r.append(tpv)
                        pp_.r.append(tdn)
                        if stop and p0 == 64:
                            pv_tok, den_tok = tpv, tdn
                setw("pb4", pv_tok)
                setw("pb5", den_tok)
                setr("kbuf", last_sc[bkA])
                setr("kbuf", last_sc[bkB])
                setr("qbuf", last_sc[bkA])
                setr("qbuf", last_sc[bkB])
                setr("vbuf", pv_tok)
                trc = E("vector", lambda e: e.reciprocal(out=rden[:], in_=pbt[5][:]),
                        waits=SR("pb5") + SW("rden"), inc="VE")
                setw("rden", trc)
                setr("pb5", trc)
                ta1 = E("vector", lambda e, hp=hp:
                        e.tensor_tensor(out=sa[:, hp * 512:(hp + 1) * 512], in0=pbt[4][:],
                                        in1=rden[:], op=ALU.mult),
                        waits=[trc] + SR("pb4") + (SW("sa") if hp == 0 else []), inc="VE")
                setr("pb4", ta1)
                setr("rden", ta1)
                if hp == 0:
                    slots["sa"].w = ta1
                    slots["sa"].r = []
                ta2 = E("vector", lambda e, hp=hp: e.tensor_copy(out=g1[:, hp * 512:(hp + 1) * 512],
                                                                 in_=sa[:, hp * 512:(hp + 1) * 512]),
                        waits=(SW("g1") if hp == 0 else []), inc="VE")
                if hp == 0:
                    slots["g1"].w = ta2
                    slots["g1"].r = []
                ta3 = E("vector", lambda e, hp=hp: e.tensor_tensor(out=g2[:, hp * 512:(hp + 1) * 512],
                                                                   in0=sa[:, hp * 512:(hp + 1) * 512],
                                                                   in1=g1[:, hp * 512:(hp + 1) * 512],
                                                                   op=ALU.subtract),
                        waits=(SW("g2") if hp == 0 else []), inc="VE")
                if hp == 0:
                    slots["g2"].w = ta3
                    slots["g2"].r = []
            slots["g1"].w = ta3
            slots["g2"].w = ta3
            for oc in range(NCH):
                wts = wload("wbuf", wbuf, wsp["woh"], wsp["wol"], oc)
                bk = f"pb{4 + (oc % 2)}"
                tok = mm3(bk,
                          lambda kc: wbuf[:, kc * 128:(kc + 1) * 128],
                          lambda kc: wbuf[:, 1024 + kc * 128:1024 + (kc + 1) * 128],
                          g1, g2, lambda t_, kc: t_[:, kc * 512:(kc + 1) * 512],
                          SR("g1") + SR("g2") + wts)
                setr("g1", tok)
                setr("g2", tok)
                setr("wbuf", tok)
                w0 = oc * S + tt * 512
                tv = E("vector", lambda e, w0=w0, bk=bk:
                       e.tensor_tensor(out=yt[:, w0:w0 + 512], in0=pbank(bk)[:],
                                       in1=yt[:, w0:w0 + 512], op=ALU.add),
                       waits=SR(bk) + SR("yt"), inc="VE")
                setr(bk, tv)
                slots["yt"].w = tv
                slots["yt"].r = []
        allp = []
        for off in (0, 512, 1024, 1536):
            allp += list(pf_slots[off].r) + ([pf_slots[off].w] if pf_slots[off].w else [])
            allp += list(pp_slots[off].r) + ([pp_slots[off].w] if pp_slots[off].w else [])
        slots["sB"].r.extend(allp)
        slots["g3"].r.extend(allp)

    emit_consts()
    emit_wsplit()
    for s in range(SEQ):
        emit_load_x(s)
        if "ffn1" in phases:
            emit_ln(s)
            emit_ffn(s)
        if "attn" in phases:
            emit_ln(s)
            emit_attn(s)
        if "ffn2" in phases:
            emit_ln(s)
            emit_ffn(s)
        emit_store(s)

    with contextlib.ExitStack() as bstack:
        sems = {nm: bstack.enter_context(nc.semaphore(nm)) for nm in SEMNAMES}
        blk = bstack.enter_context(nc.Block())
        final_st = sched.cnt.get("ST", 0)

        @blk.sync
        def _(eng):
            sched.replay("sync", eng, sems)

        @blk.gpsimd
        def _(eng):
            sched.replay("gpsimd", eng, sems)
            eng.wait_ge(sems["ST"], final_st)

        @blk.tensor
        def _(eng):
            sched.replay("tensor", eng, sems)

        @blk.vector
        def _(eng):
            sched.replay("vector", eng, sems)

        @blk.scalar
        def _(eng):
            sched.replay("scalar", eng, sems)

    stack.close()
    return nc


_NC_CACHE = {}


def kernel(L0_attn_x, L0_attn_qkv, L0_attn_wo, L0_ffn_W_r, L0_ffn_W_k,
           L0_ffn_W_v, L0_ffn_mu_r, L0_ffn_mu_k, _phases=("ffn1", "attn", "ffn2"),
           _dbg=False):
    x = np.ascontiguousarray(np.asarray(L0_attn_x, dtype=np.float32).reshape(B * S, D))
    wq = np.ascontiguousarray(np.asarray(L0_attn_qkv, dtype=np.float32).T)
    wo = np.ascontiguousarray(np.asarray(L0_attn_wo, dtype=np.float32).T)
    wr = np.ascontiguousarray(np.asarray(L0_ffn_W_r, dtype=np.float32).T)
    wk = np.ascontiguousarray(np.asarray(L0_ffn_W_k, dtype=np.float32).T)
    wv = np.ascontiguousarray(np.asarray(L0_ffn_W_v, dtype=np.float32).T)
    mur_ = np.ascontiguousarray(np.asarray(L0_ffn_mu_r, dtype=np.float32))
    muk_ = np.ascontiguousarray(np.asarray(L0_ffn_mu_k, dtype=np.float32))

    key = (_phases, _dbg)
    if key not in _NC_CACHE:
        _NC_CACHE[key] = build_program(phases=_phases, dbg=_dbg)
    nc = _NC_CACHE[key]

    in_maps = []
    for c in range(NCORE):
        shard = x[c * SEQ * S:(c + 1) * SEQ * S]
        in_maps.append({"x_in": shard, "wq_t": wq, "wo_t": wo, "wr_t": wr,
                        "wk_t": wk, "wv_t": wv, "mu_r": mur_, "mu_k": muk_})
    res = run_bass_kernel_spmd(nc, in_maps, core_ids=list(range(NCORE)))
    outs = [np.asarray(r["out"]) for r in res.results]
    full = np.concatenate(outs, axis=0).reshape(B, S, D)
    if _dbg:
        return full, res.results
    return full


# version 03
# speedup vs baseline: 1.2127x; 1.2127x over previous
import contextlib
import numpy as np
import concourse.bass as bass
import concourse.mybir as mybir
from concourse.bass_utils import run_bass_kernel_spmd

F32 = mybir.dt.float32
I32 = mybir.dt.int32
BF16 = mybir.dt.bfloat16
AF = mybir.ActivationFunctionType
ALU = mybir.AluOpType
AX = mybir.AxisListType

B, S, D = 32, 2048, 1024
H, DH = 16, 64
NCORE = 8
SEQ = B // NCORE
NCH = D // 128
NTT = S // 512
NT2 = S // 128

INV_2PI = float(1.0 / (2 * np.pi))
HALF_PI = float(np.pi / 2)
T2PI_HI = float(np.float32(round((2 * np.pi) * 2048.0) / 2048.0))
T2PI_LO = float(np.float32(2 * np.pi - T2PI_HI))
EPS = 1e-5

ENGINES = ("sync", "gpsimd", "tensor", "vector", "scalar")


class Slot:
    __slots__ = ("w", "r")

    def __init__(self):
        self.w = None
        self.r = []


class Sched:
    def __init__(self):
        self.ops = {e: [] for e in ENGINES}
        self.cnt = {}
        self.hiwater = {e: {} for e in ENGINES}

    def op(self, eng, fn, waits=(), inc=None, incn=1):
        hw = self.hiwater[eng]
        pruned = {}
        for t in waits:
            if t is None:
                continue
            s, v = t
            if hw.get(s, -1) >= v or pruned.get(s, -1) >= v:
                continue
            pruned[s] = v
        for s, v in pruned.items():
            hw[s] = v
        tok = None
        if inc is not None:
            self.cnt[inc] = self.cnt.get(inc, 0) + incn
            tok = (inc, self.cnt[inc])
        self.ops[eng].append((fn, tuple(pruned.items()), inc, incn))
        return tok

    def replay(self, eng_name, eng, sems):
        for fn, waits, inc, incn in self.ops[eng_name]:
            for s, v in waits:
                eng.wait_ge(sems[s], v)
            instr = fn(eng)
            if inc is not None:
                if instr is None:
                    eng.sem_inc(sems[inc], incn)
                else:
                    instr.then_inc(sems[inc], incn)


def build_program(phases=("ffn1", "attn", "ffn2"), dbg=False):
    nc = bass.Bass("TRN2", target_bir_lowering=False)

    x_in = nc.dram_tensor("x_in", [SEQ * S, D], F32, kind="ExternalInput").ap()
    wq_t = nc.dram_tensor("wq_t", [D, 3 * D], F32, kind="ExternalInput").ap()
    wo_t = nc.dram_tensor("wo_t", [D, D], F32, kind="ExternalInput").ap()
    wr_t = nc.dram_tensor("wr_t", [D, D], F32, kind="ExternalInput").ap()
    wk_t = nc.dram_tensor("wk_t", [D, D], F32, kind="ExternalInput").ap()
    wv_t = nc.dram_tensor("wv_t", [D, D], F32, kind="ExternalInput").ap()
    mu_r = nc.dram_tensor("mu_r", [D], F32, kind="ExternalInput").ap()
    mu_k = nc.dram_tensor("mu_k", [D], F32, kind="ExternalInput").ap()
    out = nc.dram_tensor("out", [SEQ * S, D], F32, kind="ExternalOutput").ap()

    wsp = {}
    for nm, dout in [("wq", 3 * D), ("wo", D), ("wr", D), ("wk", D), ("wv", D)]:
        for h in ("h", "l"):
            wsp[nm + h] = nc.dram_tensor(f"wsp_{nm}{h}", [dout // 128, 128, 8 * 128], BF16).ap()
    kw = dict(kind="ExternalOutput") if dbg else {}
    q_sp = {h: nc.dram_tensor(f"q_sp{h}", [D, S], BF16, **kw).ap() for h in "hl"}
    k_sp = {h: nc.dram_tensor(f"k_sp{h}", [D, S], BF16, **kw).ap() for h in "hl"}
    v_sp = {h: nc.dram_tensor(f"v_sp{h}", [S, D], BF16, **kw).ap() for h in "hl"}
    dbg_crep = nc.dram_tensor("dbg_crep", [128, H], F32, kind="ExternalOutput").ap() if dbg else None

    sched = Sched()
    E = sched.op
    stack = contextlib.ExitStack()
    slots = {}

    def sb(name, shape, dtype):
        t = stack.enter_context(nc.sbuf_tensor(name, shape, dtype))
        slots[name] = Slot()
        return t

    ident32 = sb("ident32", [128, 128], F32)
    identb = sb("identb", [128, 128], BF16)
    onesf = sb("onesf", [1, 128], F32)
    rt_b = sb("rt_b", [128, 128], BF16)
    ones_b = sb("ones_b", [128, 128], BF16)
    blk_b = sb("blk_b", [128, 2], BF16)
    cosr = sb("cosr", [128, S], F32)
    sinr = sb("sinr", [128, S], F32)
    mur = sb("mur", [128, NCH], F32)
    muk = sb("muk", [128, NCH], F32)
    cEps = sb("cEps", [128, 1], F32)
    cHpi = sb("cHpi", [128, 1], F32)
    invf_c = sb("invf_c", [128, 1], F32)
    crep = sb("crep", [128, H], F32)
    bnd = sb("bnd", [128, NCH * NTT], F32)
    rowb = sb("rowb", [128, S], F32)
    rrep = sb("rrep", [128, S], F32)
    mrrep = sb("mrrep", [128, S], F32)
    yt = sb("yt", [128, NCH * S], F32)
    sa = sb("sa", [128, 4104], F32)
    sB = sb("sB", [128, 4096], F32)
    g1 = sb("g1", [128, 4096], BF16)
    g2 = sb("g2", [128, 4096], BF16)
    g3 = sb("g3", [128, 4096], BF16)
    g4 = sb("g4", [128, 4096], BF16)
    kbuf = sb("kbuf", [128, 2 * S], BF16)
    vbuf = sb("vbuf", [128, 2 * S], BF16)
    qbuf = sb("qbuf", [128, 1024], BF16)
    wbuf = sb("wbuf", [128, 4096], BF16)
    slots["wb0"] = Slot()
    slots["wb1"] = Slot()
    iost = sb("iost", [128, 1024], F32)
    rden = sb("rden", [128, 512], F32)
    nn_i = sb("nn_i", [128, 512], I32)


    pbt = []
    for i in range(7):
        t = stack.enter_context(nc.psum_tensor(f"pb{i}", [128, 512], F32))
        slots[f"pb{i}"] = Slot()
        pbt.append(t)
    pbb = stack.enter_context(nc.psum_tensor("pbb", [128, 1024], BF16))
    slots["pbb"] = Slot()

    def pbank(bk):
        return pbt[int(bk[2])]

    SEMNAMES = ["LD", "LW", "ST", "TE", "VE", "SC", "GP"]

    def SW(name):
        s = slots[name]
        return list(s.r) + ([s.w] if s.w else [])

    def SR(name):
        s = slots[name]
        return [s.w] if s.w else []

    def setw(name, tok):
        s = slots[name]
        s.w = tok
        s.r = []

    def setr(name, tok):
        if tok is not None:
            slots[name].r.append(tok)

    CONST_NAMES = ["ident32", "identb", "onesf", "rt_b", "ones_b",
                   "blk_b", "cEps", "cHpi", "invf_c"]

    yts_ = [Slot() for _ in range(NTT)]

    def SRyt(tt):
        return [yts_[tt].w] if yts_[tt].w else []

    def SWyt(tt):
        return list(yts_[tt].r) + SRyt(tt)

    def setw_yt(tt, tok):
        yts_[tt].w = tok
        yts_[tt].r = []

    def setr_yt(tt, tok):
        if tok is not None:
            yts_[tt].r.append(tok)

    def SRyt_all():
        out = []
        for tt in range(NTT):
            out += SRyt(tt)
        return out

    def SWyt_all():
        out = []
        for tt in range(NTT):
            out += SWyt(tt)
        return out

    def emit_consts():
        g = "gpsimd"
        E(g, lambda e: e.memset(cEps[:], EPS))
        E(g, lambda e: e.memset(cHpi[:], HALF_PI))
        E(g, lambda e: e.memset(onesf[:], 1.0))
        E(g, lambda e: e.memset(ones_b[:], 1.0))
        E(g, lambda e: e.memset(blk_b[:], 0.0))
        E(g, lambda e: e.memset(blk_b[0:64, 0:1], 1.0))
        E(g, lambda e: e.memset(blk_b[64:128, 1:2], 1.0))
        for t, _ in ((ident32, F32), (identb, BF16)):
            E(g, lambda e, t=t: e.memset(t[:], 1.0))
            E(g, lambda e, t=t: e.affine_select(out=t[:], in_=t[:],
                                                pattern=[[1, 128]], compare_op=ALU.is_equal,
                                                fill=0.0, base=0, channel_multiplier=-1))
        E(g, lambda e: e.memset(rt_b[:], -1.0))
        E(g, lambda e: e.affine_select(out=rt_b[:], in_=rt_b[:],
                                       pattern=[[1, 128]], compare_op=ALU.is_equal,
                                       fill=0.0, base=32, channel_multiplier=-1))
        E(g, lambda e: e.affine_select(out=rt_b[:], in_=rt_b[:],
                                       pattern=[[0, 2], [-1, 64]], compare_op=ALU.is_ge,
                                       fill=0.0, base=31, channel_multiplier=0))
        E(g, lambda e: e.memset(sa[:, 0:128], 1.0))
        E(g, lambda e: e.affine_select(out=sa[:, 0:128], in_=sa[:, 0:128],
                                       pattern=[[1, 128]], compare_op=ALU.is_equal,
                                       fill=0.0, base=-32, channel_multiplier=-1))
        tgp = E(g, lambda e: e.affine_select(out=sa[:, 0:128], in_=sa[:, 0:128],
                                             pattern=[[0, 2], [1, 64]], compare_op=ALU.is_ge,
                                             fill=0.0, base=-32, channel_multiplier=0),
                inc="GP")
        tve = E("vector", lambda e: e.tensor_tensor(out=rt_b[:], in0=rt_b[:],
                                                    in1=sa[:, 0:128], op=ALU.add),
                waits=[tgp], inc="VE")
        setw("sa", tve)

        t1 = E("sync", lambda e: e.dma_start(out=mur[:], in_=mu_r.rearrange("(c p) -> p c", p=128),
                                             allow_slow_non_contiguous=True),
               inc="LD", incn=16)
        t2 = E("sync", lambda e: e.dma_start(out=muk[:], in_=mu_k.rearrange("(c p) -> p c", p=128),
                                             allow_slow_non_contiguous=True),
               inc="LD", incn=16)
        setw("mur", t1)
        setw("muk", t2)

        tio = E(g, lambda e: e.iota(out=sinr[:, :], pattern=[[1, S]], base=0,
                                    channel_multiplier=0,
                                    allow_small_or_imprecise_dtypes=True), inc="GP")
        for j in range(128):
            val = float(np.float32(10000.0 ** (-(j % 32) / 32.0)))
            E(g, lambda e, j=j, val=val: e.memset(rowb[0:1, j:j + 1], val))
        tgp2 = E(g, lambda e: e.engine_nop(), inc="GP")
        ttr0 = E("tensor", lambda e: e.transpose(pbt[6][:, 0:1], rowb[0:1, 0:128], onesf[0:1, 0:1]),
                 waits=[tgp2] + SW("pb6"), inc="TE")
        setw("pb6", ttr0)
        tiv = E("vector", lambda e: e.tensor_copy(out=invf_c[:], in_=pbt[6][:, 0:1]),
                waits=SR("pb6"), inc="VE")
        setr("pb6", tiv)
        setw("rowb", tgp2)
        tv1 = E("vector", lambda e: e.tensor_scalar(out=sinr[:], in0=sinr[:],
                                                    scalar1=invf_c[:], scalar2=None,
                                                    op0=ALU.mult),
                waits=[tio, tiv], inc="VE")
        tlast = tv1
        for j in range(S // 512):
            c0 = j * 512
            ta = E("vector", lambda e, c0=c0: e.tensor_scalar(out=nn_i[:], in0=sinr[:, c0:c0 + 512],
                                                              scalar1=INV_2PI, scalar2=None,
                                                              op0=ALU.mult),
                   waits=[tlast], inc="VE")
            tb = E("vector", lambda e, c0=c0: e.tensor_copy(out=sB[:, c0:c0 + 512], in_=nn_i[:]),
                   waits=[ta], inc="VE")
            tlast = tb
        tv4 = E("vector", lambda e: e.scalar_tensor_tensor(out=sinr[:], in0=sB[:, 0:S],
                                                           scalar=-T2PI_HI, in1=sinr[:],
                                                           op0=ALU.mult, op1=ALU.add),
                waits=[tlast], inc="VE")
        tv5 = E("vector", lambda e: e.scalar_tensor_tensor(out=sinr[:], in0=sB[:, 0:S],
                                                           scalar=-T2PI_LO, in1=sinr[:],
                                                           op0=ALU.mult, op1=ALU.add),
                waits=[tv4], inc="VE")
        ts1 = E("scalar", lambda e: e.activation(out=sB[:, 0:S], in_=sinr[:], func=AF.Abs),
                waits=[tv5], inc="SC")
        ts2 = E("scalar", lambda e: e.activation(out=cosr[:], in_=sB[:, 0:S], func=AF.Sin,
                                                 scale=-1.0, bias=cHpi[:]),
                waits=[ts1], inc="SC")
        ts3 = E("scalar", lambda e: e.activation(out=sinr[:], in_=sinr[:], func=AF.Sin),
                waits=[ts2], inc="SC")
        setw("cosr", ts2)
        setw("sinr", ts3)
        setw("sB", ts1)
        tcb = E(g, lambda e: e.engine_nop(), inc="GP")
        for nm in CONST_NAMES:
            slots[nm].w = tcb

    def emit_wsplit():
        srcs = [(wq_t, "wq", 3 * D), (wo_t, "wo", D), (wr_t, "wr", D),
                (wk_t, "wk", D), (wv_t, "wv", D)]
        for src, nm, dout in srcs:
            for kc in range(NCH):
                tld = E("sync", lambda e, src=src, kc=kc, dout=dout:
                        e.dma_start(out=sa[:, 0:dout], in_=src[kc * 128:(kc + 1) * 128, :]),
                        waits=SW("sa"), inc="LD", incn=16)
                setw("sa", tld)
                th = E("vector", lambda e, dout=dout: e.tensor_copy(out=g1[:, 0:dout], in_=sa[:, 0:dout]),
                       waits=SR("sa") + SW("g1"), inc="VE")
                setw("g1", th)
                tl = E("vector", lambda e, dout=dout: e.tensor_tensor(out=g2[:, 0:dout], in0=sa[:, 0:dout],
                                                                      in1=g1[:, 0:dout], op=ALU.subtract),
                       waits=SW("g2"), inc="VE")
                setw("g2", tl)
                setr("sa", tl)
                noc = dout // 128
                hf = noc // 2
                for g_t, gn, suf in ((g1, "g1", "h"), (g2, "g2", "l")):
                    for lo, hi in ((0, hf), (hf, noc)):
                        st_ = E("gpsimd", lambda e, nm=nm, kc=kc, suf=suf, lo=lo, hi=hi, g_t=g_t:
                                e.dma_start(
                                    out=wsp[nm + suf][lo:hi, :, kc * 128:(kc + 1) * 128]
                                    .rearrange("oc p d -> p oc d"),
                                    in_=g_t[:, lo * 128:hi * 128]
                                    .rearrange("p (oc d) -> p oc d", d=128)),
                                waits=SR(gn), inc="ST", incn=16)
                        setr(gn, st_)

    def emit_load_x(s):
        ytw = SWyt_all()
        for t2 in range(NT2):
            tld = E("sync", lambda e, s=s, t2=t2:
                    e.dma_start(out=iost[:], in_=x_in[s * S + t2 * 128: s * S + (t2 + 1) * 128, :]),
                    waits=SW("iost"), inc="LD", incn=16)
            setw("iost", tld)
            for c in range(NCH):
                bk = f"pb{5 + (c % 2)}"
                tmm = E("tensor", lambda e, c=c, bk=bk:
                        e.transpose(pbank(bk)[:, 0:128], iost[:, c * 128:(c + 1) * 128], ident32[:]),
                        waits=SR("iost") + SW(bk) + SR("ident32"), inc="TE")
                setw(bk, tmm)
                setr("iost", tmm)
                tev = E("scalar", lambda e, c=c, t2=t2, bk=bk:
                        e.copy(out=yt[:, c * S + t2 * 128: c * S + (t2 + 1) * 128], in_=pbank(bk)[:, 0:128]),
                        waits=SR(bk) + (ytw if (t2 == 0 and c == 0) else []), inc="SC")
                setr(bk, tev)
        for tt in range(NTT):
            setw_yt(tt, tev)

    def emit_store(s):
        laststore = None
        for t2 in range(NT2):
            evs = []
            for c in range(NCH):
                bk = f"pb{5 + (c % 2)}"
                tmm = E("tensor", lambda e, c=c, t2=t2, bk=bk:
                        e.transpose(pbank(bk)[:, 0:128],
                                    yt[:, c * S + t2 * 128: c * S + (t2 + 1) * 128], ident32[:]),
                        waits=SRyt(t2 // 4) + SW(bk), inc="TE")
                setw(bk, tmm)
                setr_yt(t2 // 4, tmm)
                tev = E("scalar", lambda e, c=c, bk=bk:
                        e.copy(out=iost[:, c * 128:(c + 1) * 128], in_=pbank(bk)[:, 0:128]),
                        waits=SR(bk) + (SW("iost") if c == 0 else []), inc="SC")
                setr(bk, tev)
                evs.append(tev)
            slots["iost"].w = evs[-1]
            slots["iost"].r = []
            laststore = E("gpsimd", lambda e, s=s, t2=t2:
                          e.dma_start(out=out[s * S + t2 * 128: s * S + (t2 + 1) * 128, :], in_=iost[:]),
                          waits=evs, inc="ST", incn=16)
            setr("iost", laststore)
        return laststore

    def emit_ln(s):
        for tt in range(NTT):
            ty = tq2 = None
            for c in range(NCH):
                w0 = c * S + tt * 512
                th = E("vector", lambda e, w0=w0: e.tensor_copy(out=g1[:, 0:512], in_=yt[:, w0:w0 + 512]),
                       waits=SRyt(tt) + SW("g1"), inc="VE")
                setw("g1", th)
                tl = E("vector", lambda e, w0=w0: e.tensor_tensor(out=g2[:, 0:512], in0=yt[:, w0:w0 + 512],
                                                                  in1=g1[:, 0:512], op=ALU.subtract),
                       waits=SW("g2"), inc="VE")
                setw("g2", tl)
                tqs = E("scalar", lambda e, w0=w0: e.activation(out=sB[:, 0:512], in_=yt[:, w0:w0 + 512],
                                                                func=AF.Square),
                        waits=SRyt(tt) + SW("sB"), inc="SC")
                setw("sB", tqs)
                setr_yt(tt, th)
                setr_yt(tt, tqs)
                th2 = E("vector", lambda e: e.tensor_copy(out=g3[:, 0:512], in_=sB[:, 0:512]),
                        waits=SR("sB") + SW("g3"), inc="VE")
                setw("g3", th2)
                tl2 = E("vector", lambda e: e.tensor_tensor(out=g4[:, 0:512], in0=sB[:, 0:512],
                                                            in1=g3[:, 0:512], op=ALU.subtract),
                        waits=SW("g4"), inc="VE")
                setw("g4", tl2)
                setr("sB", tl2)
                first = (c == 0)
                stop = (c == NCH - 1)
                E("tensor", lambda e, first=first: e.matmul(pbt[6][0:1, :], ones_b[:, 0:1], g1[:, 0:512],
                                                            start=first, stop=False),
                  waits=SR("g1") + SR("g2") + SR("ones_b") + (SW("pb6") if first else []))
                ty = E("tensor", lambda e, stop=stop: e.matmul(pbt[6][0:1, :], ones_b[:, 0:1], g2[:, 0:512],
                                                               start=False, stop=stop), inc="TE")
                setr("g1", ty)
                setr("g2", ty)
                E("tensor", lambda e, first=first: e.matmul(pbt[5][0:1, :], ones_b[:, 0:1], g3[:, 0:512],
                                                            start=first, stop=False),
                  waits=SR("g3") + SR("g4") + (SW("pb5") if first else []))
                tq2 = E("tensor", lambda e, stop=stop: e.matmul(pbt[5][0:1, :], ones_b[:, 0:1], g4[:, 0:512],
                                                                start=False, stop=stop), inc="TE")
                setr("g3", tq2)
                setr("g4", tq2)
            setw("pb6", ty)
            setw("pb5", tq2)
            e1 = E("vector", lambda e, tt=tt: e.tensor_copy(out=rowb[0:1, tt * 512:(tt + 1) * 512],
                                                            in_=pbt[6][0:1, :]),
                   waits=SR("pb6") + (SW("rowb") if tt == 0 else []), inc="VE")
            setr("pb6", e1)
            e2 = E("vector", lambda e, tt=tt: e.tensor_copy(out=sa[0:1, tt * 512:(tt + 1) * 512],
                                                            in_=pbt[5][0:1, :]),
                   waits=SR("pb5") + (SW("sa") if tt == 0 else []), inc="VE")
            setr("pb5", e2)
            if tt == 0:
                slots["rowb"].w = e2
                slots["rowb"].r = []
                slots["sa"].w = e2
                slots["sa"].r = []
        slots["rowb"].w = e2
        slots["sa"].w = e2
        tv = E("vector", lambda e: e.tensor_scalar(out=sa[0:1, 2048:2048 + S], in0=rowb[0:1, 0:S],
                                                   scalar1=1.0 / D, scalar2=None, op0=ALU.mult),
               inc="VE")
        tv2 = E("vector", lambda e: e.tensor_tensor(out=rowb[0:1, 0:S], in0=sa[0:1, 2048:2048 + S],
                                                    in1=sa[0:1, 2048:2048 + S], op=ALU.mult),
                waits=[tv], inc="VE")
        tv3 = E("vector", lambda e: e.tensor_scalar(out=sa[0:1, 0:S], in0=sa[0:1, 0:S],
                                                    scalar1=1.0 / D, scalar2=None, op0=ALU.mult),
                waits=[tv2], inc="VE")
        tv4 = E("vector", lambda e: e.tensor_tensor(out=sa[0:1, 0:S], in0=sa[0:1, 0:S],
                                                    in1=rowb[0:1, 0:S], op=ALU.subtract),
                waits=[tv3], inc="VE")
        ts = E("scalar", lambda e: e.activation(out=sa[0:1, 0:S], in_=sa[0:1, 0:S],
                                                func=AF.Sqrt, bias=cEps[0:1, :]),
               waits=[tv4], inc="SC")
        tv5 = E("vector", lambda e: e.reciprocal(out=rowb[0:1, 0:S], in_=sa[0:1, 0:S]),
                waits=[ts], inc="VE")
        tv6 = E("vector", lambda e: e.tensor_tensor(out=sa[0:1, 2048:2048 + S],
                                                    in0=sa[0:1, 2048:2048 + S],
                                                    in1=rowb[0:1, 0:S], op=ALU.mult),
                waits=[tv5], inc="VE")
        slots["rowb"].w = tv6
        slots["sa"].w = tv6
        te2 = None
        for tt in range(NTT):
            bk = f"pb{5 + (tt % 2)}"
            tm = E("tensor", lambda e, tt=tt, bk=bk:
                   e.matmul(pbank(bk)[:, 0:512], onesf[0:1, :], rowb[0:1, tt * 512:(tt + 1) * 512],
                            start=True, stop=True),
                   waits=SR("rowb") + SW(bk) + SR("onesf"), inc="TE")
            setw(bk, tm)
            setr("rowb", tm)
            te1 = E("scalar", lambda e, tt=tt, bk=bk: e.copy(out=rrep[:, tt * 512:(tt + 1) * 512],
                                                             in_=pbank(bk)[:, 0:512]),
                    waits=SR(bk) + (SW("rrep") if tt == 0 else []), inc="SC")
            setr(bk, te1)
            if tt == 0:
                slots["rrep"].w = te1
                slots["rrep"].r = []
            tm2 = E("tensor", lambda e, tt=tt, bk=bk:
                    e.matmul(pbank(bk)[:, 0:512], onesf[0:1, :],
                             sa[0:1, 2048 + tt * 512:2048 + (tt + 1) * 512], start=True, stop=True),
                    waits=SR("sa") + [te1] + SR("onesf"), inc="TE")
            setw(bk, tm2)
            setr("sa", tm2)
            te2 = E("scalar", lambda e, tt=tt, bk=bk: e.copy(out=mrrep[:, tt * 512:(tt + 1) * 512],
                                                             in_=pbank(bk)[:, 0:512]),
                    waits=SR(bk) + (SW("mrrep") if tt == 0 else []), inc="SC")
            setr(bk, te2)
            if tt == 0:
                slots["mrrep"].w = te2
                slots["mrrep"].r = []
        slots["rrep"].w = te2
        slots["mrrep"].w = te2

    def mm3(bk, wsl_h, wsl_l, xh_t, xl_t, xsl, extra_waits):
        tok = None
        for kc in range(NCH):
            wh_ = wsl_h(kc)
            wl_ = wsl_l(kc)
            xh_ = xsl(xh_t, kc)
            xl_ = xsl(xl_t, kc)
            w = (extra_waits + SW(bk)) if kc == 0 else []
            E("tensor", lambda e, wh_=wh_, xh_=xh_, kc=kc, bk=bk:
              e.matmul(pbank(bk)[:], wh_, xh_, start=(kc == 0), stop=False), waits=w)
            E("tensor", lambda e, wl_=wl_, xh_=xh_, bk=bk:
              e.matmul(pbank(bk)[:], wl_, xh_, start=False, stop=False))
            tok = E("tensor", lambda e, wh_=wh_, xl_=xl_, kc=kc, bk=bk:
                    e.matmul(pbank(bk)[:], wh_, xl_, start=False, stop=(kc == NCH - 1)),
                    inc=("TE" if kc == NCH - 1 else None))
        setw(bk, tok)
        return tok

    def wload(src_h, src_l, oc, slot):
        bname = f"wb{slot}"
        b0 = slot * 2048
        t1 = E("sync", lambda e, oc=oc, src_h=src_h, b0=b0: e.dma_start(
            out=wbuf[:, b0:b0 + 1024], in_=src_h[oc]),
            waits=SW(bname), inc="LW", incn=16)
        t2 = E("sync", lambda e, oc=oc, src_l=src_l, b0=b0: e.dma_start(
            out=wbuf[:, b0 + 1024:b0 + 2048], in_=src_l[oc]),
            inc="LW", incn=16)
        setw(bname, t2)
        return [t1, t2], b0

    def wchunks(b0):
        return (lambda kc: wbuf[:, b0 + kc * 128: b0 + (kc + 1) * 128],
                lambda kc: wbuf[:, b0 + 1024 + kc * 128: b0 + 1024 + (kc + 1) * 128])

    def emit_ffn(s):
        t = None
        for c in range(NCH):
            for j in range(1, NTT):
                col = c * S + j * 512 - 1
                t = E("vector", lambda e, c=c, j=j, col=col:
                      e.tensor_copy(out=bnd[:, (c * NTT + j):(c * NTT + j) + 1], in_=yt[:, col:col + 1]),
                      waits=SRyt(j - 1) + (SW("bnd") if (c == 0 and j == 1) else []), inc="VE")
                setr_yt(j - 1, t)
        setw("bnd", t)
        for tt in range(NTT):
            r0 = tt * 512
            bq = None
            for c in range(NCH):
                base = c * 513
                if tt == 0:
                    bq = E("vector", lambda e, base=base: e.memset(sa[:, base:base + 1], 0.0),
                           waits=(SW("sa") if c == 0 else []), inc="VE")
                else:
                    bq = E("vector", lambda e, base=base, c=c, tt=tt, r0=r0:
                           e.tensor_tensor(out=sa[:, base:base + 1],
                                           in0=bnd[:, c * NTT + tt:c * NTT + tt + 1],
                                           in1=rrep[:, r0 - 1:r0], op=ALU.mult),
                           waits=SR("bnd") + SR("rrep") + (SW("sa") if c == 0 else []), inc="VE")
            if tt != 0:
                bq2 = None
                for c in range(NCH):
                    base = c * 513
                    bq2 = E("vector", lambda e, base=base, r0=r0:
                            e.tensor_tensor(out=sa[:, base:base + 1], in0=sa[:, base:base + 1],
                                            in1=mrrep[:, r0 - 1:r0], op=ALU.subtract),
                            waits=SR("mrrep") + ([bq] if c == 0 else []), inc="VE")
                bq = bq2
            mix_last = None
            for c in range(NCH):
                w0 = c * S + tt * 512
                base = c * 513
                t1 = E("vector", lambda e, base=base, w0=w0, r0=r0:
                       e.tensor_tensor(out=sa[:, base + 1: base + 513], in0=yt[:, w0:w0 + 512],
                                       in1=rrep[:, r0:r0 + 512], op=ALU.mult),
                       waits=SRyt(tt) + SR("rrep"), inc="VE")
                setr_yt(tt, t1)
                E("vector", lambda e, base=base, r0=r0:
                  e.tensor_tensor(out=sa[:, base + 1: base + 513], in0=sa[:, base + 1: base + 513],
                                  in1=mrrep[:, r0:r0 + 512], op=ALU.subtract),
                  waits=SR("mrrep"), inc="VE")
                E("vector", lambda e, base=base, c=c:
                  e.tensor_tensor(out=sB[:, c * 512:(c + 1) * 512], in0=sa[:, base + 1: base + 513],
                                  in1=sa[:, base: base + 512], op=ALU.subtract),
                  waits=([bq] if c == 0 else []) + (SW("sB") if c == 0 else []), inc="VE")
                E("vector", lambda e, c=c:
                  e.scalar_tensor_tensor(out=g3[:, c * 512:(c + 1) * 512],
                                         in0=sB[:, c * 512:(c + 1) * 512], scalar=muk[:, c:c + 1],
                                         in1=sa[:, c * 513: c * 513 + 512],
                                         op0=ALU.mult, op1=ALU.add),
                  waits=SR("muk") + (SW("g3") if c == 0 else []), inc="VE")
                E("vector", lambda e, c=c:
                  e.scalar_tensor_tensor(out=iost[:, 0:512],
                                         in0=sB[:, c * 512:(c + 1) * 512], scalar=muk[:, c:c + 1],
                                         in1=sa[:, c * 513: c * 513 + 512],
                                         op0=ALU.mult, op1=ALU.add),
                  waits=SW("iost"), inc="VE")
                t7 = E("vector", lambda e, c=c:
                       e.tensor_tensor(out=g4[:, c * 512:(c + 1) * 512], in0=iost[:, 0:512],
                                       in1=g3[:, c * 512:(c + 1) * 512], op=ALU.subtract),
                       waits=(SW("g4") if c == 0 else []), inc="VE")
                slots["iost"].w = t7
                slots["iost"].r = []
                E("vector", lambda e, c=c:
                  e.scalar_tensor_tensor(out=g1[:, c * 512:(c + 1) * 512],
                                         in0=sB[:, c * 512:(c + 1) * 512], scalar=mur[:, c:c + 1],
                                         in1=sa[:, c * 513: c * 513 + 512],
                                         op0=ALU.mult, op1=ALU.add),
                  waits=SR("mur") + (SW("g1") if c == 0 else []), inc="VE")
                E("vector", lambda e, c=c:
                  e.scalar_tensor_tensor(out=iost[:, 512:1024],
                                         in0=sB[:, c * 512:(c + 1) * 512], scalar=mur[:, c:c + 1],
                                         in1=sa[:, c * 513: c * 513 + 512],
                                         op0=ALU.mult, op1=ALU.add),
                  inc="VE")
                t9 = E("vector", lambda e, c=c:
                       e.tensor_tensor(out=g2[:, c * 512:(c + 1) * 512], in0=iost[:, 512:1024],
                                       in1=g1[:, c * 512:(c + 1) * 512], op=ALU.subtract),
                       waits=(SW("g2") if c == 0 else []), inc="VE")
                slots["iost"].w = t9
                mix_last = t9
            for nm in ("g1", "g2", "g3", "g4", "sa", "sB"):
                slots[nm].w = mix_last
                slots[nm].r = []
            ter = None
            for oc in range(NCH):
                wts, b0 = wload(wsp["wrh"], wsp["wrl"], oc, oc % 2)
                wh_f, wl_f = wchunks(b0)
                bk = f"pb{0 + (oc % 2)}"
                tok = mm3(bk, wh_f, wl_f,
                          g1, g2, lambda t_, kc: t_[:, kc * 512:(kc + 1) * 512],
                          SR("g1") + SR("g2") + wts)
                setr("g1", tok)
                setr("g2", tok)
                setr(f"wb{oc % 2}", tok)
                ter = E("scalar", lambda e, oc=oc, bk=bk:
                        e.activation(out=sa[:, oc * 512:(oc + 1) * 512], in_=pbank(bk)[:],
                                     func=AF.Sigmoid),
                        waits=SR(bk) + (SW("sa") if oc == 0 else []), inc="SC")
                setr(bk, ter)
                if oc == 0:
                    slots["sa"].w = ter
                    slots["sa"].r = []
            slots["sa"].w = ter
            tk4 = None
            for oc in range(NCH):
                wts, b0 = wload(wsp["wkh"], wsp["wkl"], oc, oc % 2)
                wh_f, wl_f = wchunks(b0)
                bk = f"pb{2 + (oc % 2)}"
                tok = mm3(bk, wh_f, wl_f,
                          g3, g4, lambda t_, kc: t_[:, kc * 512:(kc + 1) * 512],
                          SR("g3") + SR("g4") + wts)
                setr("g3", tok)
                setr("g4", tok)
                setr(f"wb{oc % 2}", tok)
                tk1 = E("scalar", lambda e, oc=oc, bk=bk:
                        e.activation(out=sB[:, oc * 512:(oc + 1) * 512], in_=pbank(bk)[:],
                                     func=AF.Relu),
                        waits=SR(bk) + (SW("sB") if oc == 0 else []), inc="SC")
                setr(bk, tk1)
                if oc == 0:
                    slots["sB"].w = tk1
                    slots["sB"].r = []
                tk2 = E("scalar", lambda e, oc=oc:
                        e.activation(out=sB[:, oc * 512:(oc + 1) * 512],
                                     in_=sB[:, oc * 512:(oc + 1) * 512], func=AF.Square),
                        inc="SC")
                tk3 = E("vector", lambda e, oc=oc: e.tensor_copy(out=g1[:, oc * 512:(oc + 1) * 512],
                                                                 in_=sB[:, oc * 512:(oc + 1) * 512]),
                        waits=[tk2] + (SW("g1") if oc == 0 else []), inc="VE")
                if oc == 0:
                    slots["g1"].w = tk3
                    slots["g1"].r = []
                tk4 = E("vector", lambda e, oc=oc: e.tensor_tensor(out=g2[:, oc * 512:(oc + 1) * 512],
                                                                   in0=sB[:, oc * 512:(oc + 1) * 512],
                                                                   in1=g1[:, oc * 512:(oc + 1) * 512],
                                                                   op=ALU.subtract),
                        waits=(SW("g2") if oc == 0 else []), inc="VE")
                if oc == 0:
                    slots["g2"].w = tk4
                    slots["g2"].r = []
            slots["g1"].w = tk4
            slots["g2"].w = tk4
            for oc in range(NCH):
                wts, b0 = wload(wsp["wvh"], wsp["wvl"], oc, oc % 2)
                wh_f, wl_f = wchunks(b0)
                bk = f"pb{4 + (oc % 2)}"
                tok = mm3(bk, wh_f, wl_f,
                          g1, g2, lambda t_, kc: t_[:, kc * 512:(kc + 1) * 512],
                          SR("g1") + SR("g2") + wts)
                setr("g1", tok)
                setr("g2", tok)
                setr(f"wb{oc % 2}", tok)
                w0 = oc * S + tt * 512
                tv1 = E("vector", lambda e, oc=oc, bk=bk:
                        e.tensor_tensor(out=iost[:, 0:512], in0=pbank(bk)[:],
                                        in1=sa[:, oc * 512:(oc + 1) * 512], op=ALU.mult),
                        waits=SR(bk) + SR("sa") + SW("iost"), inc="VE")
                setr(bk, tv1)
                setr("sa", tv1)
                slots["iost"].w = tv1
                slots["iost"].r = []
                tv2 = E("vector", lambda e, w0=w0:
                        e.scalar_tensor_tensor(out=yt[:, w0:w0 + 512], in0=iost[:, 0:512],
                                               scalar=0.5, in1=yt[:, w0:w0 + 512],
                                               op0=ALU.mult, op1=ALU.add),
                        waits=SWyt(tt), inc="VE")
                slots["iost"].r.append(tv2)
                setw_yt(tt, tv2)

    pf_slots = {off: Slot() for off in (0, 512, 1024, 1536)}
    pp_slots = {off: Slot() for off in (0, 512, 1024, 1536)}

    def emit_attn(s):
        qn_toks = []
        for tt in range(NTT):
            r0 = tt * 512
            for c in range(NCH):
                w0 = c * S + tt * 512
                t1 = E("vector", lambda e, c=c, w0=w0, r0=r0:
                       e.tensor_tensor(out=sa[:, c * 512:(c + 1) * 512], in0=yt[:, w0:w0 + 512],
                                       in1=rrep[:, r0:r0 + 512], op=ALU.mult),
                       waits=SRyt(tt) + SR("rrep") + (SW("sa") if c == 0 else []), inc="VE")
                setr_yt(tt, t1)
                if c == 0:
                    slots["sa"].w = t1
                    slots["sa"].r = []
                E("vector", lambda e, c=c, r0=r0:
                  e.tensor_tensor(out=sa[:, c * 512:(c + 1) * 512], in0=sa[:, c * 512:(c + 1) * 512],
                                  in1=mrrep[:, r0:r0 + 512], op=ALU.subtract),
                  waits=SR("mrrep"), inc="VE")
                t3 = E("vector", lambda e, c=c: e.tensor_copy(out=g1[:, c * 512:(c + 1) * 512],
                                                              in_=sa[:, c * 512:(c + 1) * 512]),
                       waits=(SW("g1") if c == 0 else []), inc="VE")
                if c == 0:
                    slots["g1"].w = t3
                    slots["g1"].r = []
                t4 = E("vector", lambda e, c=c: e.tensor_tensor(out=g2[:, c * 512:(c + 1) * 512],
                                                                in0=sa[:, c * 512:(c + 1) * 512],
                                                                in1=g1[:, c * 512:(c + 1) * 512],
                                                                op=ALU.subtract),
                       waits=(SW("g2") if c == 0 else []), inc="VE")
                if c == 0:
                    slots["g2"].w = t4
                    slots["g2"].r = []
            slots["g1"].w = t4
            slots["g2"].w = t4
            slots["sa"].w = t4
            for oc in range(3 * NCH):
                wts, b0 = wload(wsp["wqh"], wsp["wql"], oc, oc % 2)
                wh_f, wl_f = wchunks(b0)
                bk = f"pb{0 + (oc % 2)}"
                tok = mm3(bk, wh_f, wl_f,
                          g1, g2, lambda t_, kc: t_[:, kc * 512:(kc + 1) * 512],
                          SR("g1") + SR("g2") + wts)
                setr("g1", tok)
                setr("g2", tok)
                setr(f"wb{oc % 2}", tok)
                hp = oc % NCH
                if oc < 2 * NCH:
                    is_q = oc < NCH
                    col = hp * 512
                    e1 = E("scalar", lambda e, col=col, bk=bk:
                           e.copy(out=g3[:, col:col + 512], in_=pbank(bk)[:]),
                           waits=SR(bk) + list(slots["g3"].r) + ([slots["g3"].w] if slots["g3"].w else []),
                           inc="SC")
                    slots["g3"].w = e1
                    slots["g3"].r = []
                    e2 = E("vector", lambda e, col=col, bk=bk:
                           e.tensor_tensor(out=g4[:, col:col + 512], in0=pbank(bk)[:],
                                           in1=g3[:, col:col + 512], op=ALU.subtract),
                           waits=SR(bk) + [e1] + list(slots["g4"].r) + ([slots["g4"].w] if slots["g4"].w else []),
                           inc="VE")
                    slots["g4"].w = e2
                    slots["g4"].r = []
                    E("tensor", lambda e, col=col: e.matmul(pbt[6][:], rt_b[:], g3[:, col:col + 512],
                                                            start=True, stop=False),
                      waits=[e1] + SW("pb6") + SR("rt_b"))
                    tr = E("tensor", lambda e, col=col: e.matmul(pbt[6][:], rt_b[:], g4[:, col:col + 512],
                                                                 start=False, stop=True),
                           waits=[e2], inc="TE")
                    setw("pb6", tr)
                    tv1 = E("vector", lambda e, bk=bk, r0=r0:
                            e.tensor_tensor(out=sB[:, 0:512], in0=pbank(bk)[:],
                                            in1=cosr[:, r0:r0 + 512], op=ALU.mult),
                            waits=SR(bk) + SR("cosr") + SW("sB"), inc="VE")
                    slots["sB"].w = tv1
                    slots["sB"].r = []
                    setr(bk, tv1)
                    tv2 = E("vector", lambda e, r0=r0:
                            e.tensor_tensor(out=sB[:, 512:1024], in0=pbt[6][:],
                                            in1=sinr[:, r0:r0 + 512], op=ALU.mult),
                            waits=SR("pb6") + SR("sinr"), inc="VE")
                    setr("pb6", tv2)
                    E("vector", lambda e: e.tensor_tensor(out=sB[:, 0:512], in0=sB[:, 0:512],
                                                          in1=sB[:, 512:1024], op=ALU.add),
                      inc="VE")
                    tv4 = E("vector", lambda e, col=col: e.tensor_copy(out=g3[:, col:col + 512],
                                                                       in_=sB[:, 0:512]),
                            waits=[tr], inc="VE")
                    slots["g3"].w = tv4
                    tv5 = E("vector", lambda e, col=col: e.tensor_tensor(out=g4[:, col:col + 512],
                                                                         in0=sB[:, 0:512],
                                                                         in1=g3[:, col:col + 512],
                                                                         op=ALU.subtract),
                            inc="VE")
                    slots["g4"].w = tv5
                    slots["sB"].w = tv5
                    dst = q_sp if is_q else k_sp
                    s1 = E("gpsimd", lambda e, dst=dst, hp=hp, tt=tt, col=col:
                           e.dma_start(out=dst["h"][hp * 128:(hp + 1) * 128, tt * 512:(tt + 1) * 512],
                                       in_=g3[:, col:col + 512]),
                           waits=[tv4], inc="ST", incn=16)
                    s2 = E("gpsimd", lambda e, dst=dst, hp=hp, tt=tt, col=col:
                           e.dma_start(out=dst["l"][hp * 128:(hp + 1) * 128, tt * 512:(tt + 1) * 512],
                                       in_=g4[:, col:col + 512]),
                           waits=[tv5], inc="ST", incn=16)
                    setr("g3", s1)
                    setr("g4", s2)
                    tq1 = E("vector", lambda e, col=col:
                            e.tensor_tensor(out=vbuf[:, 0:512], in0=g3[:, col:col + 512],
                                            in1=g3[:, col:col + 512], op=ALU.mult),
                            waits=[tv4] + SW("vbuf"), inc="VE")
                    setw("vbuf", tq1)
                    setr("g3", tq1)
                    for j in range(2):
                        head = 2 * hp + j
                        tqm = E("tensor", lambda e, j=j: e.matmul(pbt[5][0:1, :], blk_b[:, j:j + 1],
                                                                  vbuf[:, 0:512],
                                                                  start=True, stop=True),
                                waits=[tq1] + SW("pb5") + SR("blk_b"), inc="TE")
                        setw("pb5", tqm)
                        setr("vbuf", tqm)
                        coloff = (0 if is_q else 64) + head * 4 + tt
                        trm = E("vector", lambda e, coloff=coloff:
                                e.tensor_reduce(out=rowb[0:1, coloff:coloff + 1],
                                                in_=pbt[5][0:1, :], axis=AX.X, op=ALU.max),
                                waits=SR("pb5") + (qn_toks[-1:] if qn_toks else SW("rowb")),
                                inc="VE")
                        setr("pb5", trm)
                        qn_toks.append(trm)
                else:
                    e1 = E("scalar", lambda e, bk=bk: e.copy(out=g3[:, 0:512], in_=pbank(bk)[:]),
                           waits=SR(bk) + list(slots["g3"].r) + ([slots["g3"].w] if slots["g3"].w else []),
                           inc="SC")
                    slots["g3"].w = e1
                    slots["g3"].r = []
                    e2 = E("vector", lambda e, bk=bk: e.tensor_tensor(out=g4[:, 0:512],
                                                                      in0=pbank(bk)[:],
                                                                      in1=g3[:, 0:512], op=ALU.subtract),
                           waits=SR(bk) + [e1] + list(slots["g4"].r) + ([slots["g4"].w] if slots["g4"].w else []),
                           inc="VE")
                    slots["g4"].w = e2
                    slots["g4"].r = []
                    setr(bk, e2)
                    for half, gt, gn, etok in (("h", g3, "g3", e1), ("l", g4, "g4", e2)):
                        for sub in range(4):
                            tmm = E("tensor", lambda e, gt=gt, sub=sub:
                                    e.transpose(pbb[:, 0:128], gt[:, sub * 128:(sub + 1) * 128], identb[:]),
                                    waits=[etok] + SW("pbb") + SR("identb"), inc="TE")
                            setw("pbb", tmm)
                            setr(gn, tmm)
                            tev = E("vector", lambda e: e.tensor_copy(out=vbuf[:, 512:640], in_=pbb[:, 0:128]),
                                    waits=SR("pbb") + SW("vbuf"), inc="VE")
                            setw("vbuf", tev)
                            setr("pbb", tev)
                            tst = E("gpsimd", lambda e, half=half, hp=hp, tt=tt, sub=sub:
                                    e.dma_start(out=v_sp[half][tt * 512 + sub * 128: tt * 512 + (sub + 1) * 128,
                                                               hp * 128:(hp + 1) * 128],
                                                in_=vbuf[:, 512:640]),
                                    waits=[tev], inc="ST", incn=16)
                            setr("vbuf", tst)
        tprev = qn_toks[-1]
        for h in range(H):
            tprev = E("vector", lambda e, h=h: e.tensor_reduce(out=rowb[0:1, 128 + h:129 + h],
                                                               in_=rowb[0:1, h * 4:h * 4 + 4],
                                                               axis=AX.X, op=ALU.max),
                      waits=[tprev], inc="VE")
            tprev = E("vector", lambda e, h=h: e.tensor_reduce(out=rowb[0:1, 144 + h:145 + h],
                                                               in_=rowb[0:1, 64 + h * 4:64 + h * 4 + 4],
                                                               axis=AX.X, op=ALU.max),
                      waits=[tprev], inc="VE")
        tsq = E("scalar", lambda e: e.activation(out=rowb[0:1, 128:160], in_=rowb[0:1, 128:160],
                                                 func=AF.Sqrt),
                waits=[tprev], inc="SC")
        tvc = E("vector", lambda e: e.tensor_tensor(out=rowb[0:1, 160:176], in0=rowb[0:1, 128:144],
                                                    in1=rowb[0:1, 144:160], op=ALU.mult),
                waits=[tsq], inc="VE")
        tvc2 = E("vector", lambda e: e.tensor_scalar(out=rowb[0:1, 160:176], in0=rowb[0:1, 160:176],
                                                     scalar1=-1.01 * 0.125, scalar2=-0.01,
                                                     op0=ALU.mult, op1=ALU.add),
                 waits=[tvc], inc="VE")
        trep = E("tensor", lambda e: e.matmul(pbt[5][:, 0:16], onesf[0:1, :], rowb[0:1, 160:176],
                                              start=True, stop=True),
                 waits=[tvc2] + SW("pb5") + SR("onesf"), inc="TE")
        setw("pb5", trep)
        setr("rowb", trep)
        slots["rowb"].w = tvc2
        tcrep = E("scalar", lambda e: e.copy(out=crep[:], in_=pbt[5][:, 0:16]),
                  waits=SR("pb5"), inc="SC")
        setr("pb5", tcrep)
        setw("crep", tcrep)
        if dbg:
            E("gpsimd", lambda e: e.dma_start(out=dbg_crep[:, :], in_=crep[:]),
              waits=SR("crep"), inc="ST", incn=16)

        trans = SW("sB") + SW("g3") + SW("g4")
        for off in (0, 512, 1024, 1536):
            pf_slots[off].r = list(trans)
            pf_slots[off].w = None
            pp_slots[off].r = list(trans)
            pp_slots[off].w = None
        pf_ap = sB
        pph_ap = g3

        ta3 = None
        for tt in range(NTT):
            nkt = (tt + 1) * 4
            for hp in range(NCH):
                headA, headB = 2 * hp, 2 * hp + 1
                lk1 = E("sync", lambda e, hp=hp, nkt=nkt:
                        e.dma_start(out=kbuf[:, 0:nkt * 128],
                                    in_=k_sp["h"][hp * 128:(hp + 1) * 128, 0:nkt * 128]),
                        waits=SW("kbuf"), inc="LD", incn=16)
                lk2 = E("sync", lambda e, hp=hp, nkt=nkt:
                        e.dma_start(out=kbuf[:, S:S + nkt * 128],
                                    in_=k_sp["l"][hp * 128:(hp + 1) * 128, 0:nkt * 128]),
                        inc="LD", incn=16)
                setw("kbuf", lk2)
                lv1 = E("sync", lambda e, hp=hp, nkt=nkt:
                        e.dma_start(out=vbuf[:, 0:nkt * 128].rearrange("p (n c) -> p n c", c=128),
                                    in_=v_sp["h"][0:nkt * 128, hp * 128:(hp + 1) * 128]
                                    .rearrange("(n p) c -> p n c", p=128)),
                        waits=SW("vbuf"), inc="LD", incn=16)
                lv2 = E("sync", lambda e, hp=hp, nkt=nkt:
                        e.dma_start(out=vbuf[:, S:S + nkt * 128].rearrange("p (n c) -> p n c", c=128),
                                    in_=v_sp["l"][0:nkt * 128, hp * 128:(hp + 1) * 128]
                                    .rearrange("(n p) c -> p n c", p=128)),
                        inc="LD", incn=16)
                setw("vbuf", lv2)
                lq1 = E("sync", lambda e, hp=hp, tt=tt:
                        e.dma_start(out=qbuf[:, 0:512],
                                    in_=q_sp["h"][hp * 128:(hp + 1) * 128, tt * 512:(tt + 1) * 512]),
                        waits=SW("qbuf"), inc="LD", incn=16)
                lq2 = E("sync", lambda e, hp=hp, tt=tt:
                        e.dma_start(out=qbuf[:, 512:1024],
                                    in_=q_sp["l"][hp * 128:(hp + 1) * 128, tt * 512:(tt + 1) * 512]),
                        inc="LD", incn=16)
                setw("qbuf", lq2)
                last_sc = {}
                pv_tok = den_tok = None
                for c in range(nkt):
                    ib = c % 2
                    bkA = f"pb{0 + ib * 2}"
                    bkB = f"pb{1 + ib * 2}"
                    kc0 = c * 128
                    for (bkX, p0) in ((bkA, 0), (bkB, 64)):
                        w0 = SW(bkX) + [lk1, lk2, lq1, lq2]
                        E("tensor", lambda e, bkX=bkX, p0=p0, kc0=kc0:
                          e.matmul(pbank(bkX)[:], kbuf[p0:p0 + 64, kc0:kc0 + 128],
                                   qbuf[p0:p0 + 64, 0:512], start=True, stop=False,
                                   tile_position=(p0, 0)),
                          waits=w0)
                        E("tensor", lambda e, bkX=bkX, p0=p0, kc0=kc0:
                          e.matmul(pbank(bkX)[:], kbuf[p0:p0 + 64, kc0:kc0 + 128],
                                   qbuf[p0:p0 + 64, 512:1024], start=False, stop=False,
                                   tile_position=(p0, 0)))
                        tsc = E("tensor", lambda e, bkX=bkX, p0=p0, kc0=kc0:
                                e.matmul(pbank(bkX)[:], kbuf[p0:p0 + 64, S + kc0:S + kc0 + 128],
                                         qbuf[p0:p0 + 64, 0:512], start=False, stop=True,
                                         tile_position=(p0, 0)), inc="TE")
                        setw(bkX, tsc)
                        last_sc[bkX] = tsc
                    partial = (c >= tt * 4)
                    for (bkX, head, off0) in ((bkA, headA, 0), (bkB, headB, 1024)):
                        pcol = off0 + ib * 512
                        ps_ = pf_slots[pcol]
                        pp_ = pp_slots[pcol]
                        te = E("scalar", lambda e, bkX=bkX, head=head, pcol=pcol:
                               e.activation(out=pf_ap[:, pcol:pcol + 512], in_=pbank(bkX)[:],
                                            func=AF.Exp, scale=0.125, bias=crep[:, head:head + 1]),
                               waits=SR(bkX) + SR("crep") + list(ps_.r) + ([ps_.w] if ps_.w else []),
                               inc="SC")
                        setr(bkX, te)
                        ps_.w = te
                        ps_.r = []
                        if partial:
                            tm = E("gpsimd", lambda e, pcol=pcol, tt=tt, c=c:
                                   e.affine_select(out=pf_ap[:, pcol:pcol + 512],
                                                   in_=pf_ap[:, pcol:pcol + 512],
                                                   pattern=[[1, 512]], compare_op=ALU.is_ge,
                                                   fill=0.0, base=tt * 512 - c * 128,
                                                   channel_multiplier=-1),
                                   waits=[te], inc="GP")
                            ps_.w = tm
                        th = E("vector", lambda e, pcol=pcol:
                               e.tensor_copy(out=pph_ap[:, pcol:pcol + 512], in_=pf_ap[:, pcol:pcol + 512]),
                               waits=[ps_.w] + list(pp_.r) + ([pp_.w] if pp_.w else []), inc="VE")
                        ps_.r.append(th)
                        tl = E("vector", lambda e, pcol=pcol:
                               e.tensor_tensor(out=pph_ap[:, 2048 + pcol:2048 + pcol + 512],
                                               in0=pf_ap[:, pcol:pcol + 512],
                                               in1=pph_ap[:, pcol:pcol + 512], op=ALU.subtract),
                               inc="VE")
                        ps_.r.append(tl)
                        pp_.w = tl
                        pp_.r = []
                    stop = (c == nkt - 1)
                    first = (c == 0)
                    for (off0, p0) in ((0, 0), (1024, 64)):
                        pcol = off0 + ib * 512
                        pp_ = pp_slots[pcol]
                        colA = kc0 + (0 if p0 == 0 else 64)
                        ww = ([pp_.w] if pp_.w else []) + [lv1, lv2]
                        if first:
                            ww += SW("pb4") + SW("pb5")
                        E("tensor", lambda e, pcol=pcol, p0=p0, colA=colA, first=first:
                          e.matmul(pbt[4][p0:p0 + 64, :], vbuf[:, colA:colA + 64],
                                   pph_ap[:, pcol:pcol + 512], start=first, stop=False,
                                   tile_position=(0, p0)),
                          waits=ww)
                        E("tensor", lambda e, pcol=pcol, p0=p0, colA=colA:
                          e.matmul(pbt[4][p0:p0 + 64, :], vbuf[:, S + colA:S + colA + 64],
                                   pph_ap[:, pcol:pcol + 512], start=False, stop=False,
                                   tile_position=(0, p0)))
                        tpv = E("tensor", lambda e, pcol=pcol, p0=p0, colA=colA, stop=stop:
                                e.matmul(pbt[4][p0:p0 + 64, :], vbuf[:, colA:colA + 64],
                                         pph_ap[:, 2048 + pcol:2048 + pcol + 512], start=False,
                                         stop=stop, tile_position=(0, p0)),
                                inc=("TE" if (stop and p0 == 64) else None))
                        ob0 = 0 if p0 == 0 else 64
                        E("tensor", lambda e, pcol=pcol, p0=p0, ob0=ob0, first=first:
                          e.matmul(pbt[5][p0:p0 + 64, :], ones_b[:, ob0:ob0 + 64],
                                   pph_ap[:, pcol:pcol + 512], start=first, stop=False,
                                   tile_position=(0, p0)),
                          waits=SR("ones_b"))
                        tdn = E("tensor", lambda e, pcol=pcol, p0=p0, ob0=ob0, stop=stop:
                                e.matmul(pbt[5][p0:p0 + 64, :], ones_b[:, ob0:ob0 + 64],
                                         pph_ap[:, 2048 + pcol:2048 + pcol + 512], start=False,
                                         stop=stop, tile_position=(0, p0)),
                                inc=("TE" if (stop and p0 == 64) else None))
                        pp_.r.append(tpv)
                        pp_.r.append(tdn)
                        if stop and p0 == 64:
                            pv_tok, den_tok = tpv, tdn
                setw("pb4", pv_tok)
                setw("pb5", den_tok)
                setr("kbuf", last_sc[bkA])
                setr("kbuf", last_sc[bkB])
                setr("qbuf", last_sc[bkA])
                setr("qbuf", last_sc[bkB])
                setr("vbuf", pv_tok)
                trc = E("vector", lambda e: e.reciprocal(out=rden[:], in_=pbt[5][:]),
                        waits=SR("pb5") + SW("rden"), inc="VE")
                setw("rden", trc)
                setr("pb5", trc)
                ta1 = E("vector", lambda e, hp=hp:
                        e.tensor_tensor(out=sa[:, hp * 512:(hp + 1) * 512], in0=pbt[4][:],
                                        in1=rden[:], op=ALU.mult),
                        waits=[trc] + SR("pb4") + (SW("sa") if hp == 0 else []), inc="VE")
                setr("pb4", ta1)
                setr("rden", ta1)
                if hp == 0:
                    slots["sa"].w = ta1
                    slots["sa"].r = []
                ta2 = E("vector", lambda e, hp=hp: e.tensor_copy(out=g1[:, hp * 512:(hp + 1) * 512],
                                                                 in_=sa[:, hp * 512:(hp + 1) * 512]),
                        waits=(SW("g1") if hp == 0 else []), inc="VE")
                if hp == 0:
                    slots["g1"].w = ta2
                    slots["g1"].r = []
                ta3 = E("vector", lambda e, hp=hp: e.tensor_tensor(out=g2[:, hp * 512:(hp + 1) * 512],
                                                                   in0=sa[:, hp * 512:(hp + 1) * 512],
                                                                   in1=g1[:, hp * 512:(hp + 1) * 512],
                                                                   op=ALU.subtract),
                        waits=(SW("g2") if hp == 0 else []), inc="VE")
                if hp == 0:
                    slots["g2"].w = ta3
                    slots["g2"].r = []
            slots["g1"].w = ta3
            slots["g2"].w = ta3
            for oc in range(NCH):
                wts, b0 = wload(wsp["woh"], wsp["wol"], oc, oc % 2)
                wh_f, wl_f = wchunks(b0)
                bk = f"pb{4 + (oc % 2)}"
                tok = mm3(bk, wh_f, wl_f,
                          g1, g2, lambda t_, kc: t_[:, kc * 512:(kc + 1) * 512],
                          SR("g1") + SR("g2") + wts)
                setr("g1", tok)
                setr("g2", tok)
                setr(f"wb{oc % 2}", tok)
                w0 = oc * S + tt * 512
                tv = E("vector", lambda e, w0=w0, bk=bk:
                       e.tensor_tensor(out=yt[:, w0:w0 + 512], in0=pbank(bk)[:],
                                       in1=yt[:, w0:w0 + 512], op=ALU.add),
                       waits=SR(bk) + SWyt(tt), inc="VE")
                setr(bk, tv)
                setw_yt(tt, tv)
        allp = []
        for off in (0, 512, 1024, 1536):
            allp += list(pf_slots[off].r) + ([pf_slots[off].w] if pf_slots[off].w else [])
            allp += list(pp_slots[off].r) + ([pp_slots[off].w] if pp_slots[off].w else [])
        slots["sB"].r.extend(allp)
        slots["g3"].r.extend(allp)

    emit_consts()
    emit_wsplit()
    for s in range(SEQ):
        emit_load_x(s)
        if "ffn1" in phases:
            emit_ln(s)
            emit_ffn(s)
        if "attn" in phases:
            emit_ln(s)
            emit_attn(s)
        if "ffn2" in phases:
            emit_ln(s)
            emit_ffn(s)
        emit_store(s)

    with contextlib.ExitStack() as bstack:
        sems = {nm: bstack.enter_context(nc.semaphore(nm)) for nm in SEMNAMES}
        blk = bstack.enter_context(nc.Block())
        final_st = sched.cnt.get("ST", 0)

        @blk.sync
        def _(eng):
            sched.replay("sync", eng, sems)

        @blk.gpsimd
        def _(eng):
            sched.replay("gpsimd", eng, sems)
            eng.wait_ge(sems["ST"], final_st)

        @blk.tensor
        def _(eng):
            sched.replay("tensor", eng, sems)

        @blk.vector
        def _(eng):
            sched.replay("vector", eng, sems)

        @blk.scalar
        def _(eng):
            sched.replay("scalar", eng, sems)

    stack.close()
    return nc


_NC_CACHE = {}


def kernel(L0_attn_x, L0_attn_qkv, L0_attn_wo, L0_ffn_W_r, L0_ffn_W_k,
           L0_ffn_W_v, L0_ffn_mu_r, L0_ffn_mu_k, _phases=("ffn1", "attn", "ffn2"),
           _dbg=False):
    x = np.ascontiguousarray(np.asarray(L0_attn_x, dtype=np.float32).reshape(B * S, D))
    wq = np.ascontiguousarray(np.asarray(L0_attn_qkv, dtype=np.float32).T)
    wo = np.ascontiguousarray(np.asarray(L0_attn_wo, dtype=np.float32).T)
    wr = np.ascontiguousarray(np.asarray(L0_ffn_W_r, dtype=np.float32).T)
    wk = np.ascontiguousarray(np.asarray(L0_ffn_W_k, dtype=np.float32).T)
    wv = np.ascontiguousarray(np.asarray(L0_ffn_W_v, dtype=np.float32).T)
    mur_ = np.ascontiguousarray(np.asarray(L0_ffn_mu_r, dtype=np.float32))
    muk_ = np.ascontiguousarray(np.asarray(L0_ffn_mu_k, dtype=np.float32))

    key = (_phases, _dbg)
    if key not in _NC_CACHE:
        _NC_CACHE[key] = build_program(phases=_phases, dbg=_dbg)
    nc = _NC_CACHE[key]

    in_maps = []
    for c in range(NCORE):
        shard = x[c * SEQ * S:(c + 1) * SEQ * S]
        in_maps.append({"x_in": shard, "wq_t": wq, "wo_t": wo, "wr_t": wr,
                        "wk_t": wk, "wv_t": wv, "mu_r": mur_, "mu_k": muk_})
    res = run_bass_kernel_spmd(nc, in_maps, core_ids=list(range(NCORE)))
    outs = [np.asarray(r["out"]) for r in res.results]
    full = np.concatenate(outs, axis=0).reshape(B, S, D)
    if _dbg:
        return full, res.results
    return full


# version 04
# speedup vs baseline: 1.4884x; 1.2274x over previous
import contextlib
import numpy as np
import concourse.bass as bass
import concourse.mybir as mybir
from concourse.bass_utils import run_bass_kernel_spmd

F32 = mybir.dt.float32
I32 = mybir.dt.int32
BF16 = mybir.dt.bfloat16
AF = mybir.ActivationFunctionType
ALU = mybir.AluOpType
AX = mybir.AxisListType

B, S, D = 32, 2048, 1024
H, DH = 16, 64
NCORE = 8
SEQ = B // NCORE
NCH = D // 128
NTT = S // 512
NT2 = S // 128

INV_2PI = float(1.0 / (2 * np.pi))
HALF_PI = float(np.pi / 2)
T2PI_HI = float(np.float32(round((2 * np.pi) * 2048.0) / 2048.0))
T2PI_LO = float(np.float32(2 * np.pi - T2PI_HI))
EPS = 1e-5

ENGINES = ("sync", "gpsimd", "tensor", "vector", "scalar")


class Slot:
    __slots__ = ("w", "r")

    def __init__(self):
        self.w = None
        self.r = []


class Sched:
    def __init__(self):
        self.ops = {e: [] for e in ENGINES}
        self.cnt = {}
        self.hiwater = {e: {} for e in ENGINES}

    def op(self, eng, fn, waits=(), inc=None, incn=1):
        hw = self.hiwater[eng]
        pruned = {}
        for t in waits:
            if t is None:
                continue
            s, v = t
            if hw.get(s, -1) >= v or pruned.get(s, -1) >= v:
                continue
            pruned[s] = v
        for s, v in pruned.items():
            hw[s] = v
        tok = None
        if inc is not None:
            self.cnt[inc] = self.cnt.get(inc, 0) + incn
            tok = (inc, self.cnt[inc])
        self.ops[eng].append((fn, tuple(pruned.items()), inc, incn))
        return tok

    def replay(self, eng_name, eng, sems):
        for fn, waits, inc, incn in self.ops[eng_name]:
            for s, v in waits:
                eng.wait_ge(sems[s], v)
            instr = fn(eng)
            if inc is not None:
                if instr is None:
                    eng.sem_inc(sems[inc], incn)
                else:
                    instr.then_inc(sems[inc], incn)


def build_program(phases=("ffn1", "attn", "ffn2"), dbg=False):
    nc = bass.Bass("TRN2", target_bir_lowering=False)

    x_in = nc.dram_tensor("x_in", [SEQ * S, D], F32, kind="ExternalInput").ap()
    wq_t = nc.dram_tensor("wq_t", [D, 3 * D], F32, kind="ExternalInput").ap()
    wo_t = nc.dram_tensor("wo_t", [D, D], F32, kind="ExternalInput").ap()
    wr_t = nc.dram_tensor("wr_t", [D, D], F32, kind="ExternalInput").ap()
    wk_t = nc.dram_tensor("wk_t", [D, D], F32, kind="ExternalInput").ap()
    wv_t = nc.dram_tensor("wv_t", [D, D], F32, kind="ExternalInput").ap()
    mu_r = nc.dram_tensor("mu_r", [D], F32, kind="ExternalInput").ap()
    mu_k = nc.dram_tensor("mu_k", [D], F32, kind="ExternalInput").ap()
    out = nc.dram_tensor("out", [SEQ * S, D], F32, kind="ExternalOutput").ap()

    wsp = {}
    for nm, dout in [("wq", 3 * D), ("wo", D), ("wr", D), ("wk", D), ("wv", D)]:
        for h in ("h", "l"):
            wsp[nm + h] = nc.dram_tensor(f"wsp_{nm}{h}", [dout // 128, 128, 8 * 128], BF16).ap()
    kw = dict(kind="ExternalOutput") if dbg else {}
    q_sp = {h: nc.dram_tensor(f"q_sp{h}", [D, S], BF16, **kw).ap() for h in "hl"}
    k_sp = {h: nc.dram_tensor(f"k_sp{h}", [D, S], BF16, **kw).ap() for h in "hl"}
    v_sp = {h: nc.dram_tensor(f"v_sp{h}", [S, D], BF16, **kw).ap() for h in "hl"}
    dbg_crep = nc.dram_tensor("dbg_crep", [128, H], F32, kind="ExternalOutput").ap() if dbg else None

    sched = Sched()
    E = sched.op
    stack = contextlib.ExitStack()
    slots = {}

    def sb(name, shape, dtype):
        t = stack.enter_context(nc.sbuf_tensor(name, shape, dtype))
        slots[name] = Slot()
        return t

    ident32 = sb("ident32", [128, 128], F32)
    identb = sb("identb", [128, 128], BF16)
    onesf = sb("onesf", [1, 128], F32)
    rt_b = sb("rt_b", [128, 128], BF16)
    ones_b = sb("ones_b", [128, 128], BF16)
    blk_b = sb("blk_b", [128, 2], BF16)
    cosr = sb("cosr", [128, S], F32)
    sinr = sb("sinr", [128, S], F32)
    mur = sb("mur", [128, NCH], F32)
    muk = sb("muk", [128, NCH], F32)
    cEps = sb("cEps", [128, 1], F32)
    cHpi = sb("cHpi", [128, 1], F32)
    invf_c = sb("invf_c", [128, 1], F32)
    crep = sb("crep", [128, H], F32)
    bnd = sb("bnd", [128, NCH * NTT], F32)
    rowb = sb("rowb", [128, S], F32)
    rrep = sb("rrep", [128, S], F32)
    mrrep = sb("mrrep", [128, S], F32)
    yt = sb("yt", [128, NCH * S], F32)
    sa = sb("sa", [128, 4104], F32)
    sB = sb("sB", [128, 4096], F32)
    g1 = sb("g1", [128, 4096], BF16)
    g2 = sb("g2", [128, 4096], BF16)
    g3 = sb("g3", [128, 4096], BF16)
    g4 = sb("g4", [128, 4096], BF16)
    kbuf = sb("kbuf", [128, 2 * S], BF16)
    vbuf = sb("vbuf", [128, 2 * S], BF16)
    qbuf = sb("qbuf", [128, 1024], BF16)
    wbuf = sb("wbuf", [128, 4096], BF16)
    slots["wb0"] = Slot()
    slots["wb1"] = Slot()
    iost = sb("iost", [128, 1024], F32)
    rden = sb("rden", [128, 512], F32)
    nn_i = sb("nn_i", [128, 512], I32)


    pbt = []
    for i in range(7):
        t = stack.enter_context(nc.psum_tensor(f"pb{i}", [128, 512], F32))
        slots[f"pb{i}"] = Slot()
        pbt.append(t)
    pbb = stack.enter_context(nc.psum_tensor("pbb", [128, 1024], BF16))
    slots["pbb"] = Slot()

    def pbank(bk):
        return pbt[int(bk[2])]

    SEMNAMES = ["LD", "LW", "ST", "TE", "VE", "SC", "GP"]

    def SW(name):
        s = slots[name]
        return list(s.r) + ([s.w] if s.w else [])

    def SR(name):
        s = slots[name]
        return [s.w] if s.w else []

    def setw(name, tok):
        s = slots[name]
        s.w = tok
        s.r = []

    def setr(name, tok):
        if tok is not None:
            slots[name].r.append(tok)

    CONST_NAMES = ["ident32", "identb", "onesf", "rt_b", "ones_b",
                   "blk_b", "cEps", "cHpi", "invf_c"]

    yts_ = [Slot() for _ in range(NTT)]

    def SRyt(tt):
        return [yts_[tt].w] if yts_[tt].w else []

    def SWyt(tt):
        return list(yts_[tt].r) + SRyt(tt)

    def setw_yt(tt, tok):
        yts_[tt].w = tok
        yts_[tt].r = []

    def setr_yt(tt, tok):
        if tok is not None:
            yts_[tt].r.append(tok)

    def SRyt_all():
        out = []
        for tt in range(NTT):
            out += SRyt(tt)
        return out

    def SWyt_all():
        out = []
        for tt in range(NTT):
            out += SWyt(tt)
        return out

    def emit_consts():
        g = "gpsimd"
        E(g, lambda e: e.memset(cEps[:], EPS))
        E(g, lambda e: e.memset(cHpi[:], HALF_PI))
        E(g, lambda e: e.memset(onesf[:], 1.0))
        E(g, lambda e: e.memset(ones_b[:], 1.0))
        E(g, lambda e: e.memset(blk_b[:], 0.0))
        E(g, lambda e: e.memset(blk_b[0:64, 0:1], 1.0))
        E(g, lambda e: e.memset(blk_b[64:128, 1:2], 1.0))
        for t, _ in ((ident32, F32), (identb, BF16)):
            E(g, lambda e, t=t: e.memset(t[:], 1.0))
            E(g, lambda e, t=t: e.affine_select(out=t[:], in_=t[:],
                                                pattern=[[1, 128]], compare_op=ALU.is_equal,
                                                fill=0.0, base=0, channel_multiplier=-1))
        E(g, lambda e: e.memset(rt_b[:], -1.0))
        E(g, lambda e: e.affine_select(out=rt_b[:], in_=rt_b[:],
                                       pattern=[[1, 128]], compare_op=ALU.is_equal,
                                       fill=0.0, base=32, channel_multiplier=-1))
        E(g, lambda e: e.affine_select(out=rt_b[:], in_=rt_b[:],
                                       pattern=[[0, 2], [-1, 64]], compare_op=ALU.is_ge,
                                       fill=0.0, base=31, channel_multiplier=0))
        E(g, lambda e: e.memset(sa[:, 0:128], 1.0))
        E(g, lambda e: e.affine_select(out=sa[:, 0:128], in_=sa[:, 0:128],
                                       pattern=[[1, 128]], compare_op=ALU.is_equal,
                                       fill=0.0, base=-32, channel_multiplier=-1))
        tgp = E(g, lambda e: e.affine_select(out=sa[:, 0:128], in_=sa[:, 0:128],
                                             pattern=[[0, 2], [1, 64]], compare_op=ALU.is_ge,
                                             fill=0.0, base=-32, channel_multiplier=0),
                inc="GP")
        tve = E("vector", lambda e: e.tensor_tensor(out=rt_b[:], in0=rt_b[:],
                                                    in1=sa[:, 0:128], op=ALU.add),
                waits=[tgp], inc="VE")
        setw("sa", tve)

        t1 = E("sync", lambda e: e.dma_start(out=mur[:], in_=mu_r.rearrange("(c p) -> p c", p=128),
                                             allow_slow_non_contiguous=True),
               inc="LD", incn=16)
        t2 = E("sync", lambda e: e.dma_start(out=muk[:], in_=mu_k.rearrange("(c p) -> p c", p=128),
                                             allow_slow_non_contiguous=True),
               inc="LD", incn=16)
        setw("mur", t1)
        setw("muk", t2)

        tio = E(g, lambda e: e.iota(out=sinr[:, :], pattern=[[1, S]], base=0,
                                    channel_multiplier=0,
                                    allow_small_or_imprecise_dtypes=True), inc="GP")
        for j in range(128):
            val = float(np.float32(10000.0 ** (-(j % 32) / 32.0)))
            E(g, lambda e, j=j, val=val: e.memset(rowb[0:1, j:j + 1], val))
        tgp2 = E(g, lambda e: e.engine_nop(), inc="GP")
        ttr0 = E("tensor", lambda e: e.transpose(pbt[6][:, 0:1], rowb[0:1, 0:128], onesf[0:1, 0:1]),
                 waits=[tgp2] + SW("pb6"), inc="TE")
        setw("pb6", ttr0)
        tiv = E("vector", lambda e: e.tensor_copy(out=invf_c[:], in_=pbt[6][:, 0:1]),
                waits=SR("pb6"), inc="VE")
        setr("pb6", tiv)
        setw("rowb", tgp2)
        tv1 = E("vector", lambda e: e.tensor_scalar(out=sinr[:], in0=sinr[:],
                                                    scalar1=invf_c[:], scalar2=None,
                                                    op0=ALU.mult),
                waits=[tio, tiv], inc="VE")
        tlast = tv1
        for j in range(S // 512):
            c0 = j * 512
            ta = E("vector", lambda e, c0=c0: e.tensor_scalar(out=nn_i[:], in0=sinr[:, c0:c0 + 512],
                                                              scalar1=INV_2PI, scalar2=None,
                                                              op0=ALU.mult),
                   waits=[tlast], inc="VE")
            tb = E("vector", lambda e, c0=c0: e.tensor_copy(out=sB[:, c0:c0 + 512], in_=nn_i[:]),
                   waits=[ta], inc="VE")
            tlast = tb
        tv4 = E("vector", lambda e: e.scalar_tensor_tensor(out=sinr[:], in0=sB[:, 0:S],
                                                           scalar=-T2PI_HI, in1=sinr[:],
                                                           op0=ALU.mult, op1=ALU.add),
                waits=[tlast], inc="VE")
        tv5 = E("vector", lambda e: e.scalar_tensor_tensor(out=sinr[:], in0=sB[:, 0:S],
                                                           scalar=-T2PI_LO, in1=sinr[:],
                                                           op0=ALU.mult, op1=ALU.add),
                waits=[tv4], inc="VE")
        ts1 = E("scalar", lambda e: e.activation(out=sB[:, 0:S], in_=sinr[:], func=AF.Abs),
                waits=[tv5], inc="SC")
        ts2 = E("scalar", lambda e: e.activation(out=cosr[:], in_=sB[:, 0:S], func=AF.Sin,
                                                 scale=-1.0, bias=cHpi[:]),
                waits=[ts1], inc="SC")
        ts3 = E("scalar", lambda e: e.activation(out=sinr[:], in_=sinr[:], func=AF.Sin),
                waits=[ts2], inc="SC")
        setw("cosr", ts2)
        setw("sinr", ts3)
        setw("sB", ts1)
        tcb = E(g, lambda e: e.engine_nop(), inc="GP")
        for nm in CONST_NAMES:
            slots[nm].w = tcb

    def emit_wsplit():
        srcs = [(wq_t, "wq", 3 * D), (wo_t, "wo", D), (wr_t, "wr", D),
                (wk_t, "wk", D), (wv_t, "wv", D)]
        for src, nm, dout in srcs:
            for kc in range(NCH):
                tld = E("sync", lambda e, src=src, kc=kc, dout=dout:
                        e.dma_start(out=sa[:, 0:dout], in_=src[kc * 128:(kc + 1) * 128, :]),
                        waits=SW("sa"), inc="LD", incn=16)
                setw("sa", tld)
                th = E("vector", lambda e, dout=dout: e.tensor_copy(out=g1[:, 0:dout], in_=sa[:, 0:dout]),
                       waits=SR("sa") + SW("g1"), inc="VE")
                setw("g1", th)
                tl = E("vector", lambda e, dout=dout: e.tensor_tensor(out=g2[:, 0:dout], in0=sa[:, 0:dout],
                                                                      in1=g1[:, 0:dout], op=ALU.subtract),
                       waits=SW("g2"), inc="VE")
                setw("g2", tl)
                setr("sa", tl)
                noc = dout // 128
                hf = noc // 2
                for g_t, gn, suf in ((g1, "g1", "h"), (g2, "g2", "l")):
                    for lo, hi in ((0, hf), (hf, noc)):
                        st_ = E("gpsimd", lambda e, nm=nm, kc=kc, suf=suf, lo=lo, hi=hi, g_t=g_t:
                                e.dma_start(
                                    out=wsp[nm + suf][lo:hi, :, kc * 128:(kc + 1) * 128]
                                    .rearrange("oc p d -> p oc d"),
                                    in_=g_t[:, lo * 128:hi * 128]
                                    .rearrange("p (oc d) -> p oc d", d=128)),
                                waits=SR(gn), inc="ST", incn=16)
                        setr(gn, st_)

    def emit_load_x(s):
        ytw = SWyt_all()
        for t2 in range(NT2):
            tld = E("sync", lambda e, s=s, t2=t2:
                    e.dma_start(out=iost[:], in_=x_in[s * S + t2 * 128: s * S + (t2 + 1) * 128, :]),
                    waits=SW("iost"), inc="LD", incn=16)
            setw("iost", tld)
            for c in range(NCH):
                bk = f"pb{5 + (c % 2)}"
                tmm = E("tensor", lambda e, c=c, bk=bk:
                        e.transpose(pbank(bk)[:, 0:128], iost[:, c * 128:(c + 1) * 128], ident32[:]),
                        waits=SR("iost") + SW(bk) + SR("ident32"), inc="TE")
                setw(bk, tmm)
                setr("iost", tmm)
                tev = E("scalar", lambda e, c=c, t2=t2, bk=bk:
                        e.copy(out=yt[:, c * S + t2 * 128: c * S + (t2 + 1) * 128], in_=pbank(bk)[:, 0:128]),
                        waits=SR(bk) + (ytw if (t2 == 0 and c == 0) else []), inc="SC")
                setr(bk, tev)
        for tt in range(NTT):
            setw_yt(tt, tev)

    def emit_store(s):
        laststore = None
        for t2 in range(NT2):
            evs = []
            for c in range(NCH):
                bk = f"pb{5 + (c % 2)}"
                tmm = E("tensor", lambda e, c=c, t2=t2, bk=bk:
                        e.transpose(pbank(bk)[:, 0:128],
                                    yt[:, c * S + t2 * 128: c * S + (t2 + 1) * 128], ident32[:]),
                        waits=SRyt(t2 // 4) + SW(bk), inc="TE")
                setw(bk, tmm)
                setr_yt(t2 // 4, tmm)
                tev = E("scalar", lambda e, c=c, bk=bk:
                        e.copy(out=iost[:, c * 128:(c + 1) * 128], in_=pbank(bk)[:, 0:128]),
                        waits=SR(bk) + (SW("iost") if c == 0 else []), inc="SC")
                setr(bk, tev)
                evs.append(tev)
            slots["iost"].w = evs[-1]
            slots["iost"].r = []
            laststore = E("gpsimd", lambda e, s=s, t2=t2:
                          e.dma_start(out=out[s * S + t2 * 128: s * S + (t2 + 1) * 128, :], in_=iost[:]),
                          waits=evs, inc="ST", incn=16)
            setr("iost", laststore)
        return laststore

    def emit_ln(s):
        for tt in range(NTT):
            ty = tq2 = None
            for c in range(NCH):
                w0 = c * S + tt * 512
                th = E("scalar", lambda e, w0=w0: e.copy(out=g1[:, 0:512], in_=yt[:, w0:w0 + 512]),
                       waits=SRyt(tt) + SW("g1"), inc="SC")
                setw("g1", th)
                tl = E("vector", lambda e, w0=w0: e.tensor_tensor(out=g2[:, 0:512], in0=yt[:, w0:w0 + 512],
                                                                  in1=g1[:, 0:512], op=ALU.subtract),
                       waits=SW("g2") + [th] + SRyt(tt), inc="VE")
                setw("g2", tl)
                setr_yt(tt, tl)
                tqs = E("scalar", lambda e, w0=w0: e.activation(out=sB[:, 0:512], in_=yt[:, w0:w0 + 512],
                                                                func=AF.Square),
                        waits=SRyt(tt) + SW("sB"), inc="SC")
                setw("sB", tqs)
                setr_yt(tt, th)
                setr_yt(tt, tqs)
                th2 = E("scalar", lambda e: e.copy(out=g3[:, 0:512], in_=sB[:, 0:512]),
                        waits=SR("sB") + SW("g3"), inc="SC")
                setw("g3", th2)
                tl2 = E("vector", lambda e: e.tensor_tensor(out=g4[:, 0:512], in0=sB[:, 0:512],
                                                            in1=g3[:, 0:512], op=ALU.subtract),
                        waits=SW("g4") + [th2] + SR("sB"), inc="VE")
                setw("g4", tl2)
                setr("sB", tl2)
                first = (c == 0)
                stop = (c == NCH - 1)
                E("tensor", lambda e, first=first: e.matmul(pbt[6][0:1, :], ones_b[:, 0:1], g1[:, 0:512],
                                                            start=first, stop=False),
                  waits=SR("g1") + SR("g2") + SR("ones_b") + (SW("pb6") if first else []))
                ty = E("tensor", lambda e, stop=stop: e.matmul(pbt[6][0:1, :], ones_b[:, 0:1], g2[:, 0:512],
                                                               start=False, stop=stop), inc="TE")
                setr("g1", ty)
                setr("g2", ty)
                E("tensor", lambda e, first=first: e.matmul(pbt[5][0:1, :], ones_b[:, 0:1], g3[:, 0:512],
                                                            start=first, stop=False),
                  waits=SR("g3") + SR("g4") + (SW("pb5") if first else []))
                tq2 = E("tensor", lambda e, stop=stop: e.matmul(pbt[5][0:1, :], ones_b[:, 0:1], g4[:, 0:512],
                                                                start=False, stop=stop), inc="TE")
                setr("g3", tq2)
                setr("g4", tq2)
            setw("pb6", ty)
            setw("pb5", tq2)
            e1 = E("vector", lambda e, tt=tt: e.tensor_copy(out=rowb[0:1, tt * 512:(tt + 1) * 512],
                                                            in_=pbt[6][0:1, :]),
                   waits=SR("pb6") + (SW("rowb") if tt == 0 else []), inc="VE")
            setr("pb6", e1)
            e2 = E("vector", lambda e, tt=tt: e.tensor_copy(out=sa[0:1, tt * 512:(tt + 1) * 512],
                                                            in_=pbt[5][0:1, :]),
                   waits=SR("pb5") + (SW("sa") if tt == 0 else []), inc="VE")
            setr("pb5", e2)
            if tt == 0:
                slots["rowb"].w = e2
                slots["rowb"].r = []
                slots["sa"].w = e2
                slots["sa"].r = []
        slots["rowb"].w = e2
        slots["sa"].w = e2
        tv = E("vector", lambda e: e.tensor_scalar(out=sa[0:1, 2048:2048 + S], in0=rowb[0:1, 0:S],
                                                   scalar1=1.0 / D, scalar2=None, op0=ALU.mult),
               inc="VE")
        tv2 = E("vector", lambda e: e.tensor_tensor(out=rowb[0:1, 0:S], in0=sa[0:1, 2048:2048 + S],
                                                    in1=sa[0:1, 2048:2048 + S], op=ALU.mult),
                waits=[tv], inc="VE")
        tv3 = E("vector", lambda e: e.tensor_scalar(out=sa[0:1, 0:S], in0=sa[0:1, 0:S],
                                                    scalar1=1.0 / D, scalar2=None, op0=ALU.mult),
                waits=[tv2], inc="VE")
        tv4 = E("vector", lambda e: e.tensor_tensor(out=sa[0:1, 0:S], in0=sa[0:1, 0:S],
                                                    in1=rowb[0:1, 0:S], op=ALU.subtract),
                waits=[tv3], inc="VE")
        ts = E("scalar", lambda e: e.activation(out=sa[0:1, 0:S], in_=sa[0:1, 0:S],
                                                func=AF.Sqrt, bias=cEps[0:1, :]),
               waits=[tv4], inc="SC")
        tv5 = E("vector", lambda e: e.reciprocal(out=rowb[0:1, 0:S], in_=sa[0:1, 0:S]),
                waits=[ts], inc="VE")
        tv6 = E("vector", lambda e: e.tensor_tensor(out=sa[0:1, 2048:2048 + S],
                                                    in0=sa[0:1, 2048:2048 + S],
                                                    in1=rowb[0:1, 0:S], op=ALU.mult),
                waits=[tv5], inc="VE")
        slots["rowb"].w = tv6
        slots["sa"].w = tv6
        te2 = None
        for tt in range(NTT):
            bk = f"pb{5 + (tt % 2)}"
            tm = E("tensor", lambda e, tt=tt, bk=bk:
                   e.matmul(pbank(bk)[:, 0:512], onesf[0:1, :], rowb[0:1, tt * 512:(tt + 1) * 512],
                            start=True, stop=True),
                   waits=SR("rowb") + SW(bk) + SR("onesf"), inc="TE")
            setw(bk, tm)
            setr("rowb", tm)
            te1 = E("scalar", lambda e, tt=tt, bk=bk: e.copy(out=rrep[:, tt * 512:(tt + 1) * 512],
                                                             in_=pbank(bk)[:, 0:512]),
                    waits=SR(bk) + (SW("rrep") if tt == 0 else []), inc="SC")
            setr(bk, te1)
            if tt == 0:
                slots["rrep"].w = te1
                slots["rrep"].r = []
            tm2 = E("tensor", lambda e, tt=tt, bk=bk:
                    e.matmul(pbank(bk)[:, 0:512], onesf[0:1, :],
                             sa[0:1, 2048 + tt * 512:2048 + (tt + 1) * 512], start=True, stop=True),
                    waits=SR("sa") + [te1] + SR("onesf"), inc="TE")
            setw(bk, tm2)
            setr("sa", tm2)
            te2 = E("scalar", lambda e, tt=tt, bk=bk: e.copy(out=mrrep[:, tt * 512:(tt + 1) * 512],
                                                             in_=pbank(bk)[:, 0:512]),
                    waits=SR(bk) + (SW("mrrep") if tt == 0 else []), inc="SC")
            setr(bk, te2)
            if tt == 0:
                slots["mrrep"].w = te2
                slots["mrrep"].r = []
        slots["rrep"].w = te2
        slots["mrrep"].w = te2

    def mm3(bk, wsl_h, wsl_l, xh_t, xl_t, xsl, extra_waits):
        tok = None
        for kc in range(NCH):
            wh_ = wsl_h(kc)
            wl_ = wsl_l(kc)
            xh_ = xsl(xh_t, kc)
            xl_ = xsl(xl_t, kc)
            w = (extra_waits + SW(bk)) if kc == 0 else []
            E("tensor", lambda e, wh_=wh_, xh_=xh_, kc=kc, bk=bk:
              e.matmul(pbank(bk)[:], wh_, xh_, start=(kc == 0), stop=False), waits=w)
            E("tensor", lambda e, wl_=wl_, xh_=xh_, bk=bk:
              e.matmul(pbank(bk)[:], wl_, xh_, start=False, stop=False))
            tok = E("tensor", lambda e, wh_=wh_, xl_=xl_, kc=kc, bk=bk:
                    e.matmul(pbank(bk)[:], wh_, xl_, start=False, stop=(kc == NCH - 1)),
                    inc=("TE" if kc == NCH - 1 else None))
        setw(bk, tok)
        return tok

    def wload(src_h, src_l, oc, slot):
        bname = f"wb{slot}"
        b0 = slot * 2048
        t1 = E("sync", lambda e, oc=oc, src_h=src_h, b0=b0: e.dma_start(
            out=wbuf[:, b0:b0 + 1024], in_=src_h[oc]),
            waits=SW(bname), inc="LW", incn=16)
        t2 = E("sync", lambda e, oc=oc, src_l=src_l, b0=b0: e.dma_start(
            out=wbuf[:, b0 + 1024:b0 + 2048], in_=src_l[oc]),
            inc="LW", incn=16)
        setw(bname, t2)
        return [t1, t2], b0

    def wchunks(b0):
        return (lambda kc: wbuf[:, b0 + kc * 128: b0 + (kc + 1) * 128],
                lambda kc: wbuf[:, b0 + 1024 + kc * 128: b0 + 1024 + (kc + 1) * 128])

    def emit_ffn(s):
        t = None
        for c in range(NCH):
            for j in range(1, NTT):
                col = c * S + j * 512 - 1
                t = E("vector", lambda e, c=c, j=j, col=col:
                      e.tensor_copy(out=bnd[:, (c * NTT + j):(c * NTT + j) + 1], in_=yt[:, col:col + 1]),
                      waits=SRyt(j - 1) + (SW("bnd") if (c == 0 and j == 1) else []), inc="VE")
                setr_yt(j - 1, t)
        setw("bnd", t)
        for tt in range(NTT):
            r0 = tt * 512
            bq = None
            for c in range(NCH):
                base = c * 513
                if tt == 0:
                    bq = E("vector", lambda e, base=base: e.memset(sa[:, base:base + 1], 0.0),
                           waits=(SW("sa") if c == 0 else []), inc="VE")
                else:
                    bq = E("vector", lambda e, base=base, c=c, tt=tt, r0=r0:
                           e.tensor_tensor(out=sa[:, base:base + 1],
                                           in0=bnd[:, c * NTT + tt:c * NTT + tt + 1],
                                           in1=rrep[:, r0 - 1:r0], op=ALU.mult),
                           waits=SR("bnd") + SR("rrep") + (SW("sa") if c == 0 else []), inc="VE")
            if tt != 0:
                bq2 = None
                for c in range(NCH):
                    base = c * 513
                    bq2 = E("vector", lambda e, base=base, r0=r0:
                            e.tensor_tensor(out=sa[:, base:base + 1], in0=sa[:, base:base + 1],
                                            in1=mrrep[:, r0 - 1:r0], op=ALU.subtract),
                            waits=SR("mrrep") + ([bq] if c == 0 else []), inc="VE")
                bq = bq2
            mix_last = None
            for c in range(NCH):
                w0 = c * S + tt * 512
                base = c * 513
                t1 = E("vector", lambda e, base=base, w0=w0, r0=r0:
                       e.tensor_tensor(out=sa[:, base + 1: base + 513], in0=yt[:, w0:w0 + 512],
                                       in1=rrep[:, r0:r0 + 512], op=ALU.mult),
                       waits=SRyt(tt) + SR("rrep"), inc="VE")
                setr_yt(tt, t1)
                E("vector", lambda e, base=base, r0=r0:
                  e.tensor_tensor(out=sa[:, base + 1: base + 513], in0=sa[:, base + 1: base + 513],
                                  in1=mrrep[:, r0:r0 + 512], op=ALU.subtract),
                  waits=SR("mrrep"), inc="VE")
                E("vector", lambda e, base=base, c=c:
                  e.tensor_tensor(out=sB[:, c * 512:(c + 1) * 512], in0=sa[:, base + 1: base + 513],
                                  in1=sa[:, base: base + 512], op=ALU.subtract),
                  waits=([bq] if c == 0 else []) + (SW("sB") if c == 0 else []), inc="VE")
                E("vector", lambda e, c=c:
                  e.scalar_tensor_tensor(out=g3[:, c * 512:(c + 1) * 512],
                                         in0=sB[:, c * 512:(c + 1) * 512], scalar=muk[:, c:c + 1],
                                         in1=sa[:, c * 513: c * 513 + 512],
                                         op0=ALU.mult, op1=ALU.add),
                  waits=SR("muk") + (SW("g3") if c == 0 else []), inc="VE")
                E("vector", lambda e, c=c:
                  e.scalar_tensor_tensor(out=iost[:, 0:512],
                                         in0=sB[:, c * 512:(c + 1) * 512], scalar=muk[:, c:c + 1],
                                         in1=sa[:, c * 513: c * 513 + 512],
                                         op0=ALU.mult, op1=ALU.add),
                  waits=SW("iost"), inc="VE")
                t7 = E("vector", lambda e, c=c:
                       e.tensor_tensor(out=g4[:, c * 512:(c + 1) * 512], in0=iost[:, 0:512],
                                       in1=g3[:, c * 512:(c + 1) * 512], op=ALU.subtract),
                       waits=(SW("g4") if c == 0 else []), inc="VE")
                slots["iost"].w = t7
                slots["iost"].r = []
                E("vector", lambda e, c=c:
                  e.scalar_tensor_tensor(out=g1[:, c * 512:(c + 1) * 512],
                                         in0=sB[:, c * 512:(c + 1) * 512], scalar=mur[:, c:c + 1],
                                         in1=sa[:, c * 513: c * 513 + 512],
                                         op0=ALU.mult, op1=ALU.add),
                  waits=SR("mur") + (SW("g1") if c == 0 else []), inc="VE")
                E("vector", lambda e, c=c:
                  e.scalar_tensor_tensor(out=iost[:, 512:1024],
                                         in0=sB[:, c * 512:(c + 1) * 512], scalar=mur[:, c:c + 1],
                                         in1=sa[:, c * 513: c * 513 + 512],
                                         op0=ALU.mult, op1=ALU.add),
                  inc="VE")
                t9 = E("vector", lambda e, c=c:
                       e.tensor_tensor(out=g2[:, c * 512:(c + 1) * 512], in0=iost[:, 512:1024],
                                       in1=g1[:, c * 512:(c + 1) * 512], op=ALU.subtract),
                       waits=(SW("g2") if c == 0 else []), inc="VE")
                slots["iost"].w = t9
                mix_last = t9
            for nm in ("g1", "g2", "g3", "g4", "sa", "sB"):
                slots[nm].w = mix_last
                slots[nm].r = []
            ter = None
            for oc in range(NCH):
                wts, b0 = wload(wsp["wrh"], wsp["wrl"], oc, oc % 2)
                wh_f, wl_f = wchunks(b0)
                bk = f"pb{0 + (oc % 2)}"
                tok = mm3(bk, wh_f, wl_f,
                          g1, g2, lambda t_, kc: t_[:, kc * 512:(kc + 1) * 512],
                          SR("g1") + SR("g2") + wts)
                setr("g1", tok)
                setr("g2", tok)
                setr(f"wb{oc % 2}", tok)
                ter = E("scalar", lambda e, oc=oc, bk=bk:
                        e.activation(out=sa[:, oc * 512:(oc + 1) * 512], in_=pbank(bk)[:],
                                     func=AF.Sigmoid),
                        waits=SR(bk) + (SW("sa") if oc == 0 else []), inc="SC")
                setr(bk, ter)
                if oc == 0:
                    slots["sa"].w = ter
                    slots["sa"].r = []
            slots["sa"].w = ter
            tk4 = None
            for oc in range(NCH):
                wts, b0 = wload(wsp["wkh"], wsp["wkl"], oc, oc % 2)
                wh_f, wl_f = wchunks(b0)
                bk = f"pb{2 + (oc % 2)}"
                tok = mm3(bk, wh_f, wl_f,
                          g3, g4, lambda t_, kc: t_[:, kc * 512:(kc + 1) * 512],
                          SR("g3") + SR("g4") + wts)
                setr("g3", tok)
                setr("g4", tok)
                setr(f"wb{oc % 2}", tok)
                tk1 = E("scalar", lambda e, oc=oc, bk=bk:
                        e.activation(out=sB[:, oc * 512:(oc + 1) * 512], in_=pbank(bk)[:],
                                     func=AF.Relu),
                        waits=SR(bk) + (SW("sB") if oc == 0 else []), inc="SC")
                setr(bk, tk1)
                if oc == 0:
                    slots["sB"].w = tk1
                    slots["sB"].r = []
                tk2 = E("scalar", lambda e, oc=oc:
                        e.activation(out=sB[:, oc * 512:(oc + 1) * 512],
                                     in_=sB[:, oc * 512:(oc + 1) * 512], func=AF.Square),
                        inc="SC")
                tk3 = E("scalar", lambda e, oc=oc: e.copy(out=g1[:, oc * 512:(oc + 1) * 512],
                                                          in_=sB[:, oc * 512:(oc + 1) * 512]),
                        waits=[tk2] + (SW("g1") if oc == 0 else []), inc="SC")
                if oc == 0:
                    slots["g1"].w = tk3
                    slots["g1"].r = []
                tk4 = E("vector", lambda e, oc=oc: e.tensor_tensor(out=g2[:, oc * 512:(oc + 1) * 512],
                                                                   in0=sB[:, oc * 512:(oc + 1) * 512],
                                                                   in1=g1[:, oc * 512:(oc + 1) * 512],
                                                                   op=ALU.subtract),
                        waits=[tk3] + (SW("g2") if oc == 0 else []), inc="VE")
                if oc == 0:
                    slots["g2"].w = tk4
                    slots["g2"].r = []
            slots["g1"].w = tk4
            slots["g2"].w = tk4
            for oc in range(NCH):
                wts, b0 = wload(wsp["wvh"], wsp["wvl"], oc, oc % 2)
                wh_f, wl_f = wchunks(b0)
                bk = f"pb{4 + (oc % 2)}"
                tok = mm3(bk, wh_f, wl_f,
                          g1, g2, lambda t_, kc: t_[:, kc * 512:(kc + 1) * 512],
                          SR("g1") + SR("g2") + wts)
                setr("g1", tok)
                setr("g2", tok)
                setr(f"wb{oc % 2}", tok)
                w0 = oc * S + tt * 512
                tv1 = E("vector", lambda e, oc=oc, bk=bk:
                        e.tensor_tensor(out=iost[:, 0:512], in0=pbank(bk)[:],
                                        in1=sa[:, oc * 512:(oc + 1) * 512], op=ALU.mult),
                        waits=SR(bk) + SR("sa") + SW("iost"), inc="VE")
                setr(bk, tv1)
                setr("sa", tv1)
                slots["iost"].w = tv1
                slots["iost"].r = []
                tv2 = E("vector", lambda e, w0=w0:
                        e.scalar_tensor_tensor(out=yt[:, w0:w0 + 512], in0=iost[:, 0:512],
                                               scalar=0.5, in1=yt[:, w0:w0 + 512],
                                               op0=ALU.mult, op1=ALU.add),
                        waits=SWyt(tt), inc="VE")
                slots["iost"].r.append(tv2)
                setw_yt(tt, tv2)

    pf_slots = {off: Slot() for off in (0, 512, 1024, 1536)}
    pp_slots = {off: Slot() for off in (0, 512, 1024, 1536)}

    def emit_attn(s):
        qn_toks = []
        for tt in range(NTT):
            r0 = tt * 512
            for c in range(NCH):
                w0 = c * S + tt * 512
                t1 = E("vector", lambda e, c=c, w0=w0, r0=r0:
                       e.tensor_tensor(out=sa[:, c * 512:(c + 1) * 512], in0=yt[:, w0:w0 + 512],
                                       in1=rrep[:, r0:r0 + 512], op=ALU.mult),
                       waits=SRyt(tt) + SR("rrep") + (SW("sa") if c == 0 else []), inc="VE")
                setr_yt(tt, t1)
                if c == 0:
                    slots["sa"].w = t1
                    slots["sa"].r = []
                E("vector", lambda e, c=c, r0=r0:
                  e.tensor_tensor(out=sa[:, c * 512:(c + 1) * 512], in0=sa[:, c * 512:(c + 1) * 512],
                                  in1=mrrep[:, r0:r0 + 512], op=ALU.subtract),
                  waits=SR("mrrep"), inc="VE")
                t3 = E("vector", lambda e, c=c: e.tensor_copy(out=g1[:, c * 512:(c + 1) * 512],
                                                              in_=sa[:, c * 512:(c + 1) * 512]),
                       waits=(SW("g1") if c == 0 else []), inc="VE")
                if c == 0:
                    slots["g1"].w = t3
                    slots["g1"].r = []
                t4 = E("vector", lambda e, c=c: e.tensor_tensor(out=g2[:, c * 512:(c + 1) * 512],
                                                                in0=sa[:, c * 512:(c + 1) * 512],
                                                                in1=g1[:, c * 512:(c + 1) * 512],
                                                                op=ALU.subtract),
                       waits=(SW("g2") if c == 0 else []), inc="VE")
                if c == 0:
                    slots["g2"].w = t4
                    slots["g2"].r = []
            slots["g1"].w = t4
            slots["g2"].w = t4
            slots["sa"].w = t4

            def _qkv_post(bk, hp, oc):
                if oc < 2 * NCH:
                    is_q = oc < NCH
                    col = hp * 512
                    e1 = E("scalar", lambda e, col=col, bk=bk:
                           e.copy(out=g3[:, col:col + 512], in_=pbank(bk)[:]),
                           waits=SR(bk) + list(slots["g3"].r) + ([slots["g3"].w] if slots["g3"].w else []),
                           inc="SC")
                    slots["g3"].w = e1
                    slots["g3"].r = []
                    e2 = E("vector", lambda e, col=col, bk=bk:
                           e.tensor_tensor(out=g4[:, col:col + 512], in0=pbank(bk)[:],
                                           in1=g3[:, col:col + 512], op=ALU.subtract),
                           waits=SR(bk) + [e1] + list(slots["g4"].r) + ([slots["g4"].w] if slots["g4"].w else []),
                           inc="VE")
                    slots["g4"].w = e2
                    slots["g4"].r = []
                    E("tensor", lambda e, col=col: e.matmul(pbt[6][:], rt_b[:], g3[:, col:col + 512],
                                                            start=True, stop=False),
                      waits=[e1] + SW("pb6") + SR("rt_b"))
                    tr = E("tensor", lambda e, col=col: e.matmul(pbt[6][:], rt_b[:], g4[:, col:col + 512],
                                                                 start=False, stop=True),
                           waits=[e2], inc="TE")
                    setw("pb6", tr)
                    tv1 = E("vector", lambda e, bk=bk, r0=r0:
                            e.tensor_tensor(out=sB[:, 0:512], in0=pbank(bk)[:],
                                            in1=cosr[:, r0:r0 + 512], op=ALU.mult),
                            waits=SR(bk) + SR("cosr") + SW("sB"), inc="VE")
                    slots["sB"].w = tv1
                    slots["sB"].r = []
                    setr(bk, tv1)
                    tv2 = E("vector", lambda e, r0=r0:
                            e.tensor_tensor(out=sB[:, 512:1024], in0=pbt[6][:],
                                            in1=sinr[:, r0:r0 + 512], op=ALU.mult),
                            waits=SR("pb6") + SR("sinr"), inc="VE")
                    setr("pb6", tv2)
                    E("vector", lambda e: e.tensor_tensor(out=sB[:, 0:512], in0=sB[:, 0:512],
                                                          in1=sB[:, 512:1024], op=ALU.add),
                      inc="VE")
                    tv4 = E("vector", lambda e, col=col: e.tensor_copy(out=g3[:, col:col + 512],
                                                                       in_=sB[:, 0:512]),
                            waits=[tr], inc="VE")
                    slots["g3"].w = tv4
                    tv5 = E("vector", lambda e, col=col: e.tensor_tensor(out=g4[:, col:col + 512],
                                                                         in0=sB[:, 0:512],
                                                                         in1=g3[:, col:col + 512],
                                                                         op=ALU.subtract),
                            inc="VE")
                    slots["g4"].w = tv5
                    slots["sB"].w = tv5
                    dst = q_sp if is_q else k_sp
                    s1 = E("gpsimd", lambda e, dst=dst, hp=hp, tt=tt, col=col:
                           e.dma_start(out=dst["h"][hp * 128:(hp + 1) * 128, tt * 512:(tt + 1) * 512],
                                       in_=g3[:, col:col + 512]),
                           waits=[tv4], inc="ST", incn=16)
                    s2 = E("gpsimd", lambda e, dst=dst, hp=hp, tt=tt, col=col:
                           e.dma_start(out=dst["l"][hp * 128:(hp + 1) * 128, tt * 512:(tt + 1) * 512],
                                       in_=g4[:, col:col + 512]),
                           waits=[tv5], inc="ST", incn=16)
                    setr("g3", s1)
                    setr("g4", s2)
                    tq1 = E("vector", lambda e, col=col:
                            e.tensor_tensor(out=vbuf[:, 0:512], in0=g3[:, col:col + 512],
                                            in1=g3[:, col:col + 512], op=ALU.mult),
                            waits=[tv4] + SW("vbuf"), inc="VE")
                    setw("vbuf", tq1)
                    setr("g3", tq1)
                    for j in range(2):
                        head = 2 * hp + j
                        tqm = E("tensor", lambda e, j=j: e.matmul(pbt[5][0:1, :], blk_b[:, j:j + 1],
                                                                  vbuf[:, 0:512],
                                                                  start=True, stop=True),
                                waits=[tq1] + SW("pb5") + SR("blk_b"), inc="TE")
                        setw("pb5", tqm)
                        setr("vbuf", tqm)
                        coloff = (0 if is_q else 64) + head * 4 + tt
                        trm = E("vector", lambda e, coloff=coloff:
                                e.tensor_reduce(out=rowb[0:1, coloff:coloff + 1],
                                                in_=pbt[5][0:1, :], axis=AX.X, op=ALU.max),
                                waits=SR("pb5") + (qn_toks[-1:] if qn_toks else SW("rowb")),
                                inc="VE")
                        setr("pb5", trm)
                        qn_toks.append(trm)
                else:
                    e1 = E("scalar", lambda e, bk=bk: e.copy(out=g3[:, 0:512], in_=pbank(bk)[:]),
                           waits=SR(bk) + list(slots["g3"].r) + ([slots["g3"].w] if slots["g3"].w else []),
                           inc="SC")
                    slots["g3"].w = e1
                    slots["g3"].r = []
                    e2 = E("vector", lambda e, bk=bk: e.tensor_tensor(out=g4[:, 0:512],
                                                                      in0=pbank(bk)[:],
                                                                      in1=g3[:, 0:512], op=ALU.subtract),
                           waits=SR(bk) + [e1] + list(slots["g4"].r) + ([slots["g4"].w] if slots["g4"].w else []),
                           inc="VE")
                    slots["g4"].w = e2
                    slots["g4"].r = []
                    setr(bk, e2)
                    for half, gt, gn, etok in (("h", g3, "g3", e1), ("l", g4, "g4", e2)):
                        for sub in range(4):
                            tmm = E("tensor", lambda e, gt=gt, sub=sub:
                                    e.transpose(pbb[:, 0:128], gt[:, sub * 128:(sub + 1) * 128], identb[:]),
                                    waits=[etok] + SW("pbb") + SR("identb"), inc="TE")
                            setw("pbb", tmm)
                            setr(gn, tmm)
                            tev = E("vector", lambda e: e.tensor_copy(out=vbuf[:, 512:640], in_=pbb[:, 0:128]),
                                    waits=SR("pbb") + SW("vbuf"), inc="VE")
                            setw("vbuf", tev)
                            setr("pbb", tev)
                            tst = E("gpsimd", lambda e, half=half, hp=hp, tt=tt, sub=sub:
                                    e.dma_start(out=v_sp[half][tt * 512 + sub * 128: tt * 512 + (sub + 1) * 128,
                                                               hp * 128:(hp + 1) * 128],
                                                in_=vbuf[:, 512:640]),
                                    waits=[tev], inc="ST", incn=16)
                            setr("vbuf", tst)

            pending = [None]
            for oc in range(3 * NCH):
                wts, b0 = wload(wsp["wqh"], wsp["wql"], oc, oc % 2)
                wh_f, wl_f = wchunks(b0)
                bk = f"pb{0 + (oc % 2)}"
                tok = mm3(bk, wh_f, wl_f,
                          g1, g2, lambda t_, kc: t_[:, kc * 512:(kc + 1) * 512],
                          SR("g1") + SR("g2") + wts)
                setr("g1", tok)
                setr("g2", tok)
                setr(f"wb{oc % 2}", tok)
                if pending[0] is not None:
                    pending[0]()
                hp = oc % NCH
                pending[0] = (lambda bk=bk, hp=hp, oc=oc: _qkv_post(bk, hp, oc))
            pending[0]()

        tprev = qn_toks[-1]
        for h in range(H):
            tprev = E("vector", lambda e, h=h: e.tensor_reduce(out=rowb[0:1, 128 + h:129 + h],
                                                               in_=rowb[0:1, h * 4:h * 4 + 4],
                                                               axis=AX.X, op=ALU.max),
                      waits=[tprev], inc="VE")
            tprev = E("vector", lambda e, h=h: e.tensor_reduce(out=rowb[0:1, 144 + h:145 + h],
                                                               in_=rowb[0:1, 64 + h * 4:64 + h * 4 + 4],
                                                               axis=AX.X, op=ALU.max),
                      waits=[tprev], inc="VE")
        tsq = E("scalar", lambda e: e.activation(out=rowb[0:1, 128:160], in_=rowb[0:1, 128:160],
                                                 func=AF.Sqrt),
                waits=[tprev], inc="SC")
        tvc = E("vector", lambda e: e.tensor_tensor(out=rowb[0:1, 160:176], in0=rowb[0:1, 128:144],
                                                    in1=rowb[0:1, 144:160], op=ALU.mult),
                waits=[tsq], inc="VE")
        tvc2 = E("vector", lambda e: e.tensor_scalar(out=rowb[0:1, 160:176], in0=rowb[0:1, 160:176],
                                                     scalar1=-1.01 * 0.125, scalar2=-0.01,
                                                     op0=ALU.mult, op1=ALU.add),
                 waits=[tvc], inc="VE")
        trep = E("tensor", lambda e: e.matmul(pbt[5][:, 0:16], onesf[0:1, :], rowb[0:1, 160:176],
                                              start=True, stop=True),
                 waits=[tvc2] + SW("pb5") + SR("onesf"), inc="TE")
        setw("pb5", trep)
        setr("rowb", trep)
        slots["rowb"].w = tvc2
        tcrep = E("scalar", lambda e: e.copy(out=crep[:], in_=pbt[5][:, 0:16]),
                  waits=SR("pb5"), inc="SC")
        setr("pb5", tcrep)
        setw("crep", tcrep)
        if dbg:
            E("gpsimd", lambda e: e.dma_start(out=dbg_crep[:, :], in_=crep[:]),
              waits=SR("crep"), inc="ST", incn=16)

        trans = SW("sB") + SW("g3") + SW("g4")
        for off in (0, 512, 1024, 1536):
            pf_slots[off].r = list(trans)
            pf_slots[off].w = None
            pp_slots[off].r = list(trans)
            pp_slots[off].w = None
        pf_ap = sB
        pph_ap = g3

        ta3 = None
        for tt in range(NTT):
            nkt = (tt + 1) * 4
            for hp in range(NCH):
                headA, headB = 2 * hp, 2 * hp + 1
                lk1 = E("sync", lambda e, hp=hp, nkt=nkt:
                        e.dma_start(out=kbuf[:, 0:nkt * 128],
                                    in_=k_sp["h"][hp * 128:(hp + 1) * 128, 0:nkt * 128]),
                        waits=SW("kbuf"), inc="LD", incn=16)
                lk2 = E("sync", lambda e, hp=hp, nkt=nkt:
                        e.dma_start(out=kbuf[:, S:S + nkt * 128],
                                    in_=k_sp["l"][hp * 128:(hp + 1) * 128, 0:nkt * 128]),
                        inc="LD", incn=16)
                setw("kbuf", lk2)
                lv1 = E("sync", lambda e, hp=hp, nkt=nkt:
                        e.dma_start(out=vbuf[:, 0:nkt * 128].rearrange("p (n c) -> p n c", c=128),
                                    in_=v_sp["h"][0:nkt * 128, hp * 128:(hp + 1) * 128]
                                    .rearrange("(n p) c -> p n c", p=128)),
                        waits=SW("vbuf"), inc="LD", incn=16)
                lv2 = E("sync", lambda e, hp=hp, nkt=nkt:
                        e.dma_start(out=vbuf[:, S:S + nkt * 128].rearrange("p (n c) -> p n c", c=128),
                                    in_=v_sp["l"][0:nkt * 128, hp * 128:(hp + 1) * 128]
                                    .rearrange("(n p) c -> p n c", p=128)),
                        inc="LD", incn=16)
                setw("vbuf", lv2)
                lq1 = E("sync", lambda e, hp=hp, tt=tt:
                        e.dma_start(out=qbuf[:, 0:512],
                                    in_=q_sp["h"][hp * 128:(hp + 1) * 128, tt * 512:(tt + 1) * 512]),
                        waits=SW("qbuf"), inc="LD", incn=16)
                lq2 = E("sync", lambda e, hp=hp, tt=tt:
                        e.dma_start(out=qbuf[:, 512:1024],
                                    in_=q_sp["l"][hp * 128:(hp + 1) * 128, tt * 512:(tt + 1) * 512]),
                        inc="LD", incn=16)
                setw("qbuf", lq2)
                last_sc = {}
                pv_tok_box = [None, None]

                def emit_scores(c):
                    ib = c % 2
                    bkA = f"pb{0 + ib * 2}"
                    bkB = f"pb{1 + ib * 2}"
                    kc0 = c * 128
                    for (bkX, p0) in ((bkA, 0), (bkB, 64)):
                        w0 = SW(bkX) + [lk1, lk2, lq1, lq2]
                        E("tensor", lambda e, bkX=bkX, p0=p0, kc0=kc0:
                          e.matmul(pbank(bkX)[:], kbuf[p0:p0 + 64, kc0:kc0 + 128],
                                   qbuf[p0:p0 + 64, 0:512], start=True, stop=False,
                                   tile_position=(p0, 0)),
                          waits=w0)
                        E("tensor", lambda e, bkX=bkX, p0=p0, kc0=kc0:
                          e.matmul(pbank(bkX)[:], kbuf[p0:p0 + 64, kc0:kc0 + 128],
                                   qbuf[p0:p0 + 64, 512:1024], start=False, stop=False,
                                   tile_position=(p0, 0)))
                        tsc = E("tensor", lambda e, bkX=bkX, p0=p0, kc0=kc0:
                                e.matmul(pbank(bkX)[:], kbuf[p0:p0 + 64, S + kc0:S + kc0 + 128],
                                         qbuf[p0:p0 + 64, 0:512], start=False, stop=True,
                                         tile_position=(p0, 0)), inc="TE")
                        setw(bkX, tsc)
                        last_sc[bkX] = tsc

                def emit_expsplit(c):
                    ib = c % 2
                    bkA = f"pb{0 + ib * 2}"
                    bkB = f"pb{1 + ib * 2}"
                    partial = (c >= tt * 4)
                    for (bkX, head, off0) in ((bkA, headA, 0), (bkB, headB, 1024)):
                        pcol = off0 + ib * 512
                        ps_ = pf_slots[pcol]
                        pp_ = pp_slots[pcol]
                        te = E("scalar", lambda e, bkX=bkX, head=head, pcol=pcol:
                               e.activation(out=pf_ap[:, pcol:pcol + 512], in_=pbank(bkX)[:],
                                            func=AF.Exp, scale=0.125, bias=crep[:, head:head + 1]),
                               waits=SR(bkX) + SR("crep") + list(ps_.r) + ([ps_.w] if ps_.w else []),
                               inc="SC")
                        setr(bkX, te)
                        ps_.w = te
                        ps_.r = []
                        if partial:
                            tm = E("gpsimd", lambda e, pcol=pcol, tt=tt, c=c:
                                   e.affine_select(out=pf_ap[:, pcol:pcol + 512],
                                                   in_=pf_ap[:, pcol:pcol + 512],
                                                   pattern=[[1, 512]], compare_op=ALU.is_ge,
                                                   fill=0.0, base=tt * 512 - c * 128,
                                                   channel_multiplier=-1),
                                   waits=[te], inc="GP")
                            ps_.w = tm
                        th = E("vector", lambda e, pcol=pcol:
                               e.tensor_copy(out=pph_ap[:, pcol:pcol + 512], in_=pf_ap[:, pcol:pcol + 512]),
                               waits=[ps_.w] + list(pp_.r) + ([pp_.w] if pp_.w else []), inc="VE")
                        ps_.r.append(th)
                        tl = E("vector", lambda e, pcol=pcol:
                               e.tensor_tensor(out=pph_ap[:, 2048 + pcol:2048 + pcol + 512],
                                               in0=pf_ap[:, pcol:pcol + 512],
                                               in1=pph_ap[:, pcol:pcol + 512], op=ALU.subtract),
                               inc="VE")
                        ps_.r.append(tl)
                        pp_.w = tl
                        pp_.r = []
                def emit_pvden(c):
                    ib = c % 2
                    kc0 = c * 128
                    stop = (c == nkt - 1)
                    first = (c == 0)
                    for (off0, p0) in ((0, 0), (1024, 64)):
                        pcol = off0 + ib * 512
                        pp_ = pp_slots[pcol]
                        colA = kc0 + (0 if p0 == 0 else 64)
                        ww = ([pp_.w] if pp_.w else []) + [lv1, lv2]
                        if first:
                            ww += SW("pb4") + SW("pb5")
                        E("tensor", lambda e, pcol=pcol, p0=p0, colA=colA, first=first:
                          e.matmul(pbt[4][p0:p0 + 64, :], vbuf[:, colA:colA + 64],
                                   pph_ap[:, pcol:pcol + 512], start=first, stop=False,
                                   tile_position=(0, p0)),
                          waits=ww)
                        E("tensor", lambda e, pcol=pcol, p0=p0, colA=colA:
                          e.matmul(pbt[4][p0:p0 + 64, :], vbuf[:, S + colA:S + colA + 64],
                                   pph_ap[:, pcol:pcol + 512], start=False, stop=False,
                                   tile_position=(0, p0)))
                        tpv = E("tensor", lambda e, pcol=pcol, p0=p0, colA=colA, stop=stop:
                                e.matmul(pbt[4][p0:p0 + 64, :], vbuf[:, colA:colA + 64],
                                         pph_ap[:, 2048 + pcol:2048 + pcol + 512], start=False,
                                         stop=stop, tile_position=(0, p0)),
                                inc=("TE" if (stop and p0 == 64) else None))
                        ob0 = 0 if p0 == 0 else 64
                        E("tensor", lambda e, pcol=pcol, p0=p0, ob0=ob0, first=first:
                          e.matmul(pbt[5][p0:p0 + 64, :], ones_b[:, ob0:ob0 + 64],
                                   pph_ap[:, pcol:pcol + 512], start=first, stop=False,
                                   tile_position=(0, p0)),
                          waits=SR("ones_b"))
                        tdn = E("tensor", lambda e, pcol=pcol, p0=p0, ob0=ob0, stop=stop:
                                e.matmul(pbt[5][p0:p0 + 64, :], ones_b[:, ob0:ob0 + 64],
                                         pph_ap[:, 2048 + pcol:2048 + pcol + 512], start=False,
                                         stop=stop, tile_position=(0, p0)),
                                inc=("TE" if (stop and p0 == 64) else None))
                        pp_.r.append(tpv)
                        pp_.r.append(tdn)
                        if stop and p0 == 64:
                            pv_tok_box[0], pv_tok_box[1] = tpv, tdn

                emit_scores(0)
                for c in range(nkt):
                    emit_expsplit(c)
                    if c + 1 < nkt:
                        emit_scores(c + 1)
                    emit_pvden(c)
                pv_tok, den_tok = pv_tok_box
                setw("pb4", pv_tok)
                setw("pb5", den_tok)
                for tsc_ in last_sc.values():
                    setr("kbuf", tsc_)
                    setr("qbuf", tsc_)
                setr("vbuf", pv_tok)
                trc = E("vector", lambda e: e.reciprocal(out=rden[:], in_=pbt[5][:]),
                        waits=SR("pb5") + SW("rden"), inc="VE")
                setw("rden", trc)
                setr("pb5", trc)
                ta1 = E("vector", lambda e, hp=hp:
                        e.tensor_tensor(out=sa[:, hp * 512:(hp + 1) * 512], in0=pbt[4][:],
                                        in1=rden[:], op=ALU.mult),
                        waits=[trc] + SR("pb4") + (SW("sa") if hp == 0 else []), inc="VE")
                setr("pb4", ta1)
                setr("rden", ta1)
                if hp == 0:
                    slots["sa"].w = ta1
                    slots["sa"].r = []
                ta2 = E("vector", lambda e, hp=hp: e.tensor_copy(out=g1[:, hp * 512:(hp + 1) * 512],
                                                                 in_=sa[:, hp * 512:(hp + 1) * 512]),
                        waits=(SW("g1") if hp == 0 else []), inc="VE")
                if hp == 0:
                    slots["g1"].w = ta2
                    slots["g1"].r = []
                ta3 = E("vector", lambda e, hp=hp: e.tensor_tensor(out=g2[:, hp * 512:(hp + 1) * 512],
                                                                   in0=sa[:, hp * 512:(hp + 1) * 512],
                                                                   in1=g1[:, hp * 512:(hp + 1) * 512],
                                                                   op=ALU.subtract),
                        waits=(SW("g2") if hp == 0 else []), inc="VE")
                if hp == 0:
                    slots["g2"].w = ta3
                    slots["g2"].r = []
            slots["g1"].w = ta3
            slots["g2"].w = ta3
            for oc in range(NCH):
                wts, b0 = wload(wsp["woh"], wsp["wol"], oc, oc % 2)
                wh_f, wl_f = wchunks(b0)
                bk = f"pb{4 + (oc % 2)}"
                tok = mm3(bk, wh_f, wl_f,
                          g1, g2, lambda t_, kc: t_[:, kc * 512:(kc + 1) * 512],
                          SR("g1") + SR("g2") + wts)
                setr("g1", tok)
                setr("g2", tok)
                setr(f"wb{oc % 2}", tok)
                w0 = oc * S + tt * 512
                tv = E("vector", lambda e, w0=w0, bk=bk:
                       e.tensor_tensor(out=yt[:, w0:w0 + 512], in0=pbank(bk)[:],
                                       in1=yt[:, w0:w0 + 512], op=ALU.add),
                       waits=SR(bk) + SWyt(tt), inc="VE")
                setr(bk, tv)
                setw_yt(tt, tv)
        allp = []
        for off in (0, 512, 1024, 1536):
            allp += list(pf_slots[off].r) + ([pf_slots[off].w] if pf_slots[off].w else [])
            allp += list(pp_slots[off].r) + ([pp_slots[off].w] if pp_slots[off].w else [])
        slots["sB"].r.extend(allp)
        slots["g3"].r.extend(allp)

    emit_consts()
    emit_wsplit()
    for s in range(SEQ):
        emit_load_x(s)
        if "ffn1" in phases:
            emit_ln(s)
            emit_ffn(s)
        if "attn" in phases:
            emit_ln(s)
            emit_attn(s)
        if "ffn2" in phases:
            emit_ln(s)
            emit_ffn(s)
        emit_store(s)

    with contextlib.ExitStack() as bstack:
        sems = {nm: bstack.enter_context(nc.semaphore(nm)) for nm in SEMNAMES}
        blk = bstack.enter_context(nc.Block())
        final_st = sched.cnt.get("ST", 0)

        @blk.sync
        def _(eng):
            sched.replay("sync", eng, sems)

        @blk.gpsimd
        def _(eng):
            sched.replay("gpsimd", eng, sems)
            eng.wait_ge(sems["ST"], final_st)

        @blk.tensor
        def _(eng):
            sched.replay("tensor", eng, sems)

        @blk.vector
        def _(eng):
            sched.replay("vector", eng, sems)

        @blk.scalar
        def _(eng):
            sched.replay("scalar", eng, sems)

    stack.close()
    return nc


_NC_CACHE = {}


def kernel(L0_attn_x, L0_attn_qkv, L0_attn_wo, L0_ffn_W_r, L0_ffn_W_k,
           L0_ffn_W_v, L0_ffn_mu_r, L0_ffn_mu_k, _phases=("ffn1", "attn", "ffn2"),
           _dbg=False):
    x = np.ascontiguousarray(np.asarray(L0_attn_x, dtype=np.float32).reshape(B * S, D))
    wq = np.ascontiguousarray(np.asarray(L0_attn_qkv, dtype=np.float32).T)
    wo = np.ascontiguousarray(np.asarray(L0_attn_wo, dtype=np.float32).T)
    wr = np.ascontiguousarray(np.asarray(L0_ffn_W_r, dtype=np.float32).T)
    wk = np.ascontiguousarray(np.asarray(L0_ffn_W_k, dtype=np.float32).T)
    wv = np.ascontiguousarray(np.asarray(L0_ffn_W_v, dtype=np.float32).T)
    mur_ = np.ascontiguousarray(np.asarray(L0_ffn_mu_r, dtype=np.float32))
    muk_ = np.ascontiguousarray(np.asarray(L0_ffn_mu_k, dtype=np.float32))

    key = (_phases, _dbg)
    if key not in _NC_CACHE:
        _NC_CACHE[key] = build_program(phases=_phases, dbg=_dbg)
    nc = _NC_CACHE[key]

    in_maps = []
    for c in range(NCORE):
        shard = x[c * SEQ * S:(c + 1) * SEQ * S]
        in_maps.append({"x_in": shard, "wq_t": wq, "wo_t": wo, "wr_t": wr,
                        "wk_t": wk, "wv_t": wv, "mu_r": mur_, "mu_k": muk_})
    res = run_bass_kernel_spmd(nc, in_maps, core_ids=list(range(NCORE)))
    outs = [np.asarray(r["out"]) for r in res.results]
    full = np.concatenate(outs, axis=0).reshape(B, S, D)
    if _dbg:
        return full, res.results
    return full


# version 06
# speedup vs baseline: 1.5542x; 1.0442x over previous
import contextlib
import numpy as np
import concourse.bass as bass
import concourse.mybir as mybir
from concourse.bass_utils import run_bass_kernel_spmd

F32 = mybir.dt.float32
I32 = mybir.dt.int32
BF16 = mybir.dt.bfloat16
AF = mybir.ActivationFunctionType
ALU = mybir.AluOpType
AX = mybir.AxisListType

B, S, D = 32, 2048, 1024
H, DH = 16, 64
NCORE = 8
SEQ = B // NCORE
NCH = D // 128
NTT = S // 512
NT2 = S // 128

INV_2PI = float(1.0 / (2 * np.pi))
HALF_PI = float(np.pi / 2)
T2PI_HI = float(np.float32(round((2 * np.pi) * 2048.0) / 2048.0))
T2PI_LO = float(np.float32(2 * np.pi - T2PI_HI))
EPS = 1e-5

ENGINES = ("sync", "gpsimd", "tensor", "vector", "scalar")


class Slot:
    __slots__ = ("w", "r")

    def __init__(self):
        self.w = None
        self.r = []


class Sched:
    def __init__(self):
        self.ops = {e: [] for e in ENGINES}
        self.cnt = {}
        self.hiwater = {e: {} for e in ENGINES}

    def op(self, eng, fn, waits=(), inc=None, incn=1):
        hw = self.hiwater[eng]
        pruned = {}
        for t in waits:
            if t is None:
                continue
            s, v = t
            if hw.get(s, -1) >= v or pruned.get(s, -1) >= v:
                continue
            pruned[s] = v
        for s, v in pruned.items():
            hw[s] = v
        tok = None
        if inc is not None:
            self.cnt[inc] = self.cnt.get(inc, 0) + incn
            tok = (inc, self.cnt[inc])
        self.ops[eng].append((fn, tuple(pruned.items()), inc, incn))
        return tok

    def replay(self, eng_name, eng, sems):
        for fn, waits, inc, incn in self.ops[eng_name]:
            for s, v in waits:
                eng.wait_ge(sems[s], v)
            instr = fn(eng)
            if inc is not None:
                if instr is None:
                    eng.sem_inc(sems[inc], incn)
                else:
                    instr.then_inc(sems[inc], incn)


def build_program(phases=("ffn1", "attn", "ffn2"), dbg=False):
    nc = bass.Bass("TRN2", target_bir_lowering=False)

    x_in = nc.dram_tensor("x_in", [SEQ * S, D], F32, kind="ExternalInput").ap()
    wq_t = nc.dram_tensor("wq_t", [D, 3 * D], F32, kind="ExternalInput").ap()
    wo_t = nc.dram_tensor("wo_t", [D, D], F32, kind="ExternalInput").ap()
    wr_t = nc.dram_tensor("wr_t", [D, D], F32, kind="ExternalInput").ap()
    wk_t = nc.dram_tensor("wk_t", [D, D], F32, kind="ExternalInput").ap()
    wv_t = nc.dram_tensor("wv_t", [D, D], F32, kind="ExternalInput").ap()
    mu_r = nc.dram_tensor("mu_r", [D], F32, kind="ExternalInput").ap()
    mu_k = nc.dram_tensor("mu_k", [D], F32, kind="ExternalInput").ap()
    out = nc.dram_tensor("out", [SEQ * S, D], F32, kind="ExternalOutput").ap()

    wsp = {}
    for nm, dout in [("wq", 3 * D), ("wo", D), ("wr", D), ("wk", D), ("wv", D)]:
        for h in ("h", "l"):
            wsp[nm + h] = nc.dram_tensor(f"wsp_{nm}{h}", [dout // 128, 128, 8 * 128], BF16).ap()
    kw = dict(kind="ExternalOutput") if dbg else {}
    q_sp = {h: nc.dram_tensor(f"q_sp{h}", [D, S], BF16, **kw).ap() for h in "hl"}
    k_sp = {h: nc.dram_tensor(f"k_sp{h}", [D, S], BF16, **kw).ap() for h in "hl"}
    v_sp = {h: nc.dram_tensor(f"v_sp{h}", [S, D], BF16, **kw).ap() for h in "hl"}
    dbg_crep = nc.dram_tensor("dbg_crep", [128, H], F32, kind="ExternalOutput").ap() if dbg else None
    dbg_ka = nc.dram_tensor("dbg_ka", [128, 4096], BF16, kind="ExternalOutput").ap() if dbg else None
    dbg_kb = nc.dram_tensor("dbg_kb", [128, 4096], BF16, kind="ExternalOutput").ap() if dbg else None

    sched = Sched()
    E = sched.op
    stack = contextlib.ExitStack()
    slots = {}

    def sb(name, shape, dtype):
        t = stack.enter_context(nc.sbuf_tensor(name, shape, dtype))
        slots[name] = Slot()
        return t

    ident32 = sb("ident32", [128, 128], F32)
    identb = sb("identb", [128, 128], BF16)
    onesf = sb("onesf", [1, 128], F32)
    rt_b = sb("rt_b", [128, 128], BF16)
    ones_b = sb("ones_b", [128, 128], BF16)
    blk_b = sb("blk_b", [128, 2], BF16)
    cosr = sb("cosr", [128, S], F32)
    sinr = sb("sinr", [128, S], F32)
    mur = sb("mur", [128, NCH], F32)
    muk = sb("muk", [128, NCH], F32)
    cEps = sb("cEps", [128, 1], F32)
    cHpi = sb("cHpi", [128, 1], F32)
    invf_c = sb("invf_c", [128, 1], F32)
    crep = sb("crep", [128, H], F32)
    bnd = sb("bnd", [128, NCH * NTT], F32)
    rowb = sb("rowb", [128, S], F32)
    rrep = sb("rrep", [128, S], F32)
    mrrep = sb("mrrep", [128, S], F32)
    yt = sb("yt", [128, NCH * S], F32)
    sa = sb("sa", [128, 4096], F32)
    sB = sb("sB", [128, 4104], F32)
    g1 = sb("g1", [128, 4096], BF16)
    g2 = sb("g2", [128, 4096], BF16)
    g3 = sb("g3", [128, 4096], BF16)
    g4 = sb("g4", [128, 4096], BF16)
    kbuf = sb("kbuf", [128, 2 * S], BF16)
    vbuf = sb("vbuf", [128, 2 * S], BF16)
    qbuf = sb("qbuf", [128, 1024], BF16)
    wbuf = sb("wbuf", [128, 4096], BF16)
    slots["wb0"] = Slot()
    slots["wb1"] = Slot()
    iost = sb("iost", [128, 1024], F32)
    rden = sb("rden", [128, 512], F32)
    nn_i = sb("nn_i", [128, 512], I32)


    pbt = []
    for i in range(7):
        t = stack.enter_context(nc.psum_tensor(f"pb{i}", [128, 512], F32))
        slots[f"pb{i}"] = Slot()
        pbt.append(t)
    pbb = stack.enter_context(nc.psum_tensor("pbb", [128, 1024], BF16))
    slots["pbb"] = Slot()

    def pbank(bk):
        return pbt[int(bk[2])]

    SEMNAMES = ["LD", "LW", "ST", "TE", "VE", "SC", "GP"]

    def SW(name):
        s = slots[name]
        return list(s.r) + ([s.w] if s.w else [])

    def SR(name):
        s = slots[name]
        return [s.w] if s.w else []

    def setw(name, tok):
        s = slots[name]
        s.w = tok
        s.r = []

    def setr(name, tok):
        if tok is not None:
            slots[name].r.append(tok)

    CONST_NAMES = ["ident32", "identb", "onesf", "rt_b", "ones_b",
                   "blk_b", "cEps", "cHpi", "invf_c"]

    yts_ = [Slot() for _ in range(NTT)]

    def SRyt(tt):
        return [yts_[tt].w] if yts_[tt].w else []

    def SWyt(tt):
        return list(yts_[tt].r) + SRyt(tt)

    def setw_yt(tt, tok):
        yts_[tt].w = tok
        yts_[tt].r = []

    def setr_yt(tt, tok):
        if tok is not None:
            yts_[tt].r.append(tok)

    def SRyt_all():
        out = []
        for tt in range(NTT):
            out += SRyt(tt)
        return out

    def SWyt_all():
        out = []
        for tt in range(NTT):
            out += SWyt(tt)
        return out

    def emit_consts():
        g = "gpsimd"
        E(g, lambda e: e.memset(cEps[:], EPS))
        E(g, lambda e: e.memset(cHpi[:], HALF_PI))
        E(g, lambda e: e.memset(onesf[:], 1.0))
        E(g, lambda e: e.memset(ones_b[:], 1.0))
        E(g, lambda e: e.memset(blk_b[:], 0.0))
        E(g, lambda e: e.memset(blk_b[0:64, 0:1], 1.0))
        E(g, lambda e: e.memset(blk_b[64:128, 1:2], 1.0))
        for t, _ in ((ident32, F32), (identb, BF16)):
            E(g, lambda e, t=t: e.memset(t[:], 1.0))
            E(g, lambda e, t=t: e.affine_select(out=t[:], in_=t[:],
                                                pattern=[[1, 128]], compare_op=ALU.is_equal,
                                                fill=0.0, base=0, channel_multiplier=-1))
        E(g, lambda e: e.memset(rt_b[:], -1.0))
        E(g, lambda e: e.affine_select(out=rt_b[:], in_=rt_b[:],
                                       pattern=[[1, 128]], compare_op=ALU.is_equal,
                                       fill=0.0, base=32, channel_multiplier=-1))
        E(g, lambda e: e.affine_select(out=rt_b[:], in_=rt_b[:],
                                       pattern=[[0, 2], [-1, 64]], compare_op=ALU.is_ge,
                                       fill=0.0, base=31, channel_multiplier=0))
        E(g, lambda e: e.memset(sa[:, 0:128], 1.0))
        E(g, lambda e: e.affine_select(out=sa[:, 0:128], in_=sa[:, 0:128],
                                       pattern=[[1, 128]], compare_op=ALU.is_equal,
                                       fill=0.0, base=-32, channel_multiplier=-1))
        tgp = E(g, lambda e: e.affine_select(out=sa[:, 0:128], in_=sa[:, 0:128],
                                             pattern=[[0, 2], [1, 64]], compare_op=ALU.is_ge,
                                             fill=0.0, base=-32, channel_multiplier=0),
                inc="GP")
        tve = E("vector", lambda e: e.tensor_tensor(out=rt_b[:], in0=rt_b[:],
                                                    in1=sa[:, 0:128], op=ALU.add),
                waits=[tgp], inc="VE")
        setw("sa", tve)

        t1 = E("sync", lambda e: e.dma_start(out=mur[:], in_=mu_r.rearrange("(c p) -> p c", p=128),
                                             allow_slow_non_contiguous=True),
               inc="LD", incn=16)
        t2 = E("sync", lambda e: e.dma_start(out=muk[:], in_=mu_k.rearrange("(c p) -> p c", p=128),
                                             allow_slow_non_contiguous=True),
               inc="LD", incn=16)
        setw("mur", t1)
        setw("muk", t2)

        tio = E(g, lambda e: e.iota(out=sinr[:, :], pattern=[[1, S]], base=0,
                                    channel_multiplier=0,
                                    allow_small_or_imprecise_dtypes=True), inc="GP")
        for j in range(128):
            val = float(np.float32(10000.0 ** (-(j % 32) / 32.0)))
            E(g, lambda e, j=j, val=val: e.memset(rowb[0:1, j:j + 1], val))
        tgp2 = E(g, lambda e: e.engine_nop(), inc="GP")
        ttr0 = E("tensor", lambda e: e.transpose(pbt[6][:, 0:1], rowb[0:1, 0:128], onesf[0:1, 0:1]),
                 waits=[tgp2] + SW("pb6"), inc="TE")
        setw("pb6", ttr0)
        tiv = E("vector", lambda e: e.tensor_copy(out=invf_c[:], in_=pbt[6][:, 0:1]),
                waits=SR("pb6"), inc="VE")
        setr("pb6", tiv)
        setw("rowb", tgp2)
        tv1 = E("vector", lambda e: e.tensor_scalar(out=sinr[:], in0=sinr[:],
                                                    scalar1=invf_c[:], scalar2=None,
                                                    op0=ALU.mult),
                waits=[tio, tiv], inc="VE")
        tlast = tv1
        for j in range(S // 512):
            c0 = j * 512
            ta = E("vector", lambda e, c0=c0: e.tensor_scalar(out=nn_i[:], in0=sinr[:, c0:c0 + 512],
                                                              scalar1=INV_2PI, scalar2=None,
                                                              op0=ALU.mult),
                   waits=[tlast], inc="VE")
            tb = E("vector", lambda e, c0=c0: e.tensor_copy(out=sB[:, c0:c0 + 512], in_=nn_i[:]),
                   waits=[ta], inc="VE")
            tlast = tb
        tv4 = E("vector", lambda e: e.scalar_tensor_tensor(out=sinr[:], in0=sB[:, 0:S],
                                                           scalar=-T2PI_HI, in1=sinr[:],
                                                           op0=ALU.mult, op1=ALU.add),
                waits=[tlast], inc="VE")
        tv5 = E("vector", lambda e: e.scalar_tensor_tensor(out=sinr[:], in0=sB[:, 0:S],
                                                           scalar=-T2PI_LO, in1=sinr[:],
                                                           op0=ALU.mult, op1=ALU.add),
                waits=[tv4], inc="VE")
        ts1 = E("scalar", lambda e: e.activation(out=sB[:, 0:S], in_=sinr[:], func=AF.Abs),
                waits=[tv5], inc="SC")
        ts2 = E("scalar", lambda e: e.activation(out=cosr[:], in_=sB[:, 0:S], func=AF.Sin,
                                                 scale=-1.0, bias=cHpi[:]),
                waits=[ts1], inc="SC")
        ts3 = E("scalar", lambda e: e.activation(out=sinr[:], in_=sinr[:], func=AF.Sin),
                waits=[ts2], inc="SC")
        setw("cosr", ts2)
        setw("sinr", ts3)
        setw("sB", ts1)
        tcb = E(g, lambda e: e.engine_nop(), inc="GP")
        for nm in CONST_NAMES:
            slots[nm].w = tcb

    def emit_wsplit():
        srcs = [(wq_t, "wq", 3 * D), (wo_t, "wo", D), (wr_t, "wr", D),
                (wk_t, "wk", D), (wv_t, "wv", D)]
        for src, nm, dout in srcs:
            for kc in range(NCH):
                tld = E("sync", lambda e, src=src, kc=kc, dout=dout:
                        e.dma_start(out=sa[:, 0:dout], in_=src[kc * 128:(kc + 1) * 128, :]),
                        waits=SW("sa"), inc="LD", incn=16)
                setw("sa", tld)
                th = E("vector", lambda e, dout=dout: e.tensor_copy(out=g1[:, 0:dout], in_=sa[:, 0:dout]),
                       waits=SR("sa") + SW("g1"), inc="VE")
                setw("g1", th)
                tl = E("vector", lambda e, dout=dout: e.tensor_tensor(out=g2[:, 0:dout], in0=sa[:, 0:dout],
                                                                      in1=g1[:, 0:dout], op=ALU.subtract),
                       waits=SW("g2"), inc="VE")
                setw("g2", tl)
                setr("sa", tl)
                noc = dout // 128
                hf = noc // 2
                for g_t, gn, suf in ((g1, "g1", "h"), (g2, "g2", "l")):
                    for lo, hi in ((0, hf), (hf, noc)):
                        st_ = E("gpsimd", lambda e, nm=nm, kc=kc, suf=suf, lo=lo, hi=hi, g_t=g_t:
                                e.dma_start(
                                    out=wsp[nm + suf][lo:hi, :, kc * 128:(kc + 1) * 128]
                                    .rearrange("oc p d -> p oc d"),
                                    in_=g_t[:, lo * 128:hi * 128]
                                    .rearrange("p (oc d) -> p oc d", d=128)),
                                waits=SR(gn), inc="ST", incn=16)
                        setr(gn, st_)

    def emit_load_x(s):
        ytw = SWyt_all()
        for t2 in range(NT2):
            tld = E("sync", lambda e, s=s, t2=t2:
                    e.dma_start(out=iost[:], in_=x_in[s * S + t2 * 128: s * S + (t2 + 1) * 128, :]),
                    waits=SW("iost"), inc="LD", incn=16)
            setw("iost", tld)
            for c in range(NCH):
                bk = f"pb{5 + (c % 2)}"
                tmm = E("tensor", lambda e, c=c, bk=bk:
                        e.transpose(pbank(bk)[:, 0:128], iost[:, c * 128:(c + 1) * 128], ident32[:]),
                        waits=SR("iost") + SW(bk) + SR("ident32"), inc="TE")
                setw(bk, tmm)
                setr("iost", tmm)
                tev = E("scalar", lambda e, c=c, t2=t2, bk=bk:
                        e.copy(out=yt[:, c * S + t2 * 128: c * S + (t2 + 1) * 128], in_=pbank(bk)[:, 0:128]),
                        waits=SR(bk) + (ytw if (t2 == 0 and c == 0) else []), inc="SC")
                setr(bk, tev)
        for tt in range(NTT):
            setw_yt(tt, tev)

    def emit_store(s):
        laststore = None
        for t2 in range(NT2):
            evs = []
            for c in range(NCH):
                bk = f"pb{5 + (c % 2)}"
                tmm = E("tensor", lambda e, c=c, t2=t2, bk=bk:
                        e.transpose(pbank(bk)[:, 0:128],
                                    yt[:, c * S + t2 * 128: c * S + (t2 + 1) * 128], ident32[:]),
                        waits=SRyt(t2 // 4) + SW(bk), inc="TE")
                setw(bk, tmm)
                setr_yt(t2 // 4, tmm)
                tev = E("scalar", lambda e, c=c, bk=bk:
                        e.copy(out=iost[:, c * 128:(c + 1) * 128], in_=pbank(bk)[:, 0:128]),
                        waits=SR(bk) + (SW("iost") if c == 0 else []), inc="SC")
                setr(bk, tev)
                evs.append(tev)
            slots["iost"].w = evs[-1]
            slots["iost"].r = []
            laststore = E("gpsimd", lambda e, s=s, t2=t2:
                          e.dma_start(out=out[s * S + t2 * 128: s * S + (t2 + 1) * 128, :], in_=iost[:]),
                          waits=evs, inc="ST", incn=16)
            setr("iost", laststore)
        return laststore

    def emit_ln(s):
        for tt in range(NTT):
            ty = tq2 = None
            for c in range(NCH):
                w0 = c * S + tt * 512
                th = E("scalar", lambda e, w0=w0: e.copy(out=g1[:, 0:512], in_=yt[:, w0:w0 + 512]),
                       waits=SRyt(tt) + SW("g1"), inc="SC")
                setw("g1", th)
                tl = E("vector", lambda e, w0=w0: e.tensor_tensor(out=g2[:, 0:512], in0=yt[:, w0:w0 + 512],
                                                                  in1=g1[:, 0:512], op=ALU.subtract),
                       waits=SW("g2") + [th] + SRyt(tt), inc="VE")
                setw("g2", tl)
                setr_yt(tt, tl)
                tqs = E("scalar", lambda e, w0=w0: e.activation(out=sB[:, 0:512], in_=yt[:, w0:w0 + 512],
                                                                func=AF.Square),
                        waits=SRyt(tt) + SW("sB"), inc="SC")
                setw("sB", tqs)
                setr_yt(tt, th)
                setr_yt(tt, tqs)
                th2 = E("scalar", lambda e: e.copy(out=g3[:, 0:512], in_=sB[:, 0:512]),
                        waits=SR("sB") + SW("g3"), inc="SC")
                setw("g3", th2)
                tl2 = E("vector", lambda e: e.tensor_tensor(out=g4[:, 0:512], in0=sB[:, 0:512],
                                                            in1=g3[:, 0:512], op=ALU.subtract),
                        waits=SW("g4") + [th2] + SR("sB"), inc="VE")
                setw("g4", tl2)
                setr("sB", tl2)
                first = (c == 0)
                stop = (c == NCH - 1)
                E("tensor", lambda e, first=first: e.matmul(pbt[6][0:1, :], ones_b[:, 0:1], g1[:, 0:512],
                                                            start=first, stop=False),
                  waits=SR("g1") + SR("g2") + SR("ones_b") + (SW("pb6") if first else []))
                ty = E("tensor", lambda e, stop=stop: e.matmul(pbt[6][0:1, :], ones_b[:, 0:1], g2[:, 0:512],
                                                               start=False, stop=stop), inc="TE")
                setr("g1", ty)
                setr("g2", ty)
                E("tensor", lambda e, first=first: e.matmul(pbt[5][0:1, :], ones_b[:, 0:1], g3[:, 0:512],
                                                            start=first, stop=False),
                  waits=SR("g3") + SR("g4") + (SW("pb5") if first else []))
                tq2 = E("tensor", lambda e, stop=stop: e.matmul(pbt[5][0:1, :], ones_b[:, 0:1], g4[:, 0:512],
                                                                start=False, stop=stop), inc="TE")
                setr("g3", tq2)
                setr("g4", tq2)
            setw("pb6", ty)
            setw("pb5", tq2)
            e1 = E("vector", lambda e, tt=tt: e.tensor_copy(out=rowb[0:1, tt * 512:(tt + 1) * 512],
                                                            in_=pbt[6][0:1, :]),
                   waits=SR("pb6") + (SW("rowb") if tt == 0 else []), inc="VE")
            setr("pb6", e1)
            e2 = E("vector", lambda e, tt=tt: e.tensor_copy(out=sa[0:1, tt * 512:(tt + 1) * 512],
                                                            in_=pbt[5][0:1, :]),
                   waits=SR("pb5") + (SW("sa") if tt == 0 else []), inc="VE")
            setr("pb5", e2)
            if tt == 0:
                slots["rowb"].w = e2
                slots["rowb"].r = []
                slots["sa"].w = e2
                slots["sa"].r = []
        slots["rowb"].w = e2
        slots["sa"].w = e2
        tv = E("vector", lambda e: e.tensor_scalar(out=sa[0:1, 2048:2048 + S], in0=rowb[0:1, 0:S],
                                                   scalar1=1.0 / D, scalar2=None, op0=ALU.mult),
               inc="VE")
        tv2 = E("vector", lambda e: e.tensor_tensor(out=rowb[0:1, 0:S], in0=sa[0:1, 2048:2048 + S],
                                                    in1=sa[0:1, 2048:2048 + S], op=ALU.mult),
                waits=[tv], inc="VE")
        tv3 = E("vector", lambda e: e.tensor_scalar(out=sa[0:1, 0:S], in0=sa[0:1, 0:S],
                                                    scalar1=1.0 / D, scalar2=None, op0=ALU.mult),
                waits=[tv2], inc="VE")
        tv4 = E("vector", lambda e: e.tensor_tensor(out=sa[0:1, 0:S], in0=sa[0:1, 0:S],
                                                    in1=rowb[0:1, 0:S], op=ALU.subtract),
                waits=[tv3], inc="VE")
        ts = E("scalar", lambda e: e.activation(out=sa[0:1, 0:S], in_=sa[0:1, 0:S],
                                                func=AF.Sqrt, bias=cEps[0:1, :]),
               waits=[tv4], inc="SC")
        tv5 = E("vector", lambda e: e.reciprocal(out=rowb[0:1, 0:S], in_=sa[0:1, 0:S]),
                waits=[ts], inc="VE")
        tv6 = E("vector", lambda e: e.tensor_tensor(out=sa[0:1, 2048:2048 + S],
                                                    in0=sa[0:1, 2048:2048 + S],
                                                    in1=rowb[0:1, 0:S], op=ALU.mult),
                waits=[tv5], inc="VE")
        slots["rowb"].w = tv6
        slots["sa"].w = tv6
        te2 = None
        for tt in range(NTT):
            bk = f"pb{5 + (tt % 2)}"
            tm = E("tensor", lambda e, tt=tt, bk=bk:
                   e.matmul(pbank(bk)[:, 0:512], onesf[0:1, :], rowb[0:1, tt * 512:(tt + 1) * 512],
                            start=True, stop=True),
                   waits=SR("rowb") + SW(bk) + SR("onesf"), inc="TE")
            setw(bk, tm)
            setr("rowb", tm)
            te1 = E("scalar", lambda e, tt=tt, bk=bk: e.copy(out=rrep[:, tt * 512:(tt + 1) * 512],
                                                             in_=pbank(bk)[:, 0:512]),
                    waits=SR(bk) + (SW("rrep") if tt == 0 else []), inc="SC")
            setr(bk, te1)
            if tt == 0:
                slots["rrep"].w = te1
                slots["rrep"].r = []
            tm2 = E("tensor", lambda e, tt=tt, bk=bk:
                    e.matmul(pbank(bk)[:, 0:512], onesf[0:1, :],
                             sa[0:1, 2048 + tt * 512:2048 + (tt + 1) * 512], start=True, stop=True),
                    waits=SR("sa") + [te1] + SR("onesf"), inc="TE")
            setw(bk, tm2)
            setr("sa", tm2)
            te2 = E("scalar", lambda e, tt=tt, bk=bk: e.copy(out=mrrep[:, tt * 512:(tt + 1) * 512],
                                                             in_=pbank(bk)[:, 0:512]),
                    waits=SR(bk) + (SW("mrrep") if tt == 0 else []), inc="SC")
            setr(bk, te2)
            if tt == 0:
                slots["mrrep"].w = te2
                slots["mrrep"].r = []
        slots["rrep"].w = te2
        slots["mrrep"].w = te2

    def mm3(bk, wsl_h, wsl_l, xh_t, xl_t, xsl, extra_waits):
        tok = None
        for kc in range(NCH):
            wh_ = wsl_h(kc)
            wl_ = wsl_l(kc)
            xh_ = xsl(xh_t, kc)
            xl_ = xsl(xl_t, kc)
            w = (extra_waits + SW(bk)) if kc == 0 else []
            E("tensor", lambda e, wh_=wh_, xh_=xh_, kc=kc, bk=bk:
              e.matmul(pbank(bk)[:], wh_, xh_, start=(kc == 0), stop=False), waits=w)
            E("tensor", lambda e, wl_=wl_, xh_=xh_, bk=bk:
              e.matmul(pbank(bk)[:], wl_, xh_, start=False, stop=False))
            tok = E("tensor", lambda e, wh_=wh_, xl_=xl_, kc=kc, bk=bk:
                    e.matmul(pbank(bk)[:], wh_, xl_, start=False, stop=(kc == NCH - 1)),
                    inc=("TE" if kc == NCH - 1 else None))
        setw(bk, tok)
        return tok

    def wload(src_h, src_l, oc, slot):
        bname = f"wb{slot}"
        b0 = slot * 2048
        t1 = E("sync", lambda e, oc=oc, src_h=src_h, b0=b0: e.dma_start(
            out=wbuf[:, b0:b0 + 1024], in_=src_h[oc]),
            waits=SW(bname), inc="LW", incn=16)
        t2 = E("sync", lambda e, oc=oc, src_l=src_l, b0=b0: e.dma_start(
            out=wbuf[:, b0 + 1024:b0 + 2048], in_=src_l[oc]),
            inc="LW", incn=16)
        setw(bname, t2)
        return [t1, t2], b0

    def wchunks(b0):
        return (lambda kc: wbuf[:, b0 + kc * 128: b0 + (kc + 1) * 128],
                lambda kc: wbuf[:, b0 + 1024 + kc * 128: b0 + 1024 + (kc + 1) * 128])

    def emit_ffn(s):
        PAIR = [(g1, "g1", g2, "g2"), (g3, "g3", g4, "g4")]
        t = None
        for c in range(NCH):
            for j in range(1, NTT):
                col = c * S + j * 512 - 1
                t = E("vector", lambda e, c=c, j=j, col=col:
                      e.tensor_copy(out=bnd[:, (c * NTT + j):(c * NTT + j) + 1], in_=yt[:, col:col + 1]),
                      waits=SRyt(j - 1) + (SW("bnd") if (c == 0 and j == 1) else []), inc="VE")
                setr_yt(j - 1, t)
        setw("bnd", t)

        def emit_xln_d(tt, c):
            r0 = tt * 512
            w0 = c * S + tt * 512
            base = c * 513
            if tt == 0:
                bq = E("vector", lambda e, base=base: e.memset(sB[:, base:base + 1], 0.0),
                       waits=SW("sB") if c == 0 else [], inc="VE")
            else:
                bq0 = E("vector", lambda e, base=base, c=c, tt=tt, r0=r0:
                        e.tensor_tensor(out=sB[:, base:base + 1],
                                        in0=bnd[:, c * NTT + tt:c * NTT + tt + 1],
                                        in1=rrep[:, r0 - 1:r0], op=ALU.mult),
                        waits=SR("bnd") + SR("rrep") + (SW("sB") if c == 0 else []), inc="VE")
                bq = E("vector", lambda e, base=base, r0=r0:
                       e.tensor_tensor(out=sB[:, base:base + 1], in0=sB[:, base:base + 1],
                                       in1=mrrep[:, r0 - 1:r0], op=ALU.subtract),
                       waits=SR("mrrep") + [bq0], inc="VE")
            t1 = E("vector", lambda e, base=base, w0=w0, r0=r0:
                   e.tensor_tensor(out=sB[:, base + 1: base + 513], in0=yt[:, w0:w0 + 512],
                                   in1=rrep[:, r0:r0 + 512], op=ALU.mult),
                   waits=SRyt(tt) + SR("rrep"), inc="VE")
            setr_yt(tt, t1)
            tsub = E("vector", lambda e, base=base, r0=r0:
                     e.tensor_tensor(out=sB[:, base + 1: base + 513], in0=sB[:, base + 1: base + 513],
                                     in1=mrrep[:, r0:r0 + 512], op=ALU.subtract),
                     waits=SR("mrrep"), inc="VE")
            slots["sB"].w = tsub
            td = E("vector", lambda e, base=base:
                   e.tensor_tensor(out=rden[:, 0:512], in0=sB[:, base + 1: base + 513],
                                   in1=sB[:, base: base + 512], op=ALU.subtract),
                   waits=[bq] + SW("rden"), inc="VE")
            setw("rden", td)
            return td

        def emit_d(c):
            base = c * 513
            td = E("vector", lambda e, base=base:
                   e.tensor_tensor(out=rden[:, 0:512], in0=sB[:, base + 1: base + 513],
                                   in1=sB[:, base: base + 512], op=ALU.subtract),
                   waits=SW("rden"), inc="VE")
            setw("rden", td)
            return td

        def emit_mix(tt, c, td, mu_t, mu_n, hi_t, hi_n, lo_t, lo_n, ioff):
            base = c * 513
            t4 = E("vector", lambda e, c=c, base=base:
                   e.scalar_tensor_tensor(out=hi_t[:, c * 512:(c + 1) * 512],
                                          in0=rden[:, 0:512], scalar=mu_t[:, c:c + 1],
                                          in1=sB[:, base: base + 512],
                                          op0=ALU.mult, op1=ALU.add),
                   waits=[td] + SR(mu_n) + SW(hi_n), inc="VE")
            if c == 0:
                slots[hi_n].w = t4
                slots[hi_n].r = []
            t6 = E("vector", lambda e, c=c, base=base, ioff=ioff:
                   e.scalar_tensor_tensor(out=iost[:, ioff:ioff + 512],
                                          in0=rden[:, 0:512], scalar=mu_t[:, c:c + 1],
                                          in1=sB[:, base: base + 512],
                                          op0=ALU.mult, op1=ALU.add),
                   waits=SW("iost"), inc="VE")
            slots["iost"].w = t6
            t7 = E("vector", lambda e, c=c, ioff=ioff:
                   e.tensor_tensor(out=lo_t[:, c * 512:(c + 1) * 512], in0=iost[:, ioff:ioff + 512],
                                   in1=hi_t[:, c * 512:(c + 1) * 512], op=ALU.subtract),
                   waits=SW(lo_n), inc="VE")
            if c == 0:
                slots[lo_n].w = t7
                slots[lo_n].r = []
            setr("iost", t7)
            setr("sB", t7)
            slots[hi_n].w = t7
            slots[lo_n].w = t7
            return t7

        def emit_mix_chunk(tt, c, which):
            ph = tt % 2
            if which in ("rw",):
                td = emit_xln_d(tt, c)
            else:
                td = emit_d(c)
            if which in ("r", "rw"):
                hi_t, hi_n, lo_t, lo_n = PAIR[ph]
                return emit_mix(tt, c, td, mur, "mur", hi_t, hi_n, lo_t, lo_n, 0)
            else:
                hi_t, hi_n, lo_t, lo_n = PAIR[1 - ph]
                return emit_mix(tt, c, td, muk, "muk", hi_t, hi_n, lo_t, lo_n, 512)

        for c in range(NCH):
            emit_mix_chunk(0, c, "rw")
        for c in range(NCH):
            emit_mix_chunk(0, c, "k")

        for tt in range(NTT):
            ph = tt % 2
            xr_t, xr_n, xr_lo_t, xr_lo_n = PAIR[ph]
            xk_t, xk_n, xk_lo_t, xk_lo_n = PAIR[1 - ph]
            ter = None
            for oc in range(NCH):
                wts, b0 = wload(wsp["wrh"], wsp["wrl"], oc, oc % 2)
                wh_f, wl_f = wchunks(b0)
                bk = f"pb{0 + (oc % 2)}"
                tok = mm3(bk, wh_f, wl_f,
                          xr_t, xr_lo_t, lambda t_, kc: t_[:, kc * 512:(kc + 1) * 512],
                          SR(xr_n) + SR(xr_lo_n) + wts)
                setr(xr_n, tok)
                setr(xr_lo_n, tok)
                setr(f"wb{oc % 2}", tok)
                ter = E("scalar", lambda e, oc=oc, bk=bk:
                        e.activation(out=sa[:, oc * 512:(oc + 1) * 512], in_=pbank(bk)[:],
                                     func=AF.Sigmoid),
                        waits=SR(bk) + (SW("sa") if oc == 0 else []), inc="SC")
                setr(bk, ter)
                if oc == 0:
                    slots["sa"].w = ter
                    slots["sa"].r = []
            slots["sa"].w = ter
            tk4 = None
            for oc in range(NCH):
                wts, b0 = wload(wsp["wkh"], wsp["wkl"], oc, oc % 2)
                wh_f, wl_f = wchunks(b0)
                bk = f"pb{2 + (oc % 2)}"
                tok = mm3(bk, wh_f, wl_f,
                          xk_t, xk_lo_t, lambda t_, kc: t_[:, kc * 512:(kc + 1) * 512],
                          SR(xk_n) + SR(xk_lo_n) + wts)
                setr(xk_n, tok)
                setr(xk_lo_n, tok)
                setr(f"wb{oc % 2}", tok)
                tk1 = E("scalar", lambda e, oc=oc, bk=bk:
                        e.activation(out=sB[:, oc * 512:(oc + 1) * 512], in_=pbank(bk)[:],
                                     func=AF.Relu),
                        waits=SR(bk) + (SW("sB") if oc == 0 else []), inc="SC")
                setr(bk, tk1)
                if oc == 0:
                    slots["sB"].w = tk1
                    slots["sB"].r = []
                tk2 = E("scalar", lambda e, oc=oc:
                        e.activation(out=sB[:, oc * 512:(oc + 1) * 512],
                                     in_=sB[:, oc * 512:(oc + 1) * 512], func=AF.Square),
                        inc="SC")
                tk3 = E("scalar", lambda e, oc=oc, xr_t=xr_t: e.copy(out=xr_t[:, oc * 512:(oc + 1) * 512],
                                                                     in_=sB[:, oc * 512:(oc + 1) * 512]),
                        waits=[tk2] + (SW(xr_n) if oc == 0 else []), inc="SC")
                if oc == 0:
                    slots[xr_n].w = tk3
                    slots[xr_n].r = []
                tk4 = E("vector", lambda e, oc=oc, xr_t=xr_t, xr_lo_t=xr_lo_t:
                        e.tensor_tensor(out=xr_lo_t[:, oc * 512:(oc + 1) * 512],
                                        in0=sB[:, oc * 512:(oc + 1) * 512],
                                        in1=xr_t[:, oc * 512:(oc + 1) * 512],
                                        op=ALU.subtract),
                        waits=[tk3] + (SW(xr_lo_n) if oc == 0 else []), inc="VE")
                if oc == 0:
                    slots[xr_lo_n].w = tk4
                    slots[xr_lo_n].r = []
                slots["sB"].w = tk4
            slots[xr_n].w = tk4
            slots[xr_lo_n].w = tk4
            for oc in range(NCH):
                wts, b0 = wload(wsp["wvh"], wsp["wvl"], oc, oc % 2)
                wh_f, wl_f = wchunks(b0)
                bk = f"pb{4 + (oc % 2)}"
                tok = mm3(bk, wh_f, wl_f,
                          xr_t, xr_lo_t, lambda t_, kc: t_[:, kc * 512:(kc + 1) * 512],
                          SR(xr_n) + SR(xr_lo_n) + wts)
                setr(xr_n, tok)
                setr(xr_lo_n, tok)
                setr(f"wb{oc % 2}", tok)
                w0 = oc * S + tt * 512
                tv1 = E("vector", lambda e, oc=oc, bk=bk:
                        e.tensor_tensor(out=iost[:, 0:512], in0=pbank(bk)[:],
                                        in1=sa[:, oc * 512:(oc + 1) * 512], op=ALU.mult),
                        waits=SR(bk) + SR("sa") + SW("iost"), inc="VE")
                setr(bk, tv1)
                setr("sa", tv1)
                slots["iost"].w = tv1
                slots["iost"].r = []
                tv2 = E("vector", lambda e, w0=w0:
                        e.scalar_tensor_tensor(out=yt[:, w0:w0 + 512], in0=iost[:, 0:512],
                                               scalar=0.5, in1=yt[:, w0:w0 + 512],
                                               op0=ALU.mult, op1=ALU.add),
                        waits=SWyt(tt), inc="VE")
                slots["iost"].r.append(tv2)
                setw_yt(tt, tv2)
                if tt + 1 < NTT:
                    emit_mix_chunk(tt + 1, oc, "rw")
            if tt + 1 < NTT:
                for c in range(NCH):
                    emit_mix_chunk(tt + 1, c, "k")

    pf_slots = {off: Slot() for off in (0, 512, 1024, 1536)}
    pp_slots = {off: Slot() for off in (0, 512, 1024, 1536)}

    def emit_attn(s):
        qn_toks = []
        for tt in range(NTT):
            r0 = tt * 512
            for c in range(NCH):
                w0 = c * S + tt * 512
                t1 = E("vector", lambda e, c=c, w0=w0, r0=r0:
                       e.tensor_tensor(out=sa[:, c * 512:(c + 1) * 512], in0=yt[:, w0:w0 + 512],
                                       in1=rrep[:, r0:r0 + 512], op=ALU.mult),
                       waits=SRyt(tt) + SR("rrep") + (SW("sa") if c == 0 else []), inc="VE")
                setr_yt(tt, t1)
                if c == 0:
                    slots["sa"].w = t1
                    slots["sa"].r = []
                E("vector", lambda e, c=c, r0=r0:
                  e.tensor_tensor(out=sa[:, c * 512:(c + 1) * 512], in0=sa[:, c * 512:(c + 1) * 512],
                                  in1=mrrep[:, r0:r0 + 512], op=ALU.subtract),
                  waits=SR("mrrep"), inc="VE")
                t3 = E("vector", lambda e, c=c: e.tensor_copy(out=g1[:, c * 512:(c + 1) * 512],
                                                              in_=sa[:, c * 512:(c + 1) * 512]),
                       waits=(SW("g1") if c == 0 else []), inc="VE")
                if c == 0:
                    slots["g1"].w = t3
                    slots["g1"].r = []
                t4 = E("vector", lambda e, c=c: e.tensor_tensor(out=g2[:, c * 512:(c + 1) * 512],
                                                                in0=sa[:, c * 512:(c + 1) * 512],
                                                                in1=g1[:, c * 512:(c + 1) * 512],
                                                                op=ALU.subtract),
                       waits=(SW("g2") if c == 0 else []), inc="VE")
                if c == 0:
                    slots["g2"].w = t4
                    slots["g2"].r = []
            slots["g1"].w = t4
            slots["g2"].w = t4
            slots["sa"].w = t4

            def _qkv_post(bk, hp, oc):
                if oc < 2 * NCH:
                    is_q = oc < NCH
                    col = hp * 512
                    e1 = E("scalar", lambda e, col=col, bk=bk:
                           e.copy(out=g3[:, col:col + 512], in_=pbank(bk)[:]),
                           waits=SR(bk) + list(slots["g3"].r) + ([slots["g3"].w] if slots["g3"].w else []),
                           inc="SC")
                    slots["g3"].w = e1
                    slots["g3"].r = []
                    e2 = E("vector", lambda e, col=col, bk=bk:
                           e.tensor_tensor(out=g4[:, col:col + 512], in0=pbank(bk)[:],
                                           in1=g3[:, col:col + 512], op=ALU.subtract),
                           waits=SR(bk) + [e1] + list(slots["g4"].r) + ([slots["g4"].w] if slots["g4"].w else []),
                           inc="VE")
                    slots["g4"].w = e2
                    slots["g4"].r = []
                    E("tensor", lambda e, col=col: e.matmul(pbt[6][:], rt_b[:], g3[:, col:col + 512],
                                                            start=True, stop=False),
                      waits=[e1] + SW("pb6") + SR("rt_b"))
                    tr = E("tensor", lambda e, col=col: e.matmul(pbt[6][:], rt_b[:], g4[:, col:col + 512],
                                                                 start=False, stop=True),
                           waits=[e2], inc="TE")
                    setw("pb6", tr)
                    tv1 = E("vector", lambda e, bk=bk, r0=r0:
                            e.tensor_tensor(out=sB[:, 0:512], in0=pbank(bk)[:],
                                            in1=cosr[:, r0:r0 + 512], op=ALU.mult),
                            waits=SR(bk) + SR("cosr") + SW("sB"), inc="VE")
                    slots["sB"].w = tv1
                    slots["sB"].r = []
                    setr(bk, tv1)
                    tv2 = E("vector", lambda e, r0=r0:
                            e.tensor_tensor(out=sB[:, 512:1024], in0=pbt[6][:],
                                            in1=sinr[:, r0:r0 + 512], op=ALU.mult),
                            waits=SR("pb6") + SR("sinr"), inc="VE")
                    setr("pb6", tv2)
                    E("vector", lambda e: e.tensor_tensor(out=sB[:, 0:512], in0=sB[:, 0:512],
                                                          in1=sB[:, 512:1024], op=ALU.add),
                      inc="VE")
                    tv4 = E("vector", lambda e, col=col: e.tensor_copy(out=g3[:, col:col + 512],
                                                                       in_=sB[:, 0:512]),
                            waits=[tr], inc="VE")
                    slots["g3"].w = tv4
                    tv5 = E("vector", lambda e, col=col: e.tensor_tensor(out=g4[:, col:col + 512],
                                                                         in0=sB[:, 0:512],
                                                                         in1=g3[:, col:col + 512],
                                                                         op=ALU.subtract),
                            inc="VE")
                    slots["g4"].w = tv5
                    slots["sB"].w = tv5
                    dst = q_sp if is_q else k_sp
                    s1 = E("gpsimd", lambda e, dst=dst, hp=hp, tt=tt, col=col:
                           e.dma_start(out=dst["h"][hp * 128:(hp + 1) * 128, tt * 512:(tt + 1) * 512],
                                       in_=g3[:, col:col + 512]),
                           waits=[tv4], inc="ST", incn=16)
                    s2 = E("gpsimd", lambda e, dst=dst, hp=hp, tt=tt, col=col:
                           e.dma_start(out=dst["l"][hp * 128:(hp + 1) * 128, tt * 512:(tt + 1) * 512],
                                       in_=g4[:, col:col + 512]),
                           waits=[tv5], inc="ST", incn=16)
                    setr("g3", s1)
                    setr("g4", s2)
                    tq1 = E("vector", lambda e, col=col:
                            e.tensor_tensor(out=vbuf[:, 0:512], in0=g3[:, col:col + 512],
                                            in1=g3[:, col:col + 512], op=ALU.mult),
                            waits=[tv4] + SW("vbuf"), inc="VE")
                    setw("vbuf", tq1)
                    setr("g3", tq1)
                    for j in range(2):
                        head = 2 * hp + j
                        tqm = E("tensor", lambda e, j=j: e.matmul(pbt[5][0:1, :], blk_b[:, j:j + 1],
                                                                  vbuf[:, 0:512],
                                                                  start=True, stop=True),
                                waits=[tq1] + SW("pb5") + SR("blk_b"), inc="TE")
                        setw("pb5", tqm)
                        setr("vbuf", tqm)
                        coloff = (0 if is_q else 64) + head * 4 + tt
                        trm = E("vector", lambda e, coloff=coloff:
                                e.tensor_reduce(out=rowb[0:1, coloff:coloff + 1],
                                                in_=pbt[5][0:1, :], axis=AX.X, op=ALU.max),
                                waits=SR("pb5") + (qn_toks[-1:] if qn_toks else SW("rowb")),
                                inc="VE")
                        setr("pb5", trm)
                        qn_toks.append(trm)
                else:
                    e1 = E("scalar", lambda e, bk=bk: e.copy(out=g3[:, 0:512], in_=pbank(bk)[:]),
                           waits=SR(bk) + list(slots["g3"].r) + ([slots["g3"].w] if slots["g3"].w else []),
                           inc="SC")
                    slots["g3"].w = e1
                    slots["g3"].r = []
                    e2 = E("vector", lambda e, bk=bk: e.tensor_tensor(out=g4[:, 0:512],
                                                                      in0=pbank(bk)[:],
                                                                      in1=g3[:, 0:512], op=ALU.subtract),
                           waits=SR(bk) + [e1] + list(slots["g4"].r) + ([slots["g4"].w] if slots["g4"].w else []),
                           inc="VE")
                    slots["g4"].w = e2
                    slots["g4"].r = []
                    setr(bk, e2)
                    for half, gt, gn, etok in (("h", g3, "g3", e1), ("l", g4, "g4", e2)):
                        for sub in range(4):
                            tmm = E("tensor", lambda e, gt=gt, sub=sub:
                                    e.transpose(pbb[:, 0:128], gt[:, sub * 128:(sub + 1) * 128], identb[:]),
                                    waits=[etok] + SW("pbb") + SR("identb"), inc="TE")
                            setw("pbb", tmm)
                            setr(gn, tmm)
                            tev = E("vector", lambda e: e.tensor_copy(out=vbuf[:, 512:640], in_=pbb[:, 0:128]),
                                    waits=SR("pbb") + SW("vbuf"), inc="VE")
                            setw("vbuf", tev)
                            setr("pbb", tev)
                            tst = E("gpsimd", lambda e, half=half, hp=hp, tt=tt, sub=sub:
                                    e.dma_start(out=v_sp[half][tt * 512 + sub * 128: tt * 512 + (sub + 1) * 128,
                                                               hp * 128:(hp + 1) * 128],
                                                in_=vbuf[:, 512:640]),
                                    waits=[tev], inc="ST", incn=16)
                            setr("vbuf", tst)

            pending = [None]
            for oc in range(3 * NCH):
                wts, b0 = wload(wsp["wqh"], wsp["wql"], oc, oc % 2)
                wh_f, wl_f = wchunks(b0)
                bk = f"pb{0 + (oc % 2)}"
                tok = mm3(bk, wh_f, wl_f,
                          g1, g2, lambda t_, kc: t_[:, kc * 512:(kc + 1) * 512],
                          SR("g1") + SR("g2") + wts)
                setr("g1", tok)
                setr("g2", tok)
                setr(f"wb{oc % 2}", tok)
                if pending[0] is not None:
                    pending[0]()
                hp = oc % NCH
                pending[0] = (lambda bk=bk, hp=hp, oc=oc: _qkv_post(bk, hp, oc))
            pending[0]()

        tprev = qn_toks[-1]
        for h in range(H):
            tprev = E("vector", lambda e, h=h: e.tensor_reduce(out=rowb[0:1, 128 + h:129 + h],
                                                               in_=rowb[0:1, h * 4:h * 4 + 4],
                                                               axis=AX.X, op=ALU.max),
                      waits=[tprev], inc="VE")
            tprev = E("vector", lambda e, h=h: e.tensor_reduce(out=rowb[0:1, 144 + h:145 + h],
                                                               in_=rowb[0:1, 64 + h * 4:64 + h * 4 + 4],
                                                               axis=AX.X, op=ALU.max),
                      waits=[tprev], inc="VE")
        tsq = E("scalar", lambda e: e.activation(out=rowb[0:1, 128:160], in_=rowb[0:1, 128:160],
                                                 func=AF.Sqrt),
                waits=[tprev], inc="SC")
        tvc = E("vector", lambda e: e.tensor_tensor(out=rowb[0:1, 160:176], in0=rowb[0:1, 128:144],
                                                    in1=rowb[0:1, 144:160], op=ALU.mult),
                waits=[tsq], inc="VE")
        tvc2 = E("vector", lambda e: e.tensor_scalar(out=rowb[0:1, 160:176], in0=rowb[0:1, 160:176],
                                                     scalar1=-1.01 * 0.125, scalar2=-0.01,
                                                     op0=ALU.mult, op1=ALU.add),
                 waits=[tvc], inc="VE")
        trep = E("tensor", lambda e: e.matmul(pbt[5][:, 0:16], onesf[0:1, :], rowb[0:1, 160:176],
                                              start=True, stop=True),
                 waits=[tvc2] + SW("pb5") + SR("onesf"), inc="TE")
        setw("pb5", trep)
        setr("rowb", trep)
        slots["rowb"].w = tvc2
        tcrep = E("scalar", lambda e: e.copy(out=crep[:], in_=pbt[5][:, 0:16]),
                  waits=SR("pb5"), inc="SC")
        setr("pb5", tcrep)
        setw("crep", tcrep)
        if dbg:
            E("gpsimd", lambda e: e.dma_start(out=dbg_crep[:, :], in_=crep[:]),
              waits=SR("crep"), inc="ST", incn=16)

        trans = SW("sB") + SW("g3") + SW("g4")
        for off in (0, 512, 1024, 1536):
            pf_slots[off].r = list(trans)
            pf_slots[off].w = None
            pp_slots[off].r = list(trans)
            pp_slots[off].w = None
        pf_ap = sB
        pph_ap = g3

        ta3 = None
        for tt in range(NTT):
            nkt = (tt + 1) * 4
            for hp in range(NCH):
                headA, headB = 2 * hp, 2 * hp + 1
                lk1 = E("sync", lambda e, hp=hp, nkt=nkt:
                        e.dma_start(out=kbuf[:, 0:nkt * 128],
                                    in_=k_sp["h"][hp * 128:(hp + 1) * 128, 0:nkt * 128]),
                        waits=SW("kbuf"), inc="LD", incn=16)
                lk2 = E("sync", lambda e, hp=hp, nkt=nkt:
                        e.dma_start(out=kbuf[:, S:S + nkt * 128],
                                    in_=k_sp["l"][hp * 128:(hp + 1) * 128, 0:nkt * 128]),
                        inc="LD", incn=16)
                setw("kbuf", lk2)
                lv1 = E("sync", lambda e, hp=hp, nkt=nkt:
                        e.dma_start(out=vbuf[:, 0:nkt * 128].rearrange("p (n c) -> p n c", c=128),
                                    in_=v_sp["h"][0:nkt * 128, hp * 128:(hp + 1) * 128]
                                    .rearrange("(n p) c -> p n c", p=128)),
                        waits=SW("vbuf"), inc="LD", incn=16)
                lv2 = E("sync", lambda e, hp=hp, nkt=nkt:
                        e.dma_start(out=vbuf[:, S:S + nkt * 128].rearrange("p (n c) -> p n c", c=128),
                                    in_=v_sp["l"][0:nkt * 128, hp * 128:(hp + 1) * 128]
                                    .rearrange("(n p) c -> p n c", p=128)),
                        inc="LD", incn=16)
                setw("vbuf", lv2)
                lq1 = E("sync", lambda e, hp=hp, tt=tt:
                        e.dma_start(out=qbuf[:, 0:512],
                                    in_=q_sp["h"][hp * 128:(hp + 1) * 128, tt * 512:(tt + 1) * 512]),
                        waits=SW("qbuf"), inc="LD", incn=16)
                lq2 = E("sync", lambda e, hp=hp, tt=tt:
                        e.dma_start(out=qbuf[:, 512:1024],
                                    in_=q_sp["l"][hp * 128:(hp + 1) * 128, tt * 512:(tt + 1) * 512]),
                        inc="LD", incn=16)
                setw("qbuf", lq2)
                last_sc = {}
                pv_tok_box = [None, None]

                def emit_scores(c):
                    ib = c % 2
                    bkA = f"pb{0 + ib * 2}"
                    bkB = f"pb{1 + ib * 2}"
                    kc0 = c * 128
                    for (bkX, p0) in ((bkA, 0), (bkB, 64)):
                        w0 = SW(bkX) + [lk1, lk2, lq1, lq2]
                        E("tensor", lambda e, bkX=bkX, p0=p0, kc0=kc0:
                          e.matmul(pbank(bkX)[:], kbuf[p0:p0 + 64, kc0:kc0 + 128],
                                   qbuf[p0:p0 + 64, 0:512], start=True, stop=False,
                                   tile_position=(p0, 0)),
                          waits=w0)
                        E("tensor", lambda e, bkX=bkX, p0=p0, kc0=kc0:
                          e.matmul(pbank(bkX)[:], kbuf[p0:p0 + 64, kc0:kc0 + 128],
                                   qbuf[p0:p0 + 64, 512:1024], start=False, stop=False,
                                   tile_position=(p0, 0)))
                        tsc = E("tensor", lambda e, bkX=bkX, p0=p0, kc0=kc0:
                                e.matmul(pbank(bkX)[:], kbuf[p0:p0 + 64, S + kc0:S + kc0 + 128],
                                         qbuf[p0:p0 + 64, 0:512], start=False, stop=True,
                                         tile_position=(p0, 0)), inc="TE")
                        setw(bkX, tsc)
                        last_sc[bkX] = tsc

                def emit_expsplit(c):
                    ib = c % 2
                    bkA = f"pb{0 + ib * 2}"
                    bkB = f"pb{1 + ib * 2}"
                    partial = (c >= tt * 4)
                    for (bkX, head, off0) in ((bkA, headA, 0), (bkB, headB, 1024)):
                        pcol = off0 + ib * 512
                        ps_ = pf_slots[pcol]
                        pp_ = pp_slots[pcol]
                        te = E("scalar", lambda e, bkX=bkX, head=head, pcol=pcol:
                               e.activation(out=pf_ap[:, pcol:pcol + 512], in_=pbank(bkX)[:],
                                            func=AF.Exp, scale=0.125, bias=crep[:, head:head + 1]),
                               waits=SR(bkX) + SR("crep") + list(ps_.r) + ([ps_.w] if ps_.w else []),
                               inc="SC")
                        setr(bkX, te)
                        ps_.w = te
                        ps_.r = []
                        if partial:
                            tm = E("gpsimd", lambda e, pcol=pcol, tt=tt, c=c:
                                   e.affine_select(out=pf_ap[:, pcol:pcol + 512],
                                                   in_=pf_ap[:, pcol:pcol + 512],
                                                   pattern=[[1, 512]], compare_op=ALU.is_ge,
                                                   fill=0.0, base=tt * 512 - c * 128,
                                                   channel_multiplier=-1),
                                   waits=[te], inc="GP")
                            ps_.w = tm
                        th = E("vector", lambda e, pcol=pcol:
                               e.tensor_copy(out=pph_ap[:, pcol:pcol + 512], in_=pf_ap[:, pcol:pcol + 512]),
                               waits=[ps_.w] + list(pp_.r) + ([pp_.w] if pp_.w else []), inc="VE")
                        ps_.r.append(th)
                        tl = E("vector", lambda e, pcol=pcol:
                               e.tensor_tensor(out=pph_ap[:, 2048 + pcol:2048 + pcol + 512],
                                               in0=pf_ap[:, pcol:pcol + 512],
                                               in1=pph_ap[:, pcol:pcol + 512], op=ALU.subtract),
                               inc="VE")
                        ps_.r.append(tl)
                        pp_.w = tl
                        pp_.r = []
                def emit_pvden(c):
                    ib = c % 2
                    kc0 = c * 128
                    stop = (c == nkt - 1)
                    first = (c == 0)
                    for (off0, p0) in ((0, 0), (1024, 64)):
                        pcol = off0 + ib * 512
                        pp_ = pp_slots[pcol]
                        colA = kc0 + (0 if p0 == 0 else 64)
                        ww = ([pp_.w] if pp_.w else []) + [lv1, lv2]
                        if first:
                            ww += SW("pb4") + SW("pb5")
                        E("tensor", lambda e, pcol=pcol, p0=p0, colA=colA, first=first:
                          e.matmul(pbt[4][p0:p0 + 64, :], vbuf[:, colA:colA + 64],
                                   pph_ap[:, pcol:pcol + 512], start=first, stop=False,
                                   tile_position=(0, p0)),
                          waits=ww)
                        E("tensor", lambda e, pcol=pcol, p0=p0, colA=colA:
                          e.matmul(pbt[4][p0:p0 + 64, :], vbuf[:, S + colA:S + colA + 64],
                                   pph_ap[:, pcol:pcol + 512], start=False, stop=False,
                                   tile_position=(0, p0)))
                        tpv = E("tensor", lambda e, pcol=pcol, p0=p0, colA=colA, stop=stop:
                                e.matmul(pbt[4][p0:p0 + 64, :], vbuf[:, colA:colA + 64],
                                         pph_ap[:, 2048 + pcol:2048 + pcol + 512], start=False,
                                         stop=stop, tile_position=(0, p0)),
                                inc=("TE" if (stop and p0 == 64) else None))
                        ob0 = 0 if p0 == 0 else 64
                        E("tensor", lambda e, pcol=pcol, p0=p0, ob0=ob0, first=first:
                          e.matmul(pbt[5][p0:p0 + 64, :], ones_b[:, ob0:ob0 + 64],
                                   pph_ap[:, pcol:pcol + 512], start=first, stop=False,
                                   tile_position=(0, p0)),
                          waits=SR("ones_b"))
                        tdn = E("tensor", lambda e, pcol=pcol, p0=p0, ob0=ob0, stop=stop:
                                e.matmul(pbt[5][p0:p0 + 64, :], ones_b[:, ob0:ob0 + 64],
                                         pph_ap[:, 2048 + pcol:2048 + pcol + 512], start=False,
                                         stop=stop, tile_position=(0, p0)),
                                inc=("TE" if (stop and p0 == 64) else None))
                        pp_.r.append(tpv)
                        pp_.r.append(tdn)
                        if stop and p0 == 64:
                            pv_tok_box[0], pv_tok_box[1] = tpv, tdn

                emit_scores(0)
                for c in range(nkt):
                    emit_expsplit(c)
                    if c + 1 < nkt:
                        emit_scores(c + 1)
                    emit_pvden(c)
                pv_tok, den_tok = pv_tok_box
                setw("pb4", pv_tok)
                setw("pb5", den_tok)
                for tsc_ in last_sc.values():
                    setr("kbuf", tsc_)
                    setr("qbuf", tsc_)
                setr("vbuf", pv_tok)
                trc = E("vector", lambda e: e.reciprocal(out=rden[:], in_=pbt[5][:]),
                        waits=SR("pb5") + SW("rden"), inc="VE")
                setw("rden", trc)
                setr("pb5", trc)
                ta1 = E("vector", lambda e, hp=hp:
                        e.tensor_tensor(out=sa[:, hp * 512:(hp + 1) * 512], in0=pbt[4][:],
                                        in1=rden[:], op=ALU.mult),
                        waits=[trc] + SR("pb4") + (SW("sa") if hp == 0 else []), inc="VE")
                setr("pb4", ta1)
                setr("rden", ta1)
                if hp == 0:
                    slots["sa"].w = ta1
                    slots["sa"].r = []
                ta2 = E("vector", lambda e, hp=hp: e.tensor_copy(out=g1[:, hp * 512:(hp + 1) * 512],
                                                                 in_=sa[:, hp * 512:(hp + 1) * 512]),
                        waits=(SW("g1") if hp == 0 else []), inc="VE")
                if hp == 0:
                    slots["g1"].w = ta2
                    slots["g1"].r = []
                ta3 = E("vector", lambda e, hp=hp: e.tensor_tensor(out=g2[:, hp * 512:(hp + 1) * 512],
                                                                   in0=sa[:, hp * 512:(hp + 1) * 512],
                                                                   in1=g1[:, hp * 512:(hp + 1) * 512],
                                                                   op=ALU.subtract),
                        waits=(SW("g2") if hp == 0 else []), inc="VE")
                if hp == 0:
                    slots["g2"].w = ta3
                    slots["g2"].r = []
            slots["g1"].w = ta3
            slots["g2"].w = ta3
            for oc in range(NCH):
                wts, b0 = wload(wsp["woh"], wsp["wol"], oc, oc % 2)
                wh_f, wl_f = wchunks(b0)
                bk = f"pb{4 + (oc % 2)}"
                tok = mm3(bk, wh_f, wl_f,
                          g1, g2, lambda t_, kc: t_[:, kc * 512:(kc + 1) * 512],
                          SR("g1") + SR("g2") + wts)
                setr("g1", tok)
                setr("g2", tok)
                setr(f"wb{oc % 2}", tok)
                w0 = oc * S + tt * 512
                tv = E("vector", lambda e, w0=w0, bk=bk:
                       e.tensor_tensor(out=yt[:, w0:w0 + 512], in0=pbank(bk)[:],
                                       in1=yt[:, w0:w0 + 512], op=ALU.add),
                       waits=SR(bk) + SWyt(tt), inc="VE")
                setr(bk, tv)
                setw_yt(tt, tv)
        allp = []
        for off in (0, 512, 1024, 1536):
            allp += list(pf_slots[off].r) + ([pf_slots[off].w] if pf_slots[off].w else [])
            allp += list(pp_slots[off].r) + ([pp_slots[off].w] if pp_slots[off].w else [])
        slots["sB"].r.extend(allp)
        slots["g3"].r.extend(allp)

    emit_consts()
    emit_wsplit()
    for s in range(SEQ):
        emit_load_x(s)
        if "ffn1" in phases:
            emit_ln(s)
            emit_ffn(s)
        if "attn" in phases:
            emit_ln(s)
            emit_attn(s)
        if "ffn2" in phases:
            emit_ln(s)
            emit_ffn(s)
        emit_store(s)

    with contextlib.ExitStack() as bstack:
        sems = {nm: bstack.enter_context(nc.semaphore(nm)) for nm in SEMNAMES}
        blk = bstack.enter_context(nc.Block())
        final_st = sched.cnt.get("ST", 0)

        @blk.sync
        def _(eng):
            sched.replay("sync", eng, sems)

        @blk.gpsimd
        def _(eng):
            sched.replay("gpsimd", eng, sems)
            eng.wait_ge(sems["ST"], final_st)

        @blk.tensor
        def _(eng):
            sched.replay("tensor", eng, sems)

        @blk.vector
        def _(eng):
            sched.replay("vector", eng, sems)

        @blk.scalar
        def _(eng):
            sched.replay("scalar", eng, sems)

    stack.close()
    return nc


_NC_CACHE = {}


def kernel(L0_attn_x, L0_attn_qkv, L0_attn_wo, L0_ffn_W_r, L0_ffn_W_k,
           L0_ffn_W_v, L0_ffn_mu_r, L0_ffn_mu_k, _phases=("ffn1", "attn", "ffn2"),
           _dbg=False):
    x = np.ascontiguousarray(np.asarray(L0_attn_x, dtype=np.float32).reshape(B * S, D))
    wq = np.ascontiguousarray(np.asarray(L0_attn_qkv, dtype=np.float32).T)
    wo = np.ascontiguousarray(np.asarray(L0_attn_wo, dtype=np.float32).T)
    wr = np.ascontiguousarray(np.asarray(L0_ffn_W_r, dtype=np.float32).T)
    wk = np.ascontiguousarray(np.asarray(L0_ffn_W_k, dtype=np.float32).T)
    wv = np.ascontiguousarray(np.asarray(L0_ffn_W_v, dtype=np.float32).T)
    mur_ = np.ascontiguousarray(np.asarray(L0_ffn_mu_r, dtype=np.float32))
    muk_ = np.ascontiguousarray(np.asarray(L0_ffn_mu_k, dtype=np.float32))

    key = (_phases, _dbg)
    if key not in _NC_CACHE:
        _NC_CACHE[key] = build_program(phases=_phases, dbg=_dbg)
    nc = _NC_CACHE[key]

    in_maps = []
    for c in range(NCORE):
        shard = x[c * SEQ * S:(c + 1) * SEQ * S]
        in_maps.append({"x_in": shard, "wq_t": wq, "wo_t": wo, "wr_t": wr,
                        "wk_t": wk, "wv_t": wv, "mu_r": mur_, "mu_k": muk_})
    res = run_bass_kernel_spmd(nc, in_maps, core_ids=list(range(NCORE)))
    outs = [np.asarray(r["out"]) for r in res.results]
    full = np.concatenate(outs, axis=0).reshape(B, S, D)
    if _dbg:
        return full, res.results
    return full


# version 08 (final)
# speedup vs baseline: 1.9078x; 1.2275x over previous
import contextlib
import numpy as np
import concourse.bass as bass
import concourse.mybir as mybir
from concourse.bass_utils import run_bass_kernel_spmd

F32 = mybir.dt.float32
I32 = mybir.dt.int32
BF16 = mybir.dt.bfloat16
AF = mybir.ActivationFunctionType
ALU = mybir.AluOpType
AX = mybir.AxisListType

B, S, D = 32, 2048, 1024
H, DH = 16, 64
NCORE = 8
SEQ = B // NCORE
NCH = D // 128
NTT = S // 512
NT2 = S // 128

INV_2PI = float(1.0 / (2 * np.pi))
HALF_PI = float(np.pi / 2)
T2PI_HI = float(np.float32(round((2 * np.pi) * 2048.0) / 2048.0))
T2PI_LO = float(np.float32(2 * np.pi - T2PI_HI))
EPS = 1e-5

ENGINES = ("sync", "gpsimd", "tensor", "vector", "scalar")


class Slot:
    __slots__ = ("w", "r")

    def __init__(self):
        self.w = None
        self.r = []


class Sched:
    def __init__(self):
        self.ops = {e: [] for e in ENGINES}
        self.cnt = {}
        self.hiwater = {e: {} for e in ENGINES}

    def op(self, eng, fn, waits=(), inc=None, incn=1):
        hw = self.hiwater[eng]
        pruned = {}
        for t in waits:
            if t is None:
                continue
            s, v = t
            if hw.get(s, -1) >= v or pruned.get(s, -1) >= v:
                continue
            pruned[s] = v
        for s, v in pruned.items():
            hw[s] = v
        tok = None
        if inc is not None:
            self.cnt[inc] = self.cnt.get(inc, 0) + incn
            tok = (inc, self.cnt[inc])
        self.ops[eng].append((fn, tuple(pruned.items()), inc, incn))
        return tok

    def replay(self, eng_name, eng, sems):
        for fn, waits, inc, incn in self.ops[eng_name]:
            for s, v in waits:
                eng.wait_ge(sems[s], v)
            instr = fn(eng)
            if inc is not None:
                if instr is None:
                    eng.sem_inc(sems[inc], incn)
                else:
                    instr.then_inc(sems[inc], incn)


def build_program(phases=("ffn1", "attn", "ffn2"), dbg=False):
    nc = bass.Bass("TRN2", target_bir_lowering=False)

    x_in = nc.dram_tensor("x_in", [SEQ * S, D], F32, kind="ExternalInput").ap()
    wq_t = nc.dram_tensor("wq_t", [D, 3 * D], F32, kind="ExternalInput").ap()
    wo_t = nc.dram_tensor("wo_t", [D, D], F32, kind="ExternalInput").ap()
    wr_t = nc.dram_tensor("wr_t", [D, D], F32, kind="ExternalInput").ap()
    wk_t = nc.dram_tensor("wk_t", [D, D], F32, kind="ExternalInput").ap()
    wv_t = nc.dram_tensor("wv_t", [D, D], F32, kind="ExternalInput").ap()
    mu_r = nc.dram_tensor("mu_r", [D], F32, kind="ExternalInput").ap()
    mu_k = nc.dram_tensor("mu_k", [D], F32, kind="ExternalInput").ap()
    out = nc.dram_tensor("out", [SEQ * S, D], F32, kind="ExternalOutput").ap()

    wsp = {}
    for nm, dout in [("wq", 3 * D), ("wo", D), ("wr", D), ("wk", D), ("wv", D)]:
        for h in ("h", "l"):
            wsp[nm + h] = nc.dram_tensor(f"wsp_{nm}{h}", [dout // 128, 128, 8 * 128], BF16).ap()
    kw = dict(kind="ExternalOutput") if dbg else {}
    q_sp = {h: nc.dram_tensor(f"q_sp{h}", [D, S], BF16, **kw).ap() for h in "hl"}
    k_sp = {h: nc.dram_tensor(f"k_sp{h}", [D, S], BF16, **kw).ap() for h in "hl"}
    v_sp = {h: nc.dram_tensor(f"v_sp{h}", [S, D], BF16, **kw).ap() for h in "hl"}
    dbg_crep = nc.dram_tensor("dbg_crep", [128, H], F32, kind="ExternalOutput").ap() if dbg else None
    dbg_ka = nc.dram_tensor("dbg_ka", [128, 4096], BF16, kind="ExternalOutput").ap() if dbg else None
    dbg_kb = nc.dram_tensor("dbg_kb", [128, 4096], BF16, kind="ExternalOutput").ap() if dbg else None

    sched = Sched()
    E = sched.op
    stack = contextlib.ExitStack()
    slots = {}

    def sb(name, shape, dtype):
        t = stack.enter_context(nc.sbuf_tensor(name, shape, dtype))
        slots[name] = Slot()
        return t

    ident32 = sb("ident32", [128, 128], F32)
    identb = sb("identb", [128, 128], BF16)
    onesf = sb("onesf", [1, 128], F32)
    rt_b = sb("rt_b", [128, 128], BF16)
    ones_b = sb("ones_b", [128, 128], BF16)
    blk_b = sb("blk_b", [128, 2], BF16)
    cosr = sb("cosr", [128, S], F32)
    sinr = sb("sinr", [128, S], F32)
    mur = sb("mur", [128, NCH], F32)
    muk = sb("muk", [128, NCH], F32)
    cEps = sb("cEps", [128, 1], F32)
    cHpi = sb("cHpi", [128, 1], F32)
    invf_c = sb("invf_c", [128, 1], F32)
    crep = sb("crep", [128, H], F32)
    bnd = sb("bnd", [128, NCH * NTT], F32)
    rowb = sb("rowb", [128, S], F32)
    rrep = sb("rrep", [128, S], F32)
    mrrep = sb("mrrep", [128, S], F32)
    yt = sb("yt", [128, NCH * S], F32)
    sa = sb("sa", [128, 4096], F32)
    sB = sb("sB", [128, 4104], F32)
    g1 = sb("g1", [128, 4096], BF16)
    g2 = sb("g2", [128, 4096], BF16)
    g3 = sb("g3", [128, 4096], BF16)
    g4 = sb("g4", [128, 4096], BF16)
    kbuf = sb("kbuf", [128, 2 * S], BF16)
    vbuf = sb("vbuf", [128, 2 * S], BF16)
    qbuf = sb("qbuf", [128, 1024], BF16)
    wbuf = sb("wbuf", [128, 4096], BF16)
    slots["wb0"] = Slot()
    slots["wb1"] = Slot()
    iost = sb("iost", [128, 1024], F32)
    rden = sb("rden", [128, 512], F32)
    nn_i = sb("nn_i", [128, 512], I32)


    pbt = []
    for i in range(7):
        t = stack.enter_context(nc.psum_tensor(f"pb{i}", [128, 512], F32))
        slots[f"pb{i}"] = Slot()
        pbt.append(t)
    pbb = stack.enter_context(nc.psum_tensor("pbb", [128, 1024], BF16))
    slots["pbb"] = Slot()

    def pbank(bk):
        return pbt[int(bk[2])]

    SEMNAMES = ["LD", "LW", "ST", "TE", "VE", "SC", "GP"]

    def SW(name):
        s = slots[name]
        return list(s.r) + ([s.w] if s.w else [])

    def SR(name):
        s = slots[name]
        return [s.w] if s.w else []

    def setw(name, tok):
        s = slots[name]
        s.w = tok
        s.r = []

    def setr(name, tok):
        if tok is not None:
            slots[name].r.append(tok)

    CONST_NAMES = ["ident32", "identb", "onesf", "rt_b", "ones_b",
                   "blk_b", "cEps", "cHpi", "invf_c"]

    yts_ = [Slot() for _ in range(NTT)]

    def SRyt(tt):
        return [yts_[tt].w] if yts_[tt].w else []

    def SWyt(tt):
        return list(yts_[tt].r) + SRyt(tt)

    def setw_yt(tt, tok):
        yts_[tt].w = tok
        yts_[tt].r = []

    def setr_yt(tt, tok):
        if tok is not None:
            yts_[tt].r.append(tok)

    def SRyt_all():
        out = []
        for tt in range(NTT):
            out += SRyt(tt)
        return out

    def SWyt_all():
        out = []
        for tt in range(NTT):
            out += SWyt(tt)
        return out

    def emit_consts():
        g = "gpsimd"
        E(g, lambda e: e.memset(cEps[:], EPS))
        E(g, lambda e: e.memset(cHpi[:], HALF_PI))
        E(g, lambda e: e.memset(onesf[:], 1.0))
        E(g, lambda e: e.memset(ones_b[:], 1.0))
        E(g, lambda e: e.memset(blk_b[:], 0.0))
        E(g, lambda e: e.memset(blk_b[0:64, 0:1], 1.0))
        E(g, lambda e: e.memset(blk_b[64:128, 1:2], 1.0))
        for t, _ in ((ident32, F32), (identb, BF16)):
            E(g, lambda e, t=t: e.memset(t[:], 1.0))
            E(g, lambda e, t=t: e.affine_select(out=t[:], in_=t[:],
                                                pattern=[[1, 128]], compare_op=ALU.is_equal,
                                                fill=0.0, base=0, channel_multiplier=-1))
        E(g, lambda e: e.memset(rt_b[:], -1.0))
        E(g, lambda e: e.affine_select(out=rt_b[:], in_=rt_b[:],
                                       pattern=[[1, 128]], compare_op=ALU.is_equal,
                                       fill=0.0, base=32, channel_multiplier=-1))
        E(g, lambda e: e.affine_select(out=rt_b[:], in_=rt_b[:],
                                       pattern=[[0, 2], [-1, 64]], compare_op=ALU.is_ge,
                                       fill=0.0, base=31, channel_multiplier=0))
        E(g, lambda e: e.memset(sa[:, 0:128], 1.0))
        E(g, lambda e: e.affine_select(out=sa[:, 0:128], in_=sa[:, 0:128],
                                       pattern=[[1, 128]], compare_op=ALU.is_equal,
                                       fill=0.0, base=-32, channel_multiplier=-1))
        tgp = E(g, lambda e: e.affine_select(out=sa[:, 0:128], in_=sa[:, 0:128],
                                             pattern=[[0, 2], [1, 64]], compare_op=ALU.is_ge,
                                             fill=0.0, base=-32, channel_multiplier=0),
                inc="GP")
        tve = E("vector", lambda e: e.tensor_tensor(out=rt_b[:], in0=rt_b[:],
                                                    in1=sa[:, 0:128], op=ALU.add),
                waits=[tgp], inc="VE")
        setw("sa", tve)

        t1 = E("sync", lambda e: e.dma_start(out=mur[:], in_=mu_r.rearrange("(c p) -> p c", p=128),
                                             allow_slow_non_contiguous=True),
               inc="LD", incn=16)
        t2 = E("sync", lambda e: e.dma_start(out=muk[:], in_=mu_k.rearrange("(c p) -> p c", p=128),
                                             allow_slow_non_contiguous=True),
               inc="LD", incn=16)
        setw("mur", t1)
        setw("muk", t2)

        tio = E(g, lambda e: e.iota(out=sinr[:, :], pattern=[[1, S]], base=0,
                                    channel_multiplier=0,
                                    allow_small_or_imprecise_dtypes=True), inc="GP")
        for j in range(128):
            val = float(np.float32(10000.0 ** (-(j % 32) / 32.0)))
            E(g, lambda e, j=j, val=val: e.memset(rowb[0:1, j:j + 1], val))
        tgp2 = E(g, lambda e: e.engine_nop(), inc="GP")
        ttr0 = E("tensor", lambda e: e.transpose(pbt[6][:, 0:1], rowb[0:1, 0:128], onesf[0:1, 0:1]),
                 waits=[tgp2] + SW("pb6"), inc="TE")
        setw("pb6", ttr0)
        tiv = E("vector", lambda e: e.tensor_copy(out=invf_c[:], in_=pbt[6][:, 0:1]),
                waits=SR("pb6"), inc="VE")
        setr("pb6", tiv)
        setw("rowb", tgp2)
        tv1 = E("vector", lambda e: e.tensor_scalar(out=sinr[:], in0=sinr[:],
                                                    scalar1=invf_c[:], scalar2=None,
                                                    op0=ALU.mult),
                waits=[tio, tiv], inc="VE")
        tlast = tv1
        for j in range(S // 512):
            c0 = j * 512
            ta = E("vector", lambda e, c0=c0: e.tensor_scalar(out=nn_i[:], in0=sinr[:, c0:c0 + 512],
                                                              scalar1=INV_2PI, scalar2=None,
                                                              op0=ALU.mult),
                   waits=[tlast], inc="VE")
            tb = E("vector", lambda e, c0=c0: e.tensor_copy(out=sB[:, c0:c0 + 512], in_=nn_i[:]),
                   waits=[ta], inc="VE")
            tlast = tb
        tv4 = E("vector", lambda e: e.scalar_tensor_tensor(out=sinr[:], in0=sB[:, 0:S],
                                                           scalar=-T2PI_HI, in1=sinr[:],
                                                           op0=ALU.mult, op1=ALU.add),
                waits=[tlast], inc="VE")
        tv5 = E("vector", lambda e: e.scalar_tensor_tensor(out=sinr[:], in0=sB[:, 0:S],
                                                           scalar=-T2PI_LO, in1=sinr[:],
                                                           op0=ALU.mult, op1=ALU.add),
                waits=[tv4], inc="VE")
        ts1 = E("scalar", lambda e: e.activation(out=sB[:, 0:S], in_=sinr[:], func=AF.Abs),
                waits=[tv5], inc="SC")
        ts2 = E("scalar", lambda e: e.activation(out=cosr[:], in_=sB[:, 0:S], func=AF.Sin,
                                                 scale=-1.0, bias=cHpi[:]),
                waits=[ts1], inc="SC")
        ts3 = E("scalar", lambda e: e.activation(out=sinr[:], in_=sinr[:], func=AF.Sin),
                waits=[ts2], inc="SC")
        setw("cosr", ts2)
        setw("sinr", ts3)
        setw("sB", ts1)
        tcb = E(g, lambda e: e.engine_nop(), inc="GP")
        for nm in CONST_NAMES:
            slots[nm].w = tcb

    def emit_wsplit():
        srcs = [(wq_t, "wq", 3 * D), (wo_t, "wo", D), (wr_t, "wr", D),
                (wk_t, "wk", D), (wv_t, "wv", D)]
        for src, nm, dout in srcs:
            for kc in range(NCH):
                tld = E("sync", lambda e, src=src, kc=kc, dout=dout:
                        e.dma_start(out=sa[:, 0:dout], in_=src[kc * 128:(kc + 1) * 128, :]),
                        waits=SW("sa"), inc="LD", incn=16)
                setw("sa", tld)
                th = E("vector", lambda e, dout=dout: e.tensor_copy(out=g1[:, 0:dout], in_=sa[:, 0:dout]),
                       waits=SR("sa") + SW("g1"), inc="VE")
                setw("g1", th)
                tl = E("vector", lambda e, dout=dout: e.tensor_tensor(out=g2[:, 0:dout], in0=sa[:, 0:dout],
                                                                      in1=g1[:, 0:dout], op=ALU.subtract),
                       waits=SW("g2"), inc="VE")
                setw("g2", tl)
                setr("sa", tl)
                noc = dout // 128
                hf = noc // 2
                for g_t, gn, suf in ((g1, "g1", "h"), (g2, "g2", "l")):
                    for lo, hi in ((0, hf), (hf, noc)):
                        st_ = E("gpsimd", lambda e, nm=nm, kc=kc, suf=suf, lo=lo, hi=hi, g_t=g_t:
                                e.dma_start(
                                    out=wsp[nm + suf][lo:hi, :, kc * 128:(kc + 1) * 128]
                                    .rearrange("oc p d -> p oc d"),
                                    in_=g_t[:, lo * 128:hi * 128]
                                    .rearrange("p (oc d) -> p oc d", d=128)),
                                waits=SR(gn), inc="ST", incn=16)
                        setr(gn, st_)

    def emit_load_x(s):
        ytw = SWyt_all()
        yt4 = yt.rearrange("p (c j) -> p c j", j=S)
        for t2 in range(NT2):
            tld = E("sync", lambda e, s=s, t2=t2:
                    e.dma_start(out=iost[:], in_=x_in[s * S + t2 * 128: s * S + (t2 + 1) * 128, :]),
                    waits=SW("iost"), inc="LD", incn=16)
            setw("iost", tld)
            for half in range(2):
                bk = f"pb{5 + half}"
                tmm = None
                for j in range(4):
                    c = half * 4 + j
                    tmm = E("tensor", lambda e, c=c, j=j, bk=bk:
                            e.transpose(pbank(bk)[:, j * 128:(j + 1) * 128],
                                        iost[:, c * 128:(c + 1) * 128], ident32[:]),
                            waits=(SR("iost") + SW(bk) + SR("ident32")) if j == 0 else [],
                            inc=("TE" if j == 3 else None))
                setw(bk, tmm)
                setr("iost", tmm)
                tev = E("scalar", lambda e, half=half, t2=t2, bk=bk:
                        e.copy(out=yt4[:, half * 4:half * 4 + 4, t2 * 128:(t2 + 1) * 128],
                               in_=pbank(bk)[:, 0:512].rearrange("p (c j) -> p c j", j=128)),
                        waits=SR(bk) + (ytw if (t2 == 0 and half == 0) else []), inc="SC")
                setr(bk, tev)
        for tt in range(NTT):
            setw_yt(tt, tev)

    def emit_store(s):
        laststore = None
        for t2 in range(NT2):
            evs = []
            for half in range(2):
                bk = f"pb{5 + half}"
                tmm = None
                for j in range(4):
                    c = half * 4 + j
                    tmm = E("tensor", lambda e, c=c, j=j, t2=t2, bk=bk:
                            e.transpose(pbank(bk)[:, j * 128:(j + 1) * 128],
                                        yt[:, c * S + t2 * 128: c * S + (t2 + 1) * 128], ident32[:]),
                            waits=(SRyt(t2 // 4) + SW(bk)) if j == 0 else [],
                            inc=("TE" if j == 3 else None))
                setw(bk, tmm)
                setr_yt(t2 // 4, tmm)
                tev = E("scalar", lambda e, half=half, bk=bk:
                        e.copy(out=iost[:, half * 512:(half + 1) * 512], in_=pbank(bk)[:, 0:512]),
                        waits=SR(bk) + (SW("iost") if half == 0 else []), inc="SC")
                setr(bk, tev)
                evs.append(tev)
            slots["iost"].w = evs[-1]
            slots["iost"].r = []
            laststore = E("gpsimd", lambda e, s=s, t2=t2:
                          e.dma_start(out=out[s * S + t2 * 128: s * S + (t2 + 1) * 128, :], in_=iost[:]),
                          waits=evs, inc="ST", incn=16)
            setr("iost", laststore)
        return laststore

    def emit_ln(s):
        lastmm = {0: None, 1: None}
        g_init = [SW("g1") + SW("g2") + SW("g3") + SW("g4"), None]
        for tt in range(NTT):
            ty = tq2 = None
            for c in range(NCH):
                i = tt * NCH + c
                sl = (i % 2) * 512
                w0 = c * S + tt * 512
                prev = [lastmm[i % 2]] if lastmm[i % 2] else (g_init[0] if i == 0 else (g_init[0] if i == 1 else []))
                th = E("scalar", lambda e, w0=w0, sl=sl: e.copy(out=g1[:, sl:sl + 512], in_=yt[:, w0:w0 + 512]),
                       waits=SRyt(tt) + prev, inc="SC")
                tl = E("vector", lambda e, w0=w0, sl=sl: e.tensor_tensor(out=g2[:, sl:sl + 512],
                                                                         in0=yt[:, w0:w0 + 512],
                                                                         in1=g1[:, sl:sl + 512], op=ALU.subtract),
                       waits=[th] + SRyt(tt) + prev, inc="VE")
                setr_yt(tt, tl)
                tqs = E("scalar", lambda e, w0=w0, sl=sl: e.activation(out=sB[:, sl:sl + 512], in_=yt[:, w0:w0 + 512],
                                                                       func=AF.Square),
                        waits=SRyt(tt) + (SW("sB") if i == 0 else []), inc="SC")
                setr_yt(tt, th)
                setr_yt(tt, tqs)
                th2 = E("scalar", lambda e, sl=sl: e.copy(out=g3[:, sl:sl + 512], in_=sB[:, sl:sl + 512]),
                        waits=prev, inc="SC")
                tl2 = E("vector", lambda e, sl=sl: e.tensor_tensor(out=g4[:, sl:sl + 512], in0=sB[:, sl:sl + 512],
                                                                   in1=g3[:, sl:sl + 512], op=ALU.subtract),
                        waits=[th2] + prev, inc="VE")
                first = (c == 0)
                stop = (c == NCH - 1)
                E("tensor", lambda e, first=first, sl=sl: e.matmul(pbt[6][0:1, :], ones_b[:, 0:1], g1[:, sl:sl + 512],
                                                                   start=first, stop=False),
                  waits=[tl] + SR("ones_b") + (SW("pb6") if first else []))
                ty = E("tensor", lambda e, stop=stop, sl=sl: e.matmul(pbt[6][0:1, :], ones_b[:, 0:1], g2[:, sl:sl + 512],
                                                                      start=False, stop=stop), inc="TE")
                E("tensor", lambda e, first=first, sl=sl: e.matmul(pbt[5][0:1, :], ones_b[:, 0:1], g3[:, sl:sl + 512],
                                                                   start=first, stop=False),
                  waits=[tl2] + (SW("pb5") if first else []))
                tq2 = E("tensor", lambda e, stop=stop, sl=sl: e.matmul(pbt[5][0:1, :], ones_b[:, 0:1], g4[:, sl:sl + 512],
                                                                       start=False, stop=stop), inc="TE")
                lastmm[i % 2] = tq2
            setw("pb6", ty)
            setw("pb5", tq2)
            e1 = E("vector", lambda e, tt=tt: e.tensor_copy(out=rowb[0:1, tt * 512:(tt + 1) * 512],
                                                            in_=pbt[6][0:1, :]),
                   waits=SR("pb6") + (SW("rowb") if tt == 0 else []), inc="VE")
            setr("pb6", e1)
            e2 = E("vector", lambda e, tt=tt: e.tensor_copy(out=sa[0:1, tt * 512:(tt + 1) * 512],
                                                            in_=pbt[5][0:1, :]),
                   waits=SR("pb5") + (SW("sa") if tt == 0 else []), inc="VE")
            setr("pb5", e2)
            if tt == 0:
                slots["rowb"].w = e2
                slots["rowb"].r = []
                slots["sa"].w = e2
                slots["sa"].r = []
        for nm in ("g1", "g2", "g3", "g4"):
            slots[nm].w = tq2
            slots[nm].r = []
        slots["sB"].w = tq2
        slots["sB"].r = []
        slots["rowb"].w = e2
        slots["sa"].w = e2
        tv = E("vector", lambda e: e.tensor_scalar(out=sa[0:1, 2048:2048 + S], in0=rowb[0:1, 0:S],
                                                   scalar1=1.0 / D, scalar2=None, op0=ALU.mult),
               inc="VE")
        tv2 = E("vector", lambda e: e.tensor_tensor(out=rowb[0:1, 0:S], in0=sa[0:1, 2048:2048 + S],
                                                    in1=sa[0:1, 2048:2048 + S], op=ALU.mult),
                waits=[tv], inc="VE")
        tv3 = E("vector", lambda e: e.tensor_scalar(out=sa[0:1, 0:S], in0=sa[0:1, 0:S],
                                                    scalar1=1.0 / D, scalar2=None, op0=ALU.mult),
                waits=[tv2], inc="VE")
        tv4 = E("vector", lambda e: e.tensor_tensor(out=sa[0:1, 0:S], in0=sa[0:1, 0:S],
                                                    in1=rowb[0:1, 0:S], op=ALU.subtract),
                waits=[tv3], inc="VE")
        ts = E("scalar", lambda e: e.activation(out=sa[0:1, 0:S], in_=sa[0:1, 0:S],
                                                func=AF.Sqrt, bias=cEps[0:1, :]),
               waits=[tv4], inc="SC")
        tv5 = E("vector", lambda e: e.reciprocal(out=rowb[0:1, 0:S], in_=sa[0:1, 0:S]),
                waits=[ts], inc="VE")
        tv6 = E("vector", lambda e: e.tensor_tensor(out=sa[0:1, 2048:2048 + S],
                                                    in0=sa[0:1, 2048:2048 + S],
                                                    in1=rowb[0:1, 0:S], op=ALU.mult),
                waits=[tv5], inc="VE")
        slots["rowb"].w = tv6
        slots["sa"].w = tv6
        te2 = None
        for tt in range(NTT):
            bk = f"pb{5 + (tt % 2)}"
            tm = E("tensor", lambda e, tt=tt, bk=bk:
                   e.matmul(pbank(bk)[:, 0:512], onesf[0:1, :], rowb[0:1, tt * 512:(tt + 1) * 512],
                            start=True, stop=True),
                   waits=SR("rowb") + SW(bk) + SR("onesf"), inc="TE")
            setw(bk, tm)
            setr("rowb", tm)
            te1 = E("scalar", lambda e, tt=tt, bk=bk: e.copy(out=rrep[:, tt * 512:(tt + 1) * 512],
                                                             in_=pbank(bk)[:, 0:512]),
                    waits=SR(bk) + (SW("rrep") if tt == 0 else []), inc="SC")
            setr(bk, te1)
            if tt == 0:
                slots["rrep"].w = te1
                slots["rrep"].r = []
            tm2 = E("tensor", lambda e, tt=tt, bk=bk:
                    e.matmul(pbank(bk)[:, 0:512], onesf[0:1, :],
                             sa[0:1, 2048 + tt * 512:2048 + (tt + 1) * 512], start=True, stop=True),
                    waits=SR("sa") + [te1] + SR("onesf"), inc="TE")
            setw(bk, tm2)
            setr("sa", tm2)
            te2 = E("scalar", lambda e, tt=tt, bk=bk: e.copy(out=mrrep[:, tt * 512:(tt + 1) * 512],
                                                             in_=pbank(bk)[:, 0:512]),
                    waits=SR(bk) + (SW("mrrep") if tt == 0 else []), inc="SC")
            setr(bk, te2)
            if tt == 0:
                slots["mrrep"].w = te2
                slots["mrrep"].r = []
        slots["rrep"].w = te2
        slots["mrrep"].w = te2

    def mm3(bk, wsl_h, wsl_l, xh_t, xl_t, xsl, extra_waits):
        tok = None
        for kc in range(NCH):
            wh_ = wsl_h(kc)
            wl_ = wsl_l(kc)
            xh_ = xsl(xh_t, kc)
            xl_ = xsl(xl_t, kc)
            w = (extra_waits + SW(bk)) if kc == 0 else []
            E("tensor", lambda e, wh_=wh_, xh_=xh_, kc=kc, bk=bk:
              e.matmul(pbank(bk)[:], wh_, xh_, start=(kc == 0), stop=False), waits=w)
            E("tensor", lambda e, wl_=wl_, xh_=xh_, bk=bk:
              e.matmul(pbank(bk)[:], wl_, xh_, start=False, stop=False))
            tok = E("tensor", lambda e, wh_=wh_, xl_=xl_, kc=kc, bk=bk:
                    e.matmul(pbank(bk)[:], wh_, xl_, start=False, stop=(kc == NCH - 1)),
                    inc=("TE" if kc == NCH - 1 else None))
        setw(bk, tok)
        return tok

    def wload(src_h, src_l, oc, slot):
        bname = f"wb{slot}"
        b0 = slot * 2048
        t1 = E("sync", lambda e, oc=oc, src_h=src_h, b0=b0: e.dma_start(
            out=wbuf[:, b0:b0 + 1024], in_=src_h[oc]),
            waits=SW(bname), inc="LW", incn=16)
        t2 = E("sync", lambda e, oc=oc, src_l=src_l, b0=b0: e.dma_start(
            out=wbuf[:, b0 + 1024:b0 + 2048], in_=src_l[oc]),
            inc="LW", incn=16)
        setw(bname, t2)
        return [t1, t2], b0

    def wchunks(b0):
        return (lambda kc: wbuf[:, b0 + kc * 128: b0 + (kc + 1) * 128],
                lambda kc: wbuf[:, b0 + 1024 + kc * 128: b0 + 1024 + (kc + 1) * 128])

    def emit_ffn(s):
        PAIR = [(g1, "g1", g2, "g2"), (g3, "g3", g4, "g4")]
        t = None
        for c in range(NCH):
            for j in range(1, NTT):
                col = c * S + j * 512 - 1
                t = E("vector", lambda e, c=c, j=j, col=col:
                      e.tensor_copy(out=bnd[:, (c * NTT + j):(c * NTT + j) + 1], in_=yt[:, col:col + 1]),
                      waits=SRyt(j - 1) + (SW("bnd") if (c == 0 and j == 1) else []), inc="VE")
                setr_yt(j - 1, t)
        setw("bnd", t)

        def emit_xln_d(tt, c):
            r0 = tt * 512
            w0 = c * S + tt * 512
            base = c * 513
            if tt == 0:
                bq = E("vector", lambda e, base=base: e.memset(sB[:, base:base + 1], 0.0),
                       waits=SW("sB") if c == 0 else [], inc="VE")
            else:
                bq0 = E("vector", lambda e, base=base, c=c, tt=tt, r0=r0:
                        e.tensor_tensor(out=sB[:, base:base + 1],
                                        in0=bnd[:, c * NTT + tt:c * NTT + tt + 1],
                                        in1=rrep[:, r0 - 1:r0], op=ALU.mult),
                        waits=SR("bnd") + SR("rrep") + (SW("sB") if c == 0 else []), inc="VE")
                bq = E("vector", lambda e, base=base, r0=r0:
                       e.tensor_tensor(out=sB[:, base:base + 1], in0=sB[:, base:base + 1],
                                       in1=mrrep[:, r0 - 1:r0], op=ALU.subtract),
                       waits=SR("mrrep") + [bq0], inc="VE")
            t1 = E("vector", lambda e, base=base, w0=w0, r0=r0:
                   e.tensor_tensor(out=sB[:, base + 1: base + 513], in0=yt[:, w0:w0 + 512],
                                   in1=rrep[:, r0:r0 + 512], op=ALU.mult),
                   waits=SRyt(tt) + SR("rrep"), inc="VE")
            setr_yt(tt, t1)
            tsub = E("vector", lambda e, base=base, r0=r0:
                     e.tensor_tensor(out=sB[:, base + 1: base + 513], in0=sB[:, base + 1: base + 513],
                                     in1=mrrep[:, r0:r0 + 512], op=ALU.subtract),
                     waits=SR("mrrep"), inc="VE")
            slots["sB"].w = tsub
            td = E("vector", lambda e, base=base:
                   e.tensor_tensor(out=rden[:, 0:512], in0=sB[:, base + 1: base + 513],
                                   in1=sB[:, base: base + 512], op=ALU.subtract),
                   waits=[bq] + SW("rden"), inc="VE")
            setw("rden", td)
            return td

        def emit_d(c):
            base = c * 513
            td = E("vector", lambda e, base=base:
                   e.tensor_tensor(out=rden[:, 0:512], in0=sB[:, base + 1: base + 513],
                                   in1=sB[:, base: base + 512], op=ALU.subtract),
                   waits=SW("rden"), inc="VE")
            setw("rden", td)
            return td

        def emit_mix(tt, c, td, mu_t, mu_n, hi_t, hi_n, lo_t, lo_n, ioff):
            base = c * 513
            t4 = E("vector", lambda e, c=c, base=base:
                   e.scalar_tensor_tensor(out=hi_t[:, c * 512:(c + 1) * 512],
                                          in0=rden[:, 0:512], scalar=mu_t[:, c:c + 1],
                                          in1=sB[:, base: base + 512],
                                          op0=ALU.mult, op1=ALU.add),
                   waits=[td] + SR(mu_n) + SW(hi_n), inc="VE")
            if c == 0:
                slots[hi_n].w = t4
                slots[hi_n].r = []
            t6 = E("vector", lambda e, c=c, base=base, ioff=ioff:
                   e.scalar_tensor_tensor(out=iost[:, ioff:ioff + 512],
                                          in0=rden[:, 0:512], scalar=mu_t[:, c:c + 1],
                                          in1=sB[:, base: base + 512],
                                          op0=ALU.mult, op1=ALU.add),
                   waits=SW("iost"), inc="VE")
            slots["iost"].w = t6
            t7 = E("vector", lambda e, c=c, ioff=ioff:
                   e.tensor_tensor(out=lo_t[:, c * 512:(c + 1) * 512], in0=iost[:, ioff:ioff + 512],
                                   in1=hi_t[:, c * 512:(c + 1) * 512], op=ALU.subtract),
                   waits=SW(lo_n), inc="VE")
            if c == 0:
                slots[lo_n].w = t7
                slots[lo_n].r = []
            setr("iost", t7)
            setr("sB", t7)
            slots[hi_n].w = t7
            slots[lo_n].w = t7
            return t7

        def emit_mix_chunk(tt, c, which):
            ph = tt % 2
            if which in ("rw",):
                td = emit_xln_d(tt, c)
            else:
                td = emit_d(c)
            if which in ("r", "rw"):
                hi_t, hi_n, lo_t, lo_n = PAIR[ph]
                return emit_mix(tt, c, td, mur, "mur", hi_t, hi_n, lo_t, lo_n, 0)
            else:
                hi_t, hi_n, lo_t, lo_n = PAIR[1 - ph]
                return emit_mix(tt, c, td, muk, "muk", hi_t, hi_n, lo_t, lo_n, 512)

        for c in range(NCH):
            emit_mix_chunk(0, c, "rw")
        for c in range(NCH):
            emit_mix_chunk(0, c, "k")

        for tt in range(NTT):
            ph = tt % 2
            xr_t, xr_n, xr_lo_t, xr_lo_n = PAIR[ph]
            xk_t, xk_n, xk_lo_t, xk_lo_n = PAIR[1 - ph]
            ter = None
            for oc in range(NCH):
                wts, b0 = wload(wsp["wrh"], wsp["wrl"], oc, oc % 2)
                wh_f, wl_f = wchunks(b0)
                bk = f"pb{0 + (oc % 2)}"
                tok = mm3(bk, wh_f, wl_f,
                          xr_t, xr_lo_t, lambda t_, kc: t_[:, kc * 512:(kc + 1) * 512],
                          SR(xr_n) + SR(xr_lo_n) + wts)
                setr(xr_n, tok)
                setr(xr_lo_n, tok)
                setr(f"wb{oc % 2}", tok)
                ter = E("scalar", lambda e, oc=oc, bk=bk:
                        e.activation(out=sa[:, oc * 512:(oc + 1) * 512], in_=pbank(bk)[:],
                                     func=AF.Sigmoid),
                        waits=SR(bk) + (SW("sa") if oc == 0 else []), inc="SC")
                setr(bk, ter)
                if oc == 0:
                    slots["sa"].w = ter
                    slots["sa"].r = []
            slots["sa"].w = ter
            tk4 = None
            for oc in range(NCH):
                wts, b0 = wload(wsp["wkh"], wsp["wkl"], oc, oc % 2)
                wh_f, wl_f = wchunks(b0)
                bk = f"pb{2 + (oc % 2)}"
                tok = mm3(bk, wh_f, wl_f,
                          xk_t, xk_lo_t, lambda t_, kc: t_[:, kc * 512:(kc + 1) * 512],
                          SR(xk_n) + SR(xk_lo_n) + wts)
                setr(xk_n, tok)
                setr(xk_lo_n, tok)
                setr(f"wb{oc % 2}", tok)
                tk1 = E("scalar", lambda e, oc=oc, bk=bk:
                        e.activation(out=sB[:, oc * 512:(oc + 1) * 512], in_=pbank(bk)[:],
                                     func=AF.Relu),
                        waits=SR(bk) + (SW("sB") if oc == 0 else []), inc="SC")
                setr(bk, tk1)
                if oc == 0:
                    slots["sB"].w = tk1
                    slots["sB"].r = []
                tk2 = E("scalar", lambda e, oc=oc:
                        e.activation(out=sB[:, oc * 512:(oc + 1) * 512],
                                     in_=sB[:, oc * 512:(oc + 1) * 512], func=AF.Square),
                        inc="SC")
                tk3 = E("scalar", lambda e, oc=oc, xr_t=xr_t: e.copy(out=xr_t[:, oc * 512:(oc + 1) * 512],
                                                                     in_=sB[:, oc * 512:(oc + 1) * 512]),
                        waits=[tk2] + (SW(xr_n) if oc == 0 else []), inc="SC")
                if oc == 0:
                    slots[xr_n].w = tk3
                    slots[xr_n].r = []
                tk4 = E("vector", lambda e, oc=oc, xr_t=xr_t, xr_lo_t=xr_lo_t:
                        e.tensor_tensor(out=xr_lo_t[:, oc * 512:(oc + 1) * 512],
                                        in0=sB[:, oc * 512:(oc + 1) * 512],
                                        in1=xr_t[:, oc * 512:(oc + 1) * 512],
                                        op=ALU.subtract),
                        waits=[tk3] + (SW(xr_lo_n) if oc == 0 else []), inc="VE")
                if oc == 0:
                    slots[xr_lo_n].w = tk4
                    slots[xr_lo_n].r = []
                slots["sB"].w = tk4
            slots[xr_n].w = tk4
            slots[xr_lo_n].w = tk4
            for oc in range(NCH):
                wts, b0 = wload(wsp["wvh"], wsp["wvl"], oc, oc % 2)
                wh_f, wl_f = wchunks(b0)
                bk = f"pb{4 + (oc % 2)}"
                tok = mm3(bk, wh_f, wl_f,
                          xr_t, xr_lo_t, lambda t_, kc: t_[:, kc * 512:(kc + 1) * 512],
                          SR(xr_n) + SR(xr_lo_n) + wts)
                setr(xr_n, tok)
                setr(xr_lo_n, tok)
                setr(f"wb{oc % 2}", tok)
                w0 = oc * S + tt * 512
                tv1 = E("vector", lambda e, oc=oc, bk=bk:
                        e.tensor_tensor(out=iost[:, 0:512], in0=pbank(bk)[:],
                                        in1=sa[:, oc * 512:(oc + 1) * 512], op=ALU.mult),
                        waits=SR(bk) + SR("sa") + SW("iost"), inc="VE")
                setr(bk, tv1)
                setr("sa", tv1)
                slots["iost"].w = tv1
                slots["iost"].r = []
                tv2 = E("vector", lambda e, w0=w0:
                        e.scalar_tensor_tensor(out=yt[:, w0:w0 + 512], in0=iost[:, 0:512],
                                               scalar=0.5, in1=yt[:, w0:w0 + 512],
                                               op0=ALU.mult, op1=ALU.add),
                        waits=SWyt(tt), inc="VE")
                slots["iost"].r.append(tv2)
                setw_yt(tt, tv2)
                if tt + 1 < NTT:
                    emit_mix_chunk(tt + 1, oc, "rw")
            if tt + 1 < NTT:
                for c in range(NCH):
                    emit_mix_chunk(tt + 1, c, "k")

    pf_slots = {off: Slot() for off in (0, 512, 1024, 1536)}
    pp_slots = {off: Slot() for off in (0, 512, 1024, 1536)}

    def emit_attn(s):
        qn_toks = []
        for tt in range(NTT):
            r0 = tt * 512
            for c in range(NCH):
                w0 = c * S + tt * 512
                t1 = E("vector", lambda e, c=c, w0=w0, r0=r0:
                       e.tensor_tensor(out=sa[:, c * 512:(c + 1) * 512], in0=yt[:, w0:w0 + 512],
                                       in1=rrep[:, r0:r0 + 512], op=ALU.mult),
                       waits=SRyt(tt) + SR("rrep") + (SW("sa") if c == 0 else []), inc="VE")
                setr_yt(tt, t1)
                if c == 0:
                    slots["sa"].w = t1
                    slots["sa"].r = []
                E("vector", lambda e, c=c, r0=r0:
                  e.tensor_tensor(out=sa[:, c * 512:(c + 1) * 512], in0=sa[:, c * 512:(c + 1) * 512],
                                  in1=mrrep[:, r0:r0 + 512], op=ALU.subtract),
                  waits=SR("mrrep"), inc="VE")
                t3 = E("vector", lambda e, c=c: e.tensor_copy(out=g1[:, c * 512:(c + 1) * 512],
                                                              in_=sa[:, c * 512:(c + 1) * 512]),
                       waits=(SW("g1") if c == 0 else []), inc="VE")
                if c == 0:
                    slots["g1"].w = t3
                    slots["g1"].r = []
                t4 = E("vector", lambda e, c=c: e.tensor_tensor(out=g2[:, c * 512:(c + 1) * 512],
                                                                in0=sa[:, c * 512:(c + 1) * 512],
                                                                in1=g1[:, c * 512:(c + 1) * 512],
                                                                op=ALU.subtract),
                       waits=(SW("g2") if c == 0 else []), inc="VE")
                if c == 0:
                    slots["g2"].w = t4
                    slots["g2"].r = []
            slots["g1"].w = t4
            slots["g2"].w = t4
            slots["sa"].w = t4

            def _qkv_post(bk, hp, oc):
                if oc < 2 * NCH:
                    is_q = oc < NCH
                    col = hp * 512
                    e1 = E("scalar", lambda e, col=col, bk=bk:
                           e.copy(out=g3[:, col:col + 512], in_=pbank(bk)[:]),
                           waits=SR(bk) + list(slots["g3"].r) + ([slots["g3"].w] if slots["g3"].w else []),
                           inc="SC")
                    slots["g3"].w = e1
                    slots["g3"].r = []
                    e2 = E("vector", lambda e, col=col, bk=bk:
                           e.tensor_tensor(out=g4[:, col:col + 512], in0=pbank(bk)[:],
                                           in1=g3[:, col:col + 512], op=ALU.subtract),
                           waits=SR(bk) + [e1] + list(slots["g4"].r) + ([slots["g4"].w] if slots["g4"].w else []),
                           inc="VE")
                    slots["g4"].w = e2
                    slots["g4"].r = []
                    E("tensor", lambda e, col=col: e.matmul(pbt[6][:], rt_b[:], g3[:, col:col + 512],
                                                            start=True, stop=False),
                      waits=[e1] + SW("pb6") + SR("rt_b"))
                    tr = E("tensor", lambda e, col=col: e.matmul(pbt[6][:], rt_b[:], g4[:, col:col + 512],
                                                                 start=False, stop=True),
                           waits=[e2], inc="TE")
                    setw("pb6", tr)
                    tv1 = E("vector", lambda e, bk=bk, r0=r0:
                            e.tensor_tensor(out=sB[:, 0:512], in0=pbank(bk)[:],
                                            in1=cosr[:, r0:r0 + 512], op=ALU.mult),
                            waits=SR(bk) + SR("cosr") + SW("sB"), inc="VE")
                    slots["sB"].w = tv1
                    slots["sB"].r = []
                    setr(bk, tv1)
                    tv2 = E("vector", lambda e, r0=r0:
                            e.tensor_tensor(out=sB[:, 512:1024], in0=pbt[6][:],
                                            in1=sinr[:, r0:r0 + 512], op=ALU.mult),
                            waits=SR("pb6") + SR("sinr"), inc="VE")
                    setr("pb6", tv2)
                    E("vector", lambda e: e.tensor_tensor(out=sB[:, 0:512], in0=sB[:, 0:512],
                                                          in1=sB[:, 512:1024], op=ALU.add),
                      inc="VE")
                    tv4 = E("vector", lambda e, col=col: e.tensor_copy(out=g3[:, col:col + 512],
                                                                       in_=sB[:, 0:512]),
                            waits=[tr], inc="VE")
                    slots["g3"].w = tv4
                    tv5 = E("vector", lambda e, col=col: e.tensor_tensor(out=g4[:, col:col + 512],
                                                                         in0=sB[:, 0:512],
                                                                         in1=g3[:, col:col + 512],
                                                                         op=ALU.subtract),
                            inc="VE")
                    slots["g4"].w = tv5
                    slots["sB"].w = tv5
                    dst = q_sp if is_q else k_sp
                    s1 = E("gpsimd", lambda e, dst=dst, hp=hp, tt=tt, col=col:
                           e.dma_start(out=dst["h"][hp * 128:(hp + 1) * 128, tt * 512:(tt + 1) * 512],
                                       in_=g3[:, col:col + 512]),
                           waits=[tv4], inc="ST", incn=16)
                    s2 = E("gpsimd", lambda e, dst=dst, hp=hp, tt=tt, col=col:
                           e.dma_start(out=dst["l"][hp * 128:(hp + 1) * 128, tt * 512:(tt + 1) * 512],
                                       in_=g4[:, col:col + 512]),
                           waits=[tv5], inc="ST", incn=16)
                    setr("g3", s1)
                    setr("g4", s2)
                    tq1 = E("vector", lambda e, col=col:
                            e.tensor_tensor(out=vbuf[:, 0:512], in0=g3[:, col:col + 512],
                                            in1=g3[:, col:col + 512], op=ALU.mult),
                            waits=[tv4] + SW("vbuf"), inc="VE")
                    setw("vbuf", tq1)
                    setr("g3", tq1)
                    for j in range(2):
                        head = 2 * hp + j
                        tqm = E("tensor", lambda e, j=j: e.matmul(pbt[5][0:1, :], blk_b[:, j:j + 1],
                                                                  vbuf[:, 0:512],
                                                                  start=True, stop=True),
                                waits=[tq1] + SW("pb5") + SR("blk_b"), inc="TE")
                        setw("pb5", tqm)
                        setr("vbuf", tqm)
                        coloff = (0 if is_q else 64) + head * 4 + tt
                        trm = E("vector", lambda e, coloff=coloff:
                                e.tensor_reduce(out=rowb[0:1, coloff:coloff + 1],
                                                in_=pbt[5][0:1, :], axis=AX.X, op=ALU.max),
                                waits=SR("pb5") + (qn_toks[-1:] if qn_toks else SW("rowb")),
                                inc="VE")
                        setr("pb5", trm)
                        qn_toks.append(trm)
                else:
                    e1 = E("scalar", lambda e, bk=bk: e.copy(out=g3[:, 0:512], in_=pbank(bk)[:]),
                           waits=SR(bk) + list(slots["g3"].r) + ([slots["g3"].w] if slots["g3"].w else []),
                           inc="SC")
                    slots["g3"].w = e1
                    slots["g3"].r = []
                    e2 = E("vector", lambda e, bk=bk: e.tensor_tensor(out=g4[:, 0:512],
                                                                      in0=pbank(bk)[:],
                                                                      in1=g3[:, 0:512], op=ALU.subtract),
                           waits=SR(bk) + [e1] + list(slots["g4"].r) + ([slots["g4"].w] if slots["g4"].w else []),
                           inc="VE")
                    slots["g4"].w = e2
                    slots["g4"].r = []
                    setr(bk, e2)
                    tmm = None
                    for hx, (gt, gn, etok) in enumerate(((g3, "g3", e1), (g4, "g4", e2))):
                        for sub in range(4):
                            tmm = E("tensor", lambda e, gt=gt, sub=sub, hx=hx:
                                    e.transpose(pbb[:, (hx * 4 + sub) * 128:(hx * 4 + sub + 1) * 128],
                                                gt[:, sub * 128:(sub + 1) * 128], identb[:]),
                                    waits=([etok] + SW("pbb") + SR("identb")) if (hx == 0 and sub == 0) else [etok],
                                    inc=("TE" if (hx == 1 and sub == 3) else None))
                            setr(gn, tmm)
                    setw("pbb", tmm)
                    tev = E("vector", lambda e: e.tensor_copy(out=vbuf[:, 512:1536], in_=pbb[:, 0:1024]),
                            waits=SR("pbb") + SW("vbuf"), inc="VE")
                    setw("vbuf", tev)
                    setr("pbb", tev)
                    for hx, half in enumerate(("h", "l")):
                        tst = E("gpsimd", lambda e, half=half, hp=hp, tt=tt, hx=hx:
                                e.dma_start(out=v_sp[half][tt * 512:(tt + 1) * 512,
                                                           hp * 128:(hp + 1) * 128]
                                            .rearrange("(n p) c -> p n c", p=128),
                                            in_=vbuf[:, 512 + hx * 512:512 + (hx + 1) * 512]
                                            .rearrange("p (n c) -> p n c", c=128)),
                                waits=[tev], inc="ST", incn=16)
                        setr("vbuf", tst)

            pending = [None]
            for oc in range(3 * NCH):
                wts, b0 = wload(wsp["wqh"], wsp["wql"], oc, oc % 2)
                wh_f, wl_f = wchunks(b0)
                bk = f"pb{0 + (oc % 2)}"
                tok = mm3(bk, wh_f, wl_f,
                          g1, g2, lambda t_, kc: t_[:, kc * 512:(kc + 1) * 512],
                          SR("g1") + SR("g2") + wts)
                setr("g1", tok)
                setr("g2", tok)
                setr(f"wb{oc % 2}", tok)
                if pending[0] is not None:
                    pending[0]()
                hp = oc % NCH
                pending[0] = (lambda bk=bk, hp=hp, oc=oc: _qkv_post(bk, hp, oc))
            pending[0]()

        tprev = qn_toks[-1]
        for h in range(H):
            tprev = E("vector", lambda e, h=h: e.tensor_reduce(out=rowb[0:1, 128 + h:129 + h],
                                                               in_=rowb[0:1, h * 4:h * 4 + 4],
                                                               axis=AX.X, op=ALU.max),
                      waits=[tprev], inc="VE")
            tprev = E("vector", lambda e, h=h: e.tensor_reduce(out=rowb[0:1, 144 + h:145 + h],
                                                               in_=rowb[0:1, 64 + h * 4:64 + h * 4 + 4],
                                                               axis=AX.X, op=ALU.max),
                      waits=[tprev], inc="VE")
        tsq = E("scalar", lambda e: e.activation(out=rowb[0:1, 128:160], in_=rowb[0:1, 128:160],
                                                 func=AF.Sqrt),
                waits=[tprev], inc="SC")
        tvc = E("vector", lambda e: e.tensor_tensor(out=rowb[0:1, 160:176], in0=rowb[0:1, 128:144],
                                                    in1=rowb[0:1, 144:160], op=ALU.mult),
                waits=[tsq], inc="VE")
        tvc2 = E("vector", lambda e: e.tensor_scalar(out=rowb[0:1, 160:176], in0=rowb[0:1, 160:176],
                                                     scalar1=-1.01 * 0.125, scalar2=-0.01,
                                                     op0=ALU.mult, op1=ALU.add),
                 waits=[tvc], inc="VE")
        trep = E("tensor", lambda e: e.matmul(pbt[5][:, 0:16], onesf[0:1, :], rowb[0:1, 160:176],
                                              start=True, stop=True),
                 waits=[tvc2] + SW("pb5") + SR("onesf"), inc="TE")
        setw("pb5", trep)
        setr("rowb", trep)
        slots["rowb"].w = tvc2
        tcrep = E("scalar", lambda e: e.copy(out=crep[:], in_=pbt[5][:, 0:16]),
                  waits=SR("pb5"), inc="SC")
        setr("pb5", tcrep)
        setw("crep", tcrep)
        if dbg:
            E("gpsimd", lambda e: e.dma_start(out=dbg_crep[:, :], in_=crep[:]),
              waits=SR("crep"), inc="ST", incn=16)

        trans = SW("sB") + SW("g3") + SW("g4")
        for off in (0, 512, 1024, 1536):
            pf_slots[off].r = list(trans)
            pf_slots[off].w = None
            pp_slots[off].r = list(trans)
            pp_slots[off].w = None
        pf_ap = sB
        pph_ap = g3

        ta3 = None
        for tt in range(NTT):
            nkt = (tt + 1) * 4
            for hp in range(NCH):
                headA, headB = 2 * hp, 2 * hp + 1
                lk1 = E("sync", lambda e, hp=hp, nkt=nkt:
                        e.dma_start(out=kbuf[:, 0:nkt * 128],
                                    in_=k_sp["h"][hp * 128:(hp + 1) * 128, 0:nkt * 128]),
                        waits=SW("kbuf"), inc="LD", incn=16)
                lk2 = E("sync", lambda e, hp=hp, nkt=nkt:
                        e.dma_start(out=kbuf[:, S:S + nkt * 128],
                                    in_=k_sp["l"][hp * 128:(hp + 1) * 128, 0:nkt * 128]),
                        inc="LD", incn=16)
                setw("kbuf", lk2)
                lv1 = E("sync", lambda e, hp=hp, nkt=nkt:
                        e.dma_start(out=vbuf[:, 0:nkt * 128].rearrange("p (n c) -> p n c", c=128),
                                    in_=v_sp["h"][0:nkt * 128, hp * 128:(hp + 1) * 128]
                                    .rearrange("(n p) c -> p n c", p=128)),
                        waits=SW("vbuf"), inc="LD", incn=16)
                lv2 = E("sync", lambda e, hp=hp, nkt=nkt:
                        e.dma_start(out=vbuf[:, S:S + nkt * 128].rearrange("p (n c) -> p n c", c=128),
                                    in_=v_sp["l"][0:nkt * 128, hp * 128:(hp + 1) * 128]
                                    .rearrange("(n p) c -> p n c", p=128)),
                        inc="LD", incn=16)
                setw("vbuf", lv2)
                lq1 = E("sync", lambda e, hp=hp, tt=tt:
                        e.dma_start(out=qbuf[:, 0:512],
                                    in_=q_sp["h"][hp * 128:(hp + 1) * 128, tt * 512:(tt + 1) * 512]),
                        waits=SW("qbuf"), inc="LD", incn=16)
                lq2 = E("sync", lambda e, hp=hp, tt=tt:
                        e.dma_start(out=qbuf[:, 512:1024],
                                    in_=q_sp["l"][hp * 128:(hp + 1) * 128, tt * 512:(tt + 1) * 512]),
                        inc="LD", incn=16)
                setw("qbuf", lq2)
                last_sc = {}
                pv_tok_box = [None, None]

                def emit_scores(c):
                    ib = c % 2
                    bkA = f"pb{0 + ib * 2}"
                    bkB = f"pb{1 + ib * 2}"
                    kc0 = c * 128
                    for (bkX, p0) in ((bkA, 0), (bkB, 64)):
                        w0 = SW(bkX) + [lk1, lk2, lq1, lq2]
                        E("tensor", lambda e, bkX=bkX, p0=p0, kc0=kc0:
                          e.matmul(pbank(bkX)[:], kbuf[p0:p0 + 64, kc0:kc0 + 128],
                                   qbuf[p0:p0 + 64, 0:512], start=True, stop=False,
                                   tile_position=(p0, 0)),
                          waits=w0)
                        E("tensor", lambda e, bkX=bkX, p0=p0, kc0=kc0:
                          e.matmul(pbank(bkX)[:], kbuf[p0:p0 + 64, kc0:kc0 + 128],
                                   qbuf[p0:p0 + 64, 512:1024], start=False, stop=False,
                                   tile_position=(p0, 0)))
                        tsc = E("tensor", lambda e, bkX=bkX, p0=p0, kc0=kc0:
                                e.matmul(pbank(bkX)[:], kbuf[p0:p0 + 64, S + kc0:S + kc0 + 128],
                                         qbuf[p0:p0 + 64, 0:512], start=False, stop=True,
                                         tile_position=(p0, 0)), inc="TE")
                        setw(bkX, tsc)
                        last_sc[bkX] = tsc

                def emit_expsplit(c):
                    ib = c % 2
                    bkA = f"pb{0 + ib * 2}"
                    bkB = f"pb{1 + ib * 2}"
                    partial = (c >= tt * 4)
                    for (bkX, head, off0) in ((bkA, headA, 0), (bkB, headB, 1024)):
                        pcol = off0 + ib * 512
                        ps_ = pf_slots[pcol]
                        pp_ = pp_slots[pcol]
                        te = E("scalar", lambda e, bkX=bkX, head=head, pcol=pcol:
                               e.activation(out=pf_ap[:, pcol:pcol + 512], in_=pbank(bkX)[:],
                                            func=AF.Exp, scale=0.125, bias=crep[:, head:head + 1]),
                               waits=SR(bkX) + SR("crep") + list(ps_.r) + ([ps_.w] if ps_.w else []),
                               inc="SC")
                        setr(bkX, te)
                        ps_.w = te
                        ps_.r = []
                        if partial:
                            tm = E("gpsimd", lambda e, pcol=pcol, tt=tt, c=c:
                                   e.affine_select(out=pf_ap[:, pcol:pcol + 512],
                                                   in_=pf_ap[:, pcol:pcol + 512],
                                                   pattern=[[1, 512]], compare_op=ALU.is_ge,
                                                   fill=0.0, base=tt * 512 - c * 128,
                                                   channel_multiplier=-1),
                                   waits=[te], inc="GP")
                            ps_.w = tm
                        th = E("vector", lambda e, pcol=pcol:
                               e.tensor_copy(out=pph_ap[:, pcol:pcol + 512], in_=pf_ap[:, pcol:pcol + 512]),
                               waits=[ps_.w] + list(pp_.r) + ([pp_.w] if pp_.w else []), inc="VE")
                        ps_.r.append(th)
                        tl = E("vector", lambda e, pcol=pcol:
                               e.tensor_tensor(out=pph_ap[:, 2048 + pcol:2048 + pcol + 512],
                                               in0=pf_ap[:, pcol:pcol + 512],
                                               in1=pph_ap[:, pcol:pcol + 512], op=ALU.subtract),
                               inc="VE")
                        ps_.r.append(tl)
                        pp_.w = tl
                        pp_.r = []
                def emit_pvden(c):
                    ib = c % 2
                    kc0 = c * 128
                    stop = (c == nkt - 1)
                    first = (c == 0)
                    for (off0, p0) in ((0, 0), (1024, 64)):
                        pcol = off0 + ib * 512
                        pp_ = pp_slots[pcol]
                        colA = kc0 + (0 if p0 == 0 else 64)
                        ww = ([pp_.w] if pp_.w else []) + [lv1, lv2]
                        if first:
                            ww += SW("pb4") + SW("pb5")
                        E("tensor", lambda e, pcol=pcol, p0=p0, colA=colA, first=first:
                          e.matmul(pbt[4][p0:p0 + 64, :], vbuf[:, colA:colA + 64],
                                   pph_ap[:, pcol:pcol + 512], start=first, stop=False,
                                   tile_position=(0, p0)),
                          waits=ww)
                        E("tensor", lambda e, pcol=pcol, p0=p0, colA=colA:
                          e.matmul(pbt[4][p0:p0 + 64, :], vbuf[:, S + colA:S + colA + 64],
                                   pph_ap[:, pcol:pcol + 512], start=False, stop=False,
                                   tile_position=(0, p0)))
                        tpv = E("tensor", lambda e, pcol=pcol, p0=p0, colA=colA, stop=stop:
                                e.matmul(pbt[4][p0:p0 + 64, :], vbuf[:, colA:colA + 64],
                                         pph_ap[:, 2048 + pcol:2048 + pcol + 512], start=False,
                                         stop=stop, tile_position=(0, p0)),
                                inc=("TE" if (stop and p0 == 64) else None))
                        ob0 = 0 if p0 == 0 else 64
                        E("tensor", lambda e, pcol=pcol, p0=p0, ob0=ob0, first=first:
                          e.matmul(pbt[5][p0:p0 + 64, :], ones_b[:, ob0:ob0 + 64],
                                   pph_ap[:, pcol:pcol + 512], start=first, stop=False,
                                   tile_position=(0, p0)),
                          waits=SR("ones_b"))
                        tdn = E("tensor", lambda e, pcol=pcol, p0=p0, ob0=ob0, stop=stop:
                                e.matmul(pbt[5][p0:p0 + 64, :], ones_b[:, ob0:ob0 + 64],
                                         pph_ap[:, 2048 + pcol:2048 + pcol + 512], start=False,
                                         stop=stop, tile_position=(0, p0)),
                                inc=("TE" if (stop and p0 == 64) else None))
                        pp_.r.append(tpv)
                        pp_.r.append(tdn)
                        if stop and p0 == 64:
                            pv_tok_box[0], pv_tok_box[1] = tpv, tdn

                emit_scores(0)
                for c in range(nkt):
                    emit_expsplit(c)
                    if c + 1 < nkt:
                        emit_scores(c + 1)
                    emit_pvden(c)
                pv_tok, den_tok = pv_tok_box
                setw("pb4", pv_tok)
                setw("pb5", den_tok)
                for tsc_ in last_sc.values():
                    setr("kbuf", tsc_)
                    setr("qbuf", tsc_)
                setr("vbuf", pv_tok)
                trc = E("vector", lambda e: e.reciprocal(out=rden[:], in_=pbt[5][:]),
                        waits=SR("pb5") + SW("rden"), inc="VE")
                setw("rden", trc)
                setr("pb5", trc)
                ta1 = E("vector", lambda e, hp=hp:
                        e.tensor_tensor(out=sa[:, hp * 512:(hp + 1) * 512], in0=pbt[4][:],
                                        in1=rden[:], op=ALU.mult),
                        waits=[trc] + SR("pb4") + (SW("sa") if hp == 0 else []), inc="VE")
                setr("pb4", ta1)
                setr("rden", ta1)
                if hp == 0:
                    slots["sa"].w = ta1
                    slots["sa"].r = []
                ta2 = E("vector", lambda e, hp=hp: e.tensor_copy(out=g1[:, hp * 512:(hp + 1) * 512],
                                                                 in_=sa[:, hp * 512:(hp + 1) * 512]),
                        waits=(SW("g1") if hp == 0 else []), inc="VE")
                if hp == 0:
                    slots["g1"].w = ta2
                    slots["g1"].r = []
                ta3 = E("vector", lambda e, hp=hp: e.tensor_tensor(out=g2[:, hp * 512:(hp + 1) * 512],
                                                                   in0=sa[:, hp * 512:(hp + 1) * 512],
                                                                   in1=g1[:, hp * 512:(hp + 1) * 512],
                                                                   op=ALU.subtract),
                        waits=(SW("g2") if hp == 0 else []), inc="VE")
                if hp == 0:
                    slots["g2"].w = ta3
                    slots["g2"].r = []
            slots["g1"].w = ta3
            slots["g2"].w = ta3
            for oc in range(NCH):
                wts, b0 = wload(wsp["woh"], wsp["wol"], oc, oc % 2)
                wh_f, wl_f = wchunks(b0)
                bk = f"pb{4 + (oc % 2)}"
                tok = mm3(bk, wh_f, wl_f,
                          g1, g2, lambda t_, kc: t_[:, kc * 512:(kc + 1) * 512],
                          SR("g1") + SR("g2") + wts)
                setr("g1", tok)
                setr("g2", tok)
                setr(f"wb{oc % 2}", tok)
                w0 = oc * S + tt * 512
                tv = E("vector", lambda e, w0=w0, bk=bk:
                       e.tensor_tensor(out=yt[:, w0:w0 + 512], in0=pbank(bk)[:],
                                       in1=yt[:, w0:w0 + 512], op=ALU.add),
                       waits=SR(bk) + SWyt(tt), inc="VE")
                setr(bk, tv)
                setw_yt(tt, tv)
        allp = []
        for off in (0, 512, 1024, 1536):
            allp += list(pf_slots[off].r) + ([pf_slots[off].w] if pf_slots[off].w else [])
            allp += list(pp_slots[off].r) + ([pp_slots[off].w] if pp_slots[off].w else [])
        slots["sB"].r.extend(allp)
        slots["g3"].r.extend(allp)

    emit_consts()
    emit_wsplit()
    for s in range(SEQ):
        emit_load_x(s)
        if "ffn1" in phases:
            emit_ln(s)
            emit_ffn(s)
        if "attn" in phases:
            emit_ln(s)
            emit_attn(s)
        if "ffn2" in phases:
            emit_ln(s)
            emit_ffn(s)
        emit_store(s)

    with contextlib.ExitStack() as bstack:
        sems = {nm: bstack.enter_context(nc.semaphore(nm)) for nm in SEMNAMES}
        blk = bstack.enter_context(nc.Block())
        final_st = sched.cnt.get("ST", 0)

        @blk.sync
        def _(eng):
            sched.replay("sync", eng, sems)

        @blk.gpsimd
        def _(eng):
            sched.replay("gpsimd", eng, sems)
            eng.wait_ge(sems["ST"], final_st)

        @blk.tensor
        def _(eng):
            sched.replay("tensor", eng, sems)

        @blk.vector
        def _(eng):
            sched.replay("vector", eng, sems)

        @blk.scalar
        def _(eng):
            sched.replay("scalar", eng, sems)

    stack.close()
    return nc


_NC_CACHE = {}


def kernel(L0_attn_x, L0_attn_qkv, L0_attn_wo, L0_ffn_W_r, L0_ffn_W_k,
           L0_ffn_W_v, L0_ffn_mu_r, L0_ffn_mu_k, _phases=("ffn1", "attn", "ffn2"),
           _dbg=False):
    x = np.ascontiguousarray(np.asarray(L0_attn_x, dtype=np.float32).reshape(B * S, D))
    wq = np.ascontiguousarray(np.asarray(L0_attn_qkv, dtype=np.float32).T)
    wo = np.ascontiguousarray(np.asarray(L0_attn_wo, dtype=np.float32).T)
    wr = np.ascontiguousarray(np.asarray(L0_ffn_W_r, dtype=np.float32).T)
    wk = np.ascontiguousarray(np.asarray(L0_ffn_W_k, dtype=np.float32).T)
    wv = np.ascontiguousarray(np.asarray(L0_ffn_W_v, dtype=np.float32).T)
    mur_ = np.ascontiguousarray(np.asarray(L0_ffn_mu_r, dtype=np.float32))
    muk_ = np.ascontiguousarray(np.asarray(L0_ffn_mu_k, dtype=np.float32))

    key = (_phases, _dbg)
    if key not in _NC_CACHE:
        _NC_CACHE[key] = build_program(phases=_phases, dbg=_dbg)
    nc = _NC_CACHE[key]

    in_maps = []
    for c in range(NCORE):
        shard = x[c * SEQ * S:(c + 1) * SEQ * S]
        in_maps.append({"x_in": shard, "wq_t": wq, "wo_t": wo, "wr_t": wr,
                        "wk_t": wk, "wv_t": wv, "mu_r": mur_, "mu_k": muk_})
    res = run_bass_kernel_spmd(nc, in_maps, core_ids=list(range(NCORE)))
    outs = [np.asarray(r["out"]) for r in res.results]
    full = np.concatenate(outs, axis=0).reshape(B, S, D)
    if _dbg:
        return full, res.results
    return full
